# version 05
# baseline (speedup 1.0000x reference)
import math
import numpy as np
import ml_dtypes
import concourse.bass as bass
import concourse.mybir as mybir
from concourse.bass_utils import run_bass_kernel_spmd
from contextlib import ExitStack

F32 = mybir.dt.float32
BF16 = mybir.dt.bfloat16
I8 = mybir.dt.int8
AF = mybir.ActivationFunctionType
ALU = mybir.AluOpType
AX = mybir.AxisListType

D = 2048
NT = 16
HT = 8
TOK = 128 * NT
H0, DH0 = 16, 128
H1, DH1 = 16, 64
DL = 1024
IF = 8192
EPS_LN = 1e-5
EPS_RMS = 1e-6
EPS_Q = 1e-5
SQD0 = 1.0 / math.sqrt(DH0)
SQD1 = 1.0 / math.sqrt(DH1)
BF = np.dtype(ml_dtypes.bfloat16)

DEPTH = 2
_DBG_SCRATCH = False


class Prog:
    ENGINES = ("tensor", "vector", "scalar", "sync", "gpsimd")

    def __init__(self, nc, es):
        self.nc = nc
        self.es = es
        self.ops = {e: [] for e in self.ENGINES}
        self.semvals = {}
        self.sems = {}

    def sem(self, name):
        s = self.es.enter_context(self.nc.semaphore(name))
        self.semvals[name] = 0
        self.sems[name] = s
        return s

    def emit(self, engine, fn, inc=None, inc_by=1):
        if inc is not None:
            self.semvals[inc.name] += inc_by

            def wrapped(eng, fn=fn, inc=inc, n=inc_by):
                fn(eng).then_inc(inc, n)
            self.ops[engine].append(wrapped)
        else:
            self.ops[engine].append(fn)

    def dma(self, engine, out, in_, sem, transpose=False, slow=False):
        self.emit(engine,
                  lambda e, o=out, i=in_, tr=transpose, sl=slow: e.dma_start(
                      out=o, in_=i, transpose=tr, allow_slow_non_contiguous=sl),
                  inc=sem, inc_by=16)

    def wait(self, engine, sem, val=None):
        v = self.semvals[sem.name] if val is None else val
        self.ops[engine].append(lambda e, s=sem, v=v: e.wait_ge(s, v))

    def val(self, sem):
        return self.semvals[sem.name]

    def barrier(self):
        vals = [(s, self.semvals[n]) for n, s in self.sems.items()
                if n != "sem_st"]
        for e in self.ENGINES:
            for s, v in vals:
                if v > 0:
                    self.ops[e].append(lambda eng, s=s, v=v: eng.wait_ge(s, v))


def build_program():
    nc = bass.Bass("TRN2", target_bir_lowering=False)
    es = ExitStack()
    p = Prog(nc, es)

    def din(name, shape, dtype=F32):
        return nc.dram_tensor(name, list(shape), dtype, kind="ExternalInput").ap()

    def dsc(name, shape, dtype=BF16):
        kind = "ExternalOutput" if _DBG_SCRATCH else "Internal"
        return nc.dram_tensor(name, list(shape), dtype, kind=kind).ap()

    x_in = din("x", (TOK, D))
    out = nc.dram_tensor("out", [TOK, D], F32, kind="ExternalOutput").ap()
    dbg = nc.dram_tensor("dbg", [128, 96], F32, kind="ExternalOutput").ap()
    dbg2 = nc.dram_tensor("dbg2", [128, 3 * D], BF16, kind="ExternalOutput").ap()
    dbg3 = nc.dram_tensor("dbg3", [128, 4 * D], BF16, kind="ExternalOutput").ap()
    dbg4 = nc.dram_tensor("dbg4", [128, 4 * 2064], F32, kind="ExternalOutput").ap()

    w_qkv = din("w_qkv", (D, 3 * D))
    b_qkv = din("b_qkv", (1, 3 * D))
    w_dense = din("w_dense", (D, D))
    b_dense = din("b_dense", (1, D))
    w_ffn0 = din("w_ffn0", (D, D))
    s_ffn0 = din("s_ffn0", (1, D))
    b_ffn0 = din("b_ffn0", (1, D))
    wk1 = [din(f"w{m}1", (D, D)) for m in "kqv"]
    wl1 = [din(f"w{m}l", (D, DL)) for m in "kqv"]
    b1r = [din(f"b{m}1", (1, D)) for m in "kqv"]
    bl1 = [din(f"b{m}l", (1, DL)) for m in "kqv"]
    wo = din("wo", (DL, D))
    bo = din("bo", (1, D))
    w_gT = din("w_gT", (D, IF))
    w_uT = din("w_uT", (D, IF))
    w_dT = din("w_dT", (IF, D))
    w_ln = din("w_ln", (1, IF))

    identity = din("identity", (128, 128), BF16)
    trimask = din("trimask", (128, 128), BF16)
    cos_r = din("cos_r", (TOK, 64))
    sin_r = din("sin_r", (TOK, 64))

    q_sc = dsc("q_sc", (NT, 128, D))
    k_sc = dsc("k_sc", (NT, 128, D))
    v_sc = dsc("v_sc", (NT, 128, D))
    k1t = dsc("k1t", (NT, 128, DL))
    q1t = dsc("q1t", (NT, 128, DL))
    v1_sc = dsc("v1_sc", (NT, 128, DL))
    wg_t = dsc("wg_t", (D, IF))
    wu_t = dsc("wu_t", (D, IF))
    wd_t = dsc("wd_t", (IF, D))
    h_sc = dsc("h_sc", (NT, 128, IF))
    misc_sc = dsc("misc_sc", (1, 4096), F32)

    def sba(name, shape, dtype=BF16):
        return es.enter_context(nc.sbuf_tensor(name, list(shape), dtype))

    x_sb = sba("x_sb", (128, NT * D))
    idn = sba("idn", (128, 128))
    trim = sba("trim", (128, 128))
    ones_f = sba("ones_f", (1, 128), F32)
    onecol = sba("onecol", (128, 1), F32)
    cvec = sba("cvec", (1, 3 * D), F32)
    ln_mu = sba("ln_mu", (128, NT), F32)
    ln_rs = sba("ln_rs", (128, NT), F32)
    nrm = sba("nrm", (128, 32), F32)
    negb = sba("negb", (128, 16), F32)
    hstat = sba("hstat", (128, NT), F32)
    hmax = sba("hmax", (128, NT), F32)
    hfac = sba("hfac", (128, NT), F32)
    beffc = sba("beffc", (128, 3 * 8), F32)
    colA = sba("colA", (128, 1), F32)
    colB = sba("colB", (128, 1), F32)
    colC = sba("colC", (128, 1), F32)
    colD = sba("colD", (128, 1), F32)
    r16 = sba("r16", (1, 64), F32)
    vpad = sba("vpad", (128, 1), F32)
    wmr = sba("wmr", (1, 16), F32)
    wmc = sba("wmc", (128, 12), F32)
    sxi = sba("sxi", (128, NT), F32)
    q2s = sba("q2s", (128, NT), F32)
    dss = sba("dss", (128, NT), F32)
    rr2 = sba("rr2", (128, NT), F32)
    rcq = sba("rcq", (128, NT), F32)
    gu = sba("gu", (128, 16), F32)

    sem_w = p.sem("sem_w")
    sem_wp = [p.sem("sem_w0"), p.sem("sem_w1")]
    sem_cp = [p.sem("sem_c0"), p.sem("sem_c1")]
    sem_gp = [p.sem("sem_g0"), p.sem("sem_g1")]
    sem_kp = [p.sem("sem_k0"), p.sem("sem_k1")]
    sem_up = [p.sem("sem_u0"), p.sem("sem_u1")]
    sem_c = p.sem("sem_c")
    sem_g = p.sem("sem_g")
    sem_t = p.sem("sem_t")
    sem_v = p.sem("sem_v")
    sem_s = p.sem("sem_s")
    sem_st = p.sem("sem_st")

    E = p.emit
    W = p.wait

    def VF():
        E("vector", lambda e: e.memset(vpad[:], 0.0))

    p.dma("sync", idn[:], identity, sem_w)
    p.dma("sync", trim[:], trimask, sem_w)

    with nc.sbuf_tensor("xld", [128, 2 * D], F32, allow_name_mangling=True) as xld:
        for t in range(NT):
            buf = xld[:, (t % 2) * D:(t % 2) * D + D]
            if t >= 2:
                p.wait("scalar", sem_v, p.val(sem_v) - 1)
            p.dma("scalar", buf, x_in[128 * t:128 * (t + 1), :], sem_cp[t % 2])
            p.wait("vector", sem_cp[t % 2])
            E("vector", lambda e, b=buf, t=t: e.tensor_copy(
                x_sb[:, D * t:D * (t + 1)], b), inc=sem_v)
    E("vector", lambda e: e.memset(ones_f[:], 1.0))
    E("vector", lambda e: e.memset(onecol[:], 1.0), inc=sem_v)
    p.barrier()

    def ln_stats_apply(t, xn_out, dumA, dumB):
        xt = x_sb[:, D * t:D * (t + 1)]
        E("scalar", lambda e, xt=xt, s=dumA: e.activation(
            s, xt, AF.Copy, accum_out=colA[:]))
        E("scalar", lambda e, xt=xt, s=dumB: e.activation(
            s, xt, AF.Square, accum_out=colB[:]), inc=sem_s)
        W("vector", sem_s)
        E("vector", lambda e: e.tensor_scalar(colA[:], colA[:], 1.0 / D, None, ALU.mult))
        E("vector", lambda e: e.tensor_scalar(colB[:], colB[:], 1.0 / D, None, ALU.mult))
        E("vector", lambda e: e.tensor_tensor(colC[:], colA[:], colA[:], ALU.mult))
        VF()
        E("vector", lambda e: e.tensor_tensor(colB[:], colB[:], colC[:], ALU.subtract))
        VF()
        E("vector", lambda e: e.tensor_scalar(colB[:], colB[:], EPS_LN, None, ALU.add))
        VF()
        E("vector", lambda e: e.reciprocal(colB[:], colB[:]), inc=sem_v)
        W("scalar", sem_v)
        E("scalar", lambda e, t=t: e.activation(ln_rs[:, t:t + 1], colB[:], AF.Sqrt), inc=sem_s)
        W("vector", sem_s)
        E("vector", lambda e, t=t: e.scalar_tensor_tensor(
            ln_mu[:, t:t + 1], colA[:], -1.0, ln_rs[:, t:t + 1],
            ALU.mult, ALU.mult), inc=sem_v)
        W("scalar", sem_v)
        E("scalar", lambda e, t=t, xo=xn_out, xt=xt: e.activation(
            xo, xt, AF.Identity, scale=ln_rs[:, t:t + 1], bias=ln_mu[:, t:t + 1]),
            inc=sem_s)

    def xn_to_xnf(t_local, xn_tile, xnf):
        for dc in range(16):
            p.dma("sync", xnf[:, 1024 * dc + 128 * t_local:1024 * dc + 128 * (t_local + 1)],
                  xn_tile[:, 128 * dc:128 * (dc + 1)], sem_w, transpose=True)

    def stream_mm(half, n_oc, n_dc, load_chunk, lhsT_fn, psums, post_fn,
                  bias_fn=None, cast_scale=None, wstage=None, wbf=None):
        idx = 0
        for oc in range(n_oc):
            for dc in range(n_dc):
                bsel = idx % 2
                wf = wstage[bsel]
                wb = wbf[bsel]
                if idx >= 2:
                    p.wait("sync", sem_t, tmark[bsel])
                load_chunk(oc, dc, wf, sem_wp[bsel])
                p.wait("vector", sem_wp[bsel])
                if cast_scale is None:
                    E("vector", lambda e, wf=wf, wb=wb: e.tensor_copy(wb[:], wf[:]),
                      inc=sem_v)
                else:
                    E("vector", lambda e, wf=wf, wb=wb, oc=oc: e.tensor_tensor(
                        wb[:], wf[:], cast_scale(oc), ALU.mult), inc=sem_v)
                p.wait("tensor", sem_v)
                for t in range(8):
                    st = (dc == 0)
                    sp = (dc == n_dc - 1) and (bias_fn is None)
                    E("tensor", lambda e, ps=psums[t], l=lhsT_fn(dc, t), wb=wb,
                        st=st, sp=sp: e.matmul(ps[:], l, wb[:], start=st, stop=sp),
                      inc=(sem_t if t == 7 else None))
                tmark[bsel] = p.val(sem_t)
                idx += 1
            if bias_fn is not None:
                for t in range(8):
                    bias_fn(oc, t, psums[t])
            W("scalar", sem_t)
            W("vector", sem_t)
            for t in range(8):
                post_fn(oc, t, psums[t])
            if oc < n_oc - 1:
                p.wait("tensor", sem_s)
                p.wait("tensor", sem_v)
            else:
                p.barrier()

    def layer0():
        with ExitStack() as ph:
            xnf = ph.enter_context(nc.sbuf_tensor("xnf", [128, 16 * 1024], BF16, allow_name_mangling=True))
            xnt = ph.enter_context(nc.sbuf_tensor("xnt", [128, D], BF16, allow_name_mangling=True))
            dumA = ph.enter_context(nc.sbuf_tensor("dumA", [128, D], F32, allow_name_mangling=True))
            dumB = ph.enter_context(nc.sbuf_tensor("dumB", [128, D], F32, allow_name_mangling=True))
            cosr = ph.enter_context(nc.sbuf_tensor("cosr", [128, NT * 64], F32, allow_name_mangling=True))
            sinr = ph.enter_context(nc.sbuf_tensor("sinr", [128, NT * 64], F32, allow_name_mangling=True))
            wstg0 = ph.enter_context(nc.sbuf_tensor("wstg0", [128, 512], F32, allow_name_mangling=True))
            wstg1 = ph.enter_context(nc.sbuf_tensor("wstg1", [128, 512], F32, allow_name_mangling=True))
            wbf0 = ph.enter_context(nc.sbuf_tensor("wbf0", [128, 512], BF16, allow_name_mangling=True))
            wbf1 = ph.enter_context(nc.sbuf_tensor("wbf1", [128, 512], BF16, allow_name_mangling=True))
            evac0 = ph.enter_context(nc.sbuf_tensor("evac0", [128, 512], BF16, allow_name_mangling=True))
            evac1 = ph.enter_context(nc.sbuf_tensor("evac1", [128, 512], BF16, allow_name_mangling=True))
            ropes = ph.enter_context(nc.sbuf_tensor("ropes", [128, 128], F32, allow_name_mangling=True))
            sqs = ph.enter_context(nc.sbuf_tensor("sqs", [128, 512], F32, allow_name_mangling=True))
            st16 = ph.enter_context(nc.sbuf_tensor("st16", [128, 16], F32, allow_name_mangling=True))
            s0bc = ph.enter_context(nc.sbuf_tensor("s0bc", [128, D], F32, allow_name_mangling=True))
            pool = ph.enter_context(nc.sbuf_tensor("pool", [1, 4 * D], F32, allow_name_mangling=True))
            p.dma("sync", cosr[:].rearrange("p (t c) -> p t c", c=64),
                  cos_r.rearrange("(t p) c -> p t c", p=128), sem_w)
            p.dma("sync", sinr[:].rearrange("p (t c) -> p t c", c=64),
                  sin_r.rearrange("(t p) c -> p t c", p=128), sem_w)
            E("vector", lambda e: e.memset(nrm[:], 0.0), inc=sem_v)
            with nc.psum_tensor("pb", [128, 512], F32, allow_name_mangling=True) as pb:
                p.dma("sync", pool[:, 0:D], s_ffn0, sem_w)
                p.barrier()
                for j in range(4):
                    E("tensor", lambda e, j=j, pb=pb: e.matmul(
                        pb[:], ones_f[:], pool[:, 512 * j:512 * (j + 1)],
                        start=True, stop=True), inc=sem_t)
                    W("scalar", sem_t)
                    E("scalar", lambda e, j=j, pb=pb: e.copy(
                        s0bc[:, 512 * j:512 * (j + 1)], pb[:]), inc=sem_s)
                    p.barrier()
            p.barrier()
            p.dma("sync", pool[:, 0:3 * D], b_qkv, sem_w)
            p.dma("sync", pool[:, 3 * D:4 * D], b_ffn0, sem_w)

            for half in range(2):
                for tl in range(8):
                    t = 8 * half + tl
                    ln_stats_apply(t, xnt[:], dumA[:], dumB[:])
                    W("sync", sem_s)
                    xn_to_xnf(tl, xnt, xnf)
                    p.barrier()

                if half == 0:
                    p.dma("sync", dbg3[:, 3 * D:4 * D], xnt[:], sem_st)
                    p.dma("sync", dbg4[:, 2 * 2064:2 * 2064 + 16], ln_rs[:, 0:16], sem_st)
                    p.dma("sync", dbg4[:, 2 * 2064 + 16:2 * 2064 + 32], ln_mu[:, 0:16], sem_st)
                    p.dma("sync", dbg3[:, 0:D], x_sb[:, 0:D], sem_st)
                    for dc in range(16):
                        p.dma("sync", dbg3[:, D + 128 * dc:D + 128 * (dc + 1)],
                              xnf[:, 1024 * dc:1024 * dc + 128], sem_st)
                    p.barrier()
                with ExitStack() as ph2:
                    p0 = ph2.enter_context(nc.psum_tensor("p0", [128, 512], F32, allow_name_mangling=True))
                    p1 = ph2.enter_context(nc.psum_tensor("p1", [128, 512], F32, allow_name_mangling=True))
                    p2 = ph2.enter_context(nc.psum_tensor("p2", [128, 512], F32, allow_name_mangling=True))
                    p3 = ph2.enter_context(nc.psum_tensor("p3", [128, 512], F32, allow_name_mangling=True))
                    p4 = ph2.enter_context(nc.psum_tensor("p4", [128, 512], F32, allow_name_mangling=True))
                    p5 = ph2.enter_context(nc.psum_tensor("p5", [128, 512], F32, allow_name_mangling=True))
                    p6 = ph2.enter_context(nc.psum_tensor("p6", [128, 512], F32, allow_name_mangling=True))
                    p7 = ph2.enter_context(nc.psum_tensor("p7", [128, 512], F32, allow_name_mangling=True))
                    psums = [p0, p1, p2, p3, p4, p5, p6, p7]

                    def load_qkv(oc, dc, wf, sem):
                        p.dma("sync", wf[:],
                              w_qkv[128 * dc:128 * (dc + 1), 512 * oc:512 * (oc + 1)],
                              sem)

                    def lhsT_xnf(dc, t):
                        return xnf[:, 1024 * dc + 128 * t:1024 * dc + 128 * (t + 1)]

                    def bias_qkv(oc, t, ps):
                        E("tensor", lambda e, ps=ps, oc=oc: e.matmul(
                            ps[:], ones_f[:], pool[:, 512 * oc:512 * (oc + 1)],
                            start=False, stop=True), inc=sem_t)

                    def post_qkv(oc, t, ps, half=half):
                        tg = 8 * half + t
                        ev = evac0 if t % 2 == 0 else evac1
                        if evmark[t % 2]:
                            p.wait("scalar", sem_gp[t % 2], evmark[t % 2])
                        E("scalar", lambda e, ev=ev, ps=ps: e.copy(ev[:], ps[:]),
                          inc=sem_s)
                        if oc == 0 and t == 0 and half == 0:
                            W("sync", sem_s)
                            p.dma("sync", dbg3[:, 2 * D:2 * D + 512], ev[:], sem_st)
                        if oc < 8:
                            W("vector", sem_s)
                            co = cosr[:, 64 * tg:64 * (tg + 1)].rearrange(
                                "p (j c) -> p j c", j=1)
                            si = sinr[:, 64 * tg:64 * (tg + 1)].rearrange(
                                "p (j c) -> p j c", j=1)
                            co4 = cosr[:, 64 * tg:64 * (tg + 1)].rearrange(
                                "p (c j) -> p c j", j=1)
                            Av = ev[:].rearrange("p (j c) -> p j c", j=4)[:, :, 0:16]
                            Bv = ev[:].rearrange("p (j c) -> p j c", j=4)[:, :, 16:32]
                            cv = cosr[:, 64 * tg:64 * (tg + 1)].rearrange(
                                "p (j c) -> p j c", j=4)
                            sv = sinr[:, 64 * tg:64 * (tg + 1)].rearrange(
                                "p (j c) -> p j c", j=4)
                            T1 = ropes[:, 0:64].rearrange("p (j c) -> p j c", j=4)
                            T2 = ropes[:, 64:128].rearrange("p (j c) -> p j c", j=4)
                            E("vector", lambda e, A=Av, s=sv, T=T1: e.tensor_tensor(T, A, s, ALU.mult))
                            E("vector", lambda e, B=Bv, s=sv, T=T2: e.tensor_tensor(T, B, s, ALU.mult))
                            E("vector", lambda e, A=Av, c=cv: e.tensor_tensor(A, A, c, ALU.mult))
                            E("vector", lambda e, B=Bv, c=cv: e.tensor_tensor(B, B, c, ALU.mult))
                            E("vector", lambda e, A=Av, T=T2: e.tensor_tensor(A, A, T, ALU.subtract))
                            E("vector", lambda e, B=Bv, T=T1: e.tensor_tensor(B, B, T, ALU.add))
                            VF()
                            E("vector", lambda e, ev=ev: e.tensor_tensor(
                                sqs[:], ev[:], ev[:], ALU.mult))
                            VF()
                            E("vector", lambda e: e.tensor_reduce(
                                st16[:, 0:4].rearrange("p (a o) -> p a o", o=1),
                                sqs[:].rearrange("p (a c) -> p a c", c=128),
                                AX.X, ALU.add))
                            VF()
                            hb = 4 * oc
                            E("vector", lambda e, hb=hb: e.tensor_tensor(
                                nrm[:, hb:hb + 4], nrm[:, hb:hb + 4],
                                st16[:, 0:4], ALU.max), inc=sem_v)
                            W("gpsimd", sem_v)
                            dst = (q_sc if oc < 4 else k_sc)
                            cb = 512 * (oc % 4)
                            p.dma("gpsimd", dst[tg, :, cb:cb + 512], ev[:],
                                  sem_gp[t % 2])
                        else:
                            W("gpsimd", sem_s)
                            cb = 512 * (oc - 8)
                            p.dma("gpsimd", v_sc[tg, :, cb:cb + 512], ev[:],
                                  sem_gp[t % 2])
                        evmark[t % 2] = p.val(sem_gp[t % 2])

                    tmark[0] = tmark[1] = 0
                    evmark[0] = evmark[1] = 0
                    stream_mm(half, 12, 16, load_qkv, lhsT_xnf, psums, post_qkv,
                              bias_fn=bias_qkv, wstage=[wstg0, wstg1],
                              wbf=[wbf0, wbf1])

                    def load_f0(oc, dc, wf, sem):
                        p.dma("sync", wf[:],
                              w_ffn0[128 * dc:128 * (dc + 1), 512 * oc:512 * (oc + 1)],
                              sem)

                    def bias_f0(oc, t, ps):
                        E("tensor", lambda e, ps=ps, oc=oc: e.matmul(
                            ps[:], ones_f[:], pool[:, 3 * D + 512 * oc:3 * D + 512 * (oc + 1)],
                            start=False, stop=True), inc=sem_t)

                    def post_f0(oc, t, ps, half=half):
                        tg = 8 * half + t
                        xs = x_sb[:, D * tg + 512 * oc:D * tg + 512 * (oc + 1)]
                        E("vector", lambda e, xs=xs, ps=ps: e.tensor_tensor(
                            xs, xs, ps[:], ALU.add), inc=sem_v)

                    def scale_f0(oc):
                        return s0bc[:, 512 * oc:512 * (oc + 1)]

                    stream_mm(half, 4, 16, load_f0, lhsT_xnf, psums, post_f0,
                              bias_fn=bias_f0, cast_scale=scale_f0,
                              wstage=[wstg0, wstg1], wbf=[wbf0, wbf1])

            finalize_bound(SQD0)
            p.dma("sync", dbg2[:, 0:D], q_sc[0].rearrange("p c -> p c"), sem_st)
            p.dma("sync", dbg2[:, D:2 * D], k_sc[0], sem_st)
            p.dma("sync", dbg2[:, 2 * D:3 * D], v_sc[0], sem_st)
            p.barrier()

        attention(layer=0)
        dense_proj(w_in=w_dense, b_row=b_dense, n_dc=16, layer=0)

    def finalize_bound(sqd):
        with ExitStack() as ph:
            ntp = ph.enter_context(nc.sbuf_tensor("ntp", [32, 128], F32, allow_name_mangling=True))
            nmx = ph.enter_context(nc.sbuf_tensor("nmx", [32, 1], F32, allow_name_mangling=True))
            pnb = ph.enter_context(nc.psum_tensor("pnb", [128, 512], F32, allow_name_mangling=True))
            p.barrier()
            W("sync", sem_v, p.val(sem_v))
            p.dma("sync", misc_sc[0:1, 0:4096].rearrange("a (p c) -> a p c", p=128),
                  nrm[:, 0:32].rearrange("p c -> p c"), sem_w)
            p.wait("sync", sem_w)
            p.dma("sync", ntp[:],
                  misc_sc[0:1, 0:4096].rearrange("a (p c) -> a c p", p=128)[0],
                  sem_w, slow=True)
            p.wait("vector", sem_w)
            E("vector", lambda e: e.reduce_max(
                nmx[:], ntp[:], AX.X), inc=sem_v)
            VF()
            W("sync", sem_v)
            p.dma("sync", misc_sc[0:1, 0:32].rearrange("a (p c) -> a p c", c=1),
                  nmx[:], sem_w)
            p.wait("sync", sem_w)
            p.dma("sync", r16[:, 0:32], misc_sc[0:1, 0:32], sem_w)
            p.wait("scalar", sem_w)
            E("scalar", lambda e: e.activation(r16[:, 0:32], r16[:, 0:32], AF.Sqrt),
              inc=sem_s)
            W("vector", sem_s)
            E("vector", lambda e, sqd=sqd: e.scalar_tensor_tensor(
                r16[:, 32:48], r16[:, 0:16], -sqd, r16[:, 16:32],
                ALU.mult, ALU.mult), inc=sem_v)
            p.wait("tensor", sem_v)
            E("tensor", lambda e: e.matmul(
                pnb[:, 0:16], ones_f[:], r16[:, 32:48], start=True, stop=True),
                inc=sem_t)
            W("scalar", sem_t)
            E("scalar", lambda e: e.copy(negb[:], pnb[:, 0:16]), inc=sem_s)
            p.barrier()
            p.dma("sync", dbg[:, 0:32], nrm[:, 0:32], sem_st)
            p.dma("sync", dbg[:, 32:48], negb[:, 0:16], sem_st)
            p.emit("sync", lambda e: e.dma_start(out=dbg[0:32, 48:49], in_=ntp[0:32, 0:1],
                   allow_slow_non_contiguous=True), inc=sem_st, inc_by=16)
            p.dma("sync", dbg[0:1, 49:49+32], r16[:, 0:32], sem_st)
            p.barrier()

    def attention(layer):
        if layer == 0:
            nh, dv, sqd = H0, DH0, SQD0
        else:
            nh, dv, sqd = H1, DH1, SQD1
        dva = dv + 1
        ACCW = nh * dva
        with ExitStack() as pha:
            acc = pha.enter_context(nc.sbuf_tensor("acc", [128, 8 * ACCW], F32, allow_name_mangling=True))
            rcp = pha.enter_context(nc.sbuf_tensor("rcp", [128, 16], F32, allow_name_mangling=True))
            evn = pha.enter_context(nc.sbuf_tensor("evn", [128, 128], BF16, allow_name_mangling=True))
            for qh in range(2):
                with ExitStack() as ph:
                    qbuf = ph.enter_context(nc.sbuf_tensor("qbuf", [128, nh * 1024], BF16, allow_name_mangling=True))
                    kslot0 = ph.enter_context(nc.sbuf_tensor("kslot0", [128, nh * 128], BF16, allow_name_mangling=True))
                    kslot1 = ph.enter_context(nc.sbuf_tensor("kslot1", [128, nh * 128], BF16, allow_name_mangling=True))
                    vaug0 = ph.enter_context(nc.sbuf_tensor("vaug0", [128, nh * dva], BF16, allow_name_mangling=True))
                    vaug1 = ph.enter_context(nc.sbuf_tensor("vaug1", [128, nh * dva], BF16, allow_name_mangling=True))
                    pt0 = ph.enter_context(nc.sbuf_tensor("pt0", [128, 1024], BF16, allow_name_mangling=True))
                    pt1 = ph.enter_context(nc.sbuf_tensor("pt1", [128, 1024], BF16, allow_name_mangling=True))
                    ps0 = ph.enter_context(nc.psum_tensor("ps0", [128, 512], F32, allow_name_mangling=True))
                    ps1 = ph.enter_context(nc.psum_tensor("ps1", [128, 512], F32, allow_name_mangling=True))
                    ps2 = ph.enter_context(nc.psum_tensor("ps2", [128, 512], F32, allow_name_mangling=True))
                    ps3 = ph.enter_context(nc.psum_tensor("ps3", [128, 512], F32, allow_name_mangling=True))
                    pv0 = ph.enter_context(nc.psum_tensor("pv0", [128, dva], F32, allow_name_mangling=True))
                    pv1 = ph.enter_context(nc.psum_tensor("pv1", [128, dva], F32, allow_name_mangling=True))
                    pscore = [(ps0, ps1), (ps2, ps3)]
                    pv = [pv0, pv1]
                    ptile = [pt0, pt1]
                    kslot = [kslot0, kslot1]
                    vaug = [vaug0, vaug1]
                    E("vector", lambda e, v=vaug0: e.memset(v[:], 1.0))
                    E("vector", lambda e, v=vaug1: e.memset(v[:], 1.0), inc=sem_v)
                    p.barrier()

                    if layer == 0:
                        for h in range(nh):
                            for qs in range(8):
                                p.dma("sync",
                                      qbuf[:, 1024 * h + 128 * qs:1024 * h + 128 * (qs + 1)],
                                      q_sc[8 * qh + qs, :, 128 * h:128 * (h + 1)],
                                      sem_w, transpose=True)
                    else:
                        for j in range(8):
                            for qs in range(8):
                                p.dma("sync",
                                      qbuf[:, 1024 * j + 128 * qs:1024 * j + 128 * (qs + 1)],
                                      q1t[8 * qh + qs, :, 128 * j:128 * (j + 1)],
                                      sem_w, transpose=True)
                    p.barrier()

                    nslots = 8 if qh == 0 else 16
                    kmark = {0: 0, 1: 0}
                    vmark = {0: 0, 1: 0}
                    for sl in range(nslots):
                        bsel = sl % 2
                        ks, va = kslot[bsel], vaug[bsel]
                        if sl >= 2:
                            p.wait("sync", sem_t, kmark[bsel])
                            p.wait("gpsimd", sem_t, kmark[bsel])
                        if layer == 0:
                            for h in range(nh):
                                p.dma("sync", ks[:, 128 * h:128 * (h + 1)],
                                      k_sc[sl, :, 128 * h:128 * (h + 1)],
                                      sem_kp[bsel], transpose=True)
                            p.dma("gpsimd",
                                  va[:].rearrange("p (h c) -> p h c", h=nh)[:, :, 0:dv],
                                  v_sc[sl].rearrange("p (h c) -> p h c", h=nh),
                                  sem_up[bsel])
                        else:
                            for j in range(8):
                                p.dma("sync", ks[:, 128 * j:128 * (j + 1)],
                                      k1t[sl, :, 128 * j:128 * (j + 1)],
                                      sem_kp[bsel], transpose=True)
                            p.dma("gpsimd",
                                  va[:].rearrange("p (h c) -> p h c", h=nh)[:, :, 0:dv],
                                  v1_sc[sl].rearrange("p (h c) -> p h c", h=nh),
                                  sem_up[bsel])
                        wv = p.val(sem_kp[bsel])
                        gv = p.val(sem_up[bsel])

                        qstart = 128 * sl if qh == 0 else max(0, 128 * sl - 1024)
                        w_ = 1024 - qstart
                        has_diag = (qh == 0) or (sl >= 8)

                        for h in range(nh):
                            psc = pscore[(sl * nh + h) % 2]
                            pte = ptile[(sl * nh + h) % 2]
                            if layer == 0:
                                lk = ks[:, 128 * h:128 * (h + 1)]
                            else:
                                j, b = h // 2, (h % 2) * 64
                                lk = ks[b:b + 64, 128 * j:128 * (j + 1)]
                            if h == 0:
                                p.wait("tensor", sem_kp[bsel], wv)
                            nmm = (w_ + 511) // 512
                            for m in range(nmm):
                                c0 = 512 * m
                                cw = min(512, w_ - c0)
                                if layer == 0:
                                    qs_ap = qbuf[:, 1024 * h + qstart + c0:
                                                 1024 * h + qstart + c0 + cw]
                                else:
                                    b = (h % 2) * 64
                                    qs_ap = qbuf[b:b + 64,
                                                 1024 * (h // 2) + qstart + c0:
                                                 1024 * (h // 2) + qstart + c0 + cw]
                                E("tensor", lambda e, ps=psc[m], lk=lk, q=qs_ap,
                                    cw=cw: e.matmul(ps[:, 0:cw], lk, q,
                                                    start=True, stop=True),
                                  inc=(sem_t if m == nmm - 1 else None))
                            W("scalar", sem_t)
                            for m in range(nmm):
                                c0 = 512 * m
                                cw = min(512, w_ - c0)
                                E("scalar", lambda e, pt=pte, ps=psc[m], c0=c0, cw=cw,
                                    h=h, sqd=sqd: e.activation(
                                        pt[:, c0:c0 + cw], ps[:, 0:cw], AF.Exp,
                                        scale=sqd, bias=negb[:, h:h + 1]),
                                  inc=(sem_s if m == nmm - 1 else None))
                            if has_diag:
                                W("vector", sem_s)
                                E("vector", lambda e, pt=pte: e.tensor_tensor(
                                    pt[:, 0:128], pt[:, 0:128], trim[:], ALU.mult),
                                  inc=sem_v)
                                W("tensor", sem_v)
                            else:
                                W("tensor", sem_s)
                            if h == 0:
                                p.wait("tensor", sem_up[bsel], gv)
                            nqc = w_ // 128
                            for c in range(nqc):
                                pvb = pv[c % 2]
                                lt = (qstart + 128 * c) // 128
                                accsl = acc[:, ACCW * lt + dva * h:
                                            ACCW * lt + dva * (h + 1)]
                                E("tensor", lambda e, pvb=pvb, pte=pte, c=c, va=va,
                                    h=h: e.matmul(pvb[:],
                                                  pte[:, 128 * c:128 * (c + 1)],
                                                  va[:, dva * h:dva * (h + 1)],
                                                  start=True, stop=True), inc=sem_t)
                                W("vector", sem_t)
                                if sl == 0:
                                    E("vector", lambda e, a=accsl, pvb=pvb:
                                        e.tensor_copy(a, pvb[:]), inc=sem_v)
                                else:
                                    E("vector", lambda e, a=accsl, pvb=pvb:
                                        e.tensor_tensor(a, a, pvb[:], ALU.add),
                                        inc=sem_v)
                                p.wait("tensor", sem_v)
                        kmark[bsel] = p.val(sem_t)
                        vmark[bsel] = p.val(sem_v)
                    p.barrier()
                    if layer == 0 and qh == 0:
                        p.dma("sync", dbg4[:, 0:ACCW], acc[:, 0:ACCW], sem_st)
                        p.dma("sync", dbg4[:, ACCW:2 * ACCW],
                              acc[:, 7 * ACCW:8 * ACCW], sem_st)
                        p.barrier()

                with ExitStack() as ph:
                    nfc = (nh * dv // 128)
                    attnf = ph.enter_context(
                        nc.sbuf_tensor("attnf", [128, nfc * 1024], BF16, allow_name_mangling=True))
                    for lt in range(8):
                        lview = acc[:, ACCW * lt:ACCW * (lt + 1)].rearrange(
                            "p (h c) -> p h c", c=dva)[:, :, dv:dv + 1]
                        if lt > 0:
                            W("vector", sem_s)
                        E("vector", lambda e, lv=lview, nh=nh: e.reciprocal(
                            rcp[:, 0:nh].rearrange("p (h c) -> p h c", c=1), lv),
                            inc=sem_v)
                        W("scalar", sem_v)
                        for h in range(nh):
                            if layer == 0:
                                a = acc[:, ACCW * lt + dva * h:ACCW * lt + dva * h + dv]
                                E("scalar", lambda e, a=a, h=h, dv=dv: e.activation(
                                    evn[:, 0:dv], a, AF.Identity, scale=rcp[:, h:h + 1]),
                                    inc=sem_s)
                                W("sync", sem_s)
                                dst = attnf[:, 1024 * h + 128 * lt:
                                            1024 * h + 128 * (lt + 1)]
                                p.dma("sync", dst, evn[:, 0:dv], sem_w, transpose=True)
                                p.wait("scalar", sem_w)
                            else:
                                a = acc[:, ACCW * lt + dva * h:ACCW * lt + dva * h + dv]
                                eo = (h % 2) * dv
                                E("scalar", lambda e, a=a, h=h, dv=dv, eo=eo:
                                    e.activation(evn[:, eo:eo + dv], a, AF.Identity,
                                                 scale=rcp[:, h:h + 1]),
                                    inc=sem_s)
                                if h % 2 == 1:
                                    W("sync", sem_s)
                                    dst = attnf[:, 1024 * (h // 2) + 128 * lt:
                                                1024 * (h // 2) + 128 * (lt + 1)]
                                    p.dma("sync", dst, evn[:, 0:128], sem_w,
                                          transpose=True)
                                    p.wait("scalar", sem_w)
                    p.barrier()
                    dense_half(layer, qh, attnf)
            p.barrier()

    def dense_half(layer, half, attnf):
        w_in = w_dense if layer == 0 else wo
        b_row = b_dense if layer == 0 else bo
        n_dc = 16 if layer == 0 else 8
        with ExitStack() as ph:
            wstg0 = ph.enter_context(nc.sbuf_tensor("wstgd0", [128, 512], F32, allow_name_mangling=True))
            wstg1 = ph.enter_context(nc.sbuf_tensor("wstgd1", [128, 512], F32, allow_name_mangling=True))
            wbf0 = ph.enter_context(nc.sbuf_tensor("wbfd0", [128, 512], BF16, allow_name_mangling=True))
            wbf1 = ph.enter_context(nc.sbuf_tensor("wbfd1", [128, 512], BF16, allow_name_mangling=True))
            d0 = ph.enter_context(nc.psum_tensor("d0", [128, 512], F32, allow_name_mangling=True))
            d1 = ph.enter_context(nc.psum_tensor("d1", [128, 512], F32, allow_name_mangling=True))
            d2 = ph.enter_context(nc.psum_tensor("d2", [128, 512], F32, allow_name_mangling=True))
            d3 = ph.enter_context(nc.psum_tensor("d3", [128, 512], F32, allow_name_mangling=True))
            d4 = ph.enter_context(nc.psum_tensor("d4", [128, 512], F32, allow_name_mangling=True))
            d5 = ph.enter_context(nc.psum_tensor("d5", [128, 512], F32, allow_name_mangling=True))
            d6 = ph.enter_context(nc.psum_tensor("d6", [128, 512], F32, allow_name_mangling=True))
            d7 = ph.enter_context(nc.psum_tensor("d7", [128, 512], F32, allow_name_mangling=True))
            psums = [d0, d1, d2, d3, d4, d5, d6, d7]
            brow = ph.enter_context(nc.sbuf_tensor("brow", [1, D], F32, allow_name_mangling=True))
            p.dma("sync", brow[:, 0:D], b_row, sem_w)
            p.barrier()

            def load_w(oc, dc, wf, sem):
                p.dma("sync", wf[:],
                      w_in[128 * dc:128 * (dc + 1), 512 * oc:512 * (oc + 1)], sem)

            def lhsT(dc, t):
                return attnf[:, 1024 * dc + 128 * t:1024 * dc + 128 * (t + 1)]

            def bias(oc, t, ps):
                E("tensor", lambda e, ps=ps, oc=oc: e.matmul(
                    ps[:], ones_f[:], brow[:, 512 * oc:512 * (oc + 1)],
                    start=False, stop=True), inc=sem_t)

            def post(oc, t, ps, half=half):
                tg = 8 * half + t
                xs = x_sb[:, D * tg + 512 * oc:D * tg + 512 * (oc + 1)]
                E("vector", lambda e, xs=xs, ps=ps: e.tensor_tensor(
                    xs, xs, ps[:], ALU.add), inc=sem_v)

            tmark[0] = tmark[1] = 0
            stream_mm(half, 4, n_dc, load_w, lhsT, psums, post, bias_fn=bias,
                      wstage=[wstg0, wstg1], wbf=[wbf0, wbf1])

    def dense_proj(w_in, b_row, n_dc, layer):
        pass

    def quant_chain(src_bf, scol, tq, qf, dq, qi8, qxt):
        E("scalar", lambda e: e.activation(tq[:], src_bf, AF.Copy, scale=scol),
          inc=sem_s)
        W("vector", sem_s)
        E("vector", lambda e: e.tensor_copy(qi8[:], tq[:]))
        VF()
        E("vector", lambda e: e.tensor_copy(qf[:], qi8[:]))
        VF()
        E("vector", lambda e: e.tensor_tensor(dq[:], tq[:], qf[:], ALU.subtract))
        VF()
        E("vector", lambda e: e.scalar_tensor_tensor(
            tq[:], dq[:], 0.5, qf[:], ALU.is_ge, ALU.add))
        VF()
        E("vector", lambda e: e.scalar_tensor_tensor(
            qf[:], dq[:], -0.5, tq[:], ALU.is_ge, ALU.add))
        VF()
        E("vector", lambda e: e.tensor_scalar(qxt[:], qf[:], -1.0, None, ALU.add),
          inc=sem_v)

    def wstats_tern():
        mats = [(w_gT, wg_t, D, IF), (w_uT, wu_t, D, IF), (w_dT, wd_t, IF, D)]
        with ExitStack() as ph:
            stg = [ph.enter_context(nc.sbuf_tensor(f"tstg{i}", [128, 2048], F32,
                   allow_name_mangling=True)) for i in range(2)]
            tob = [ph.enter_context(nc.sbuf_tensor(f"ttob{i}", [128, 2048], BF16,
                   allow_name_mangling=True)) for i in range(2)]
            tb1 = ph.enter_context(nc.sbuf_tensor("tb1", [128, 2048], F32, allow_name_mangling=True))
            tb2 = ph.enter_context(nc.sbuf_tensor("tb2", [128, 2048], F32, allow_name_mangling=True))
            dumw = ph.enter_context(nc.sbuf_tensor("dumw", [128, 2048], F32, allow_name_mangling=True))
            accw = ph.enter_context(nc.sbuf_tensor("accw", [128, 1], F32, allow_name_mangling=True))
            colw = ph.enter_context(nc.sbuf_tensor("colw", [128, 1], F32, allow_name_mangling=True))
            pws = ph.enter_context(nc.psum_tensor("pws", [1, 8], F32, allow_name_mangling=True))
            pwb = ph.enter_context(nc.psum_tensor("pwb", [128, 8], F32, allow_name_mangling=True))
            for m, (wsrc, wdst, R, C) in enumerate(mats):
                E("vector", lambda e: e.memset(accw[:], 0.0), inc=sem_v)
                W("scalar", sem_v)
                idx = 0
                smark = {0: 0, 1: 0}
                for r in range(R // 128):
                    for c in range(C // 2048):
                        b = idx % 2
                        if idx >= 2:
                            p.wait("sync", sem_s, smark[b])
                        p.dma("sync", stg[b][:],
                              wsrc[128 * r:128 * (r + 1), 2048 * c:2048 * (c + 1)],
                              sem_wp[b])
                        p.wait("scalar", sem_wp[b])
                        E("scalar", lambda e, b=b: e.activation(
                            dumw[:], stg[b][:], AF.Abs, accum_out=colw[:]),
                            inc=sem_s)
                        smark[b] = p.val(sem_s)
                        W("vector", sem_s)
                        E("vector", lambda e: e.tensor_tensor(
                            accw[:], accw[:], colw[:], ALU.add))
                        VF()
                        idx += 1
                E("vector", lambda e: e.memset(vpad[:], 0.0), inc=sem_v)
                p.wait("tensor", sem_v)
                E("tensor", lambda e, m=m: e.matmul(
                    pws[:, m:m + 1], accw[:], onecol[:], start=True, stop=True),
                    inc=sem_t)
                W("scalar", sem_t)
                E("scalar", lambda e, m=m: e.copy(wmr[:, m:m + 1], pws[:, m:m + 1]),
                  inc=sem_s)
                W("vector", sem_s)
                E("vector", lambda e, m=m, N=R * C: e.tensor_scalar(
                    wmr[:, 4 + m:5 + m], wmr[:, m:m + 1], 1.0 / N, EPS_Q,
                    ALU.mult, ALU.max), inc=sem_v)
                p.barrier()
            for m in range(3):
                E("tensor", lambda e, m=m: e.matmul(
                    pwb[:, m:m + 1], ones_f[:], wmr[:, 4 + m:5 + m],
                    start=True, stop=True), inc=sem_t)
            W("scalar", sem_t)
            for m in range(3):
                E("scalar", lambda e, m=m: e.copy(wmc[:, 6 + m:7 + m],
                                                  pwb[:, m:m + 1]), inc=sem_s)
            W("vector", sem_s)
            for m in range(3):
                E("vector", lambda e, m=m: e.tensor_scalar(
                    wmc[:, m:m + 1], wmc[:, 6 + m:7 + m], 0.5, None, ALU.mult))
                E("vector", lambda e, m=m: e.tensor_scalar(
                    wmc[:, 3 + m:4 + m], wmc[:, 6 + m:7 + m], -0.5, None, ALU.mult))
            E("vector", lambda e: e.memset(vpad[:], 0.0), inc=sem_v)
            p.barrier()
            for m, (wsrc, wdst, R, C) in enumerate(mats):
                idx = 0
                vmark = {0: 0, 1: 0}
                gmark = {0: 0, 1: 0}
                for r in range(R // 128):
                    for c in range(C // 2048):
                        b = idx % 2
                        if idx >= 2:
                            p.wait("sync", sem_v, vmark[b])
                            p.wait("vector", sem_gp[b], gmark[b])
                        p.dma("sync", stg[b][:],
                              wsrc[128 * r:128 * (r + 1), 2048 * c:2048 * (c + 1)],
                              sem_wp[b])
                        p.wait("vector", sem_wp[b])
                        E("vector", lambda e, b=b, m=m: e.tensor_scalar(
                            tb1[:], stg[b][:], wmc[:, m:m + 1], None, ALU.is_gt))
                        E("vector", lambda e, b=b, m=m: e.tensor_scalar(
                            tb2[:], stg[b][:], wmc[:, 3 + m:4 + m], None, ALU.is_ge),
                            inc=sem_v)
                        vmark[b] = p.val(sem_v)
                        VF()
                        E("vector", lambda e: e.tensor_tensor(
                            tb1[:], tb1[:], tb2[:], ALU.add))
                        VF()
                        E("vector", lambda e, b=b: e.tensor_scalar(
                            tob[b][:], tb1[:], -1.0, None, ALU.add), inc=sem_v)
                        W("gpsimd", sem_v)
                        p.dma("gpsimd",
                              wdst[128 * r:128 * (r + 1), 2048 * c:2048 * (c + 1)],
                              tob[b][:], sem_gp[b])
                        gmark[b] = p.val(sem_gp[b])
                        idx += 1
                p.barrier()

    def ffn_gateup(half, qxnf, ph):
        wld = [ph.enter_context(nc.sbuf_tensor(f"guw{i}", [128, 512], BF16,
               allow_name_mangling=True)) for i in range(2)]
        gbuf = ph.enter_context(nc.sbuf_tensor("gbuf", [128, 8 * 512], BF16, allow_name_mangling=True))
        ub = [ph.enter_context(nc.sbuf_tensor(f"ub{i}", [128, 512], BF16,
              allow_name_mangling=True)) for i in range(2)]
        hb = [ph.enter_context(nc.sbuf_tensor(f"hbb{i}", [128, 512], BF16,
              allow_name_mangling=True)) for i in range(2)]
        hwb = ph.enter_context(nc.sbuf_tensor("hwb", [128, 512], BF16, allow_name_mangling=True))
        wlnb = ph.enter_context(nc.sbuf_tensor("wlnb", [128, IF], BF16, allow_name_mangling=True))
        wrow = ph.enter_context(nc.sbuf_tensor("wrow", [1, IF], F32, allow_name_mangling=True))
        colh = [ph.enter_context(nc.sbuf_tensor(f"colh{i}", [128, 1], F32,
                allow_name_mangling=True)) for i in range(2)]
        colm = [ph.enter_context(nc.sbuf_tensor(f"colm{i}", [128, 1], F32,
                allow_name_mangling=True)) for i in range(2)]
        dumq = ph.enter_context(nc.sbuf_tensor("dumq", [128, 512], F32, allow_name_mangling=True))
        with nc.psum_tensor("pwl", [128, 512], F32, allow_name_mangling=True) as pwl:
            p.dma("sync", wrow[:], w_ln, sem_w)
            p.barrier()
            for j in range(IF // 512):
                E("tensor", lambda e, j=j, pwl=pwl: e.matmul(
                    pwl[:], ones_f[:], wrow[:, 512 * j:512 * (j + 1)],
                    start=True, stop=True), inc=sem_t)
                W("scalar", sem_t)
                E("scalar", lambda e, j=j, pwl=pwl: e.copy(
                    wlnb[:, 512 * j:512 * (j + 1)], pwl[:]), inc=sem_s)
                p.barrier()
        with ExitStack() as ph2:
            psums = [ph2.enter_context(nc.psum_tensor(f"gup{t}", [128, 512], F32,
                     allow_name_mangling=True)) for t in range(8)]
            W("scalar", sem_v)
            E("scalar", lambda e, half=half: e.activation(
                gu[:, 0:8], sxi[:, 8 * half:8 * half + 8], AF.Copy,
                scale=wmc[:, 6:7]))
            E("scalar", lambda e, half=half: e.activation(
                gu[:, 8:16], sxi[:, 8 * half:8 * half + 8], AF.Copy,
                scale=wmc[:, 7:8]), inc=sem_s)
            p.barrier()
            idx = 0
            lmark = {0: 0, 1: 0}
            hmark = {0: 0, 1: 0}
            ubmark = {0: 0, 1: 0}
            for oc in range(16):
                for phase in range(2):
                    wsrc = wg_t if phase == 0 else wu_t
                    for dc in range(16):
                        b = idx % 2
                        if idx >= 2:
                            p.wait("sync", sem_t, lmark[b])
                        p.dma("sync", wld[b][:],
                              wsrc[128 * dc:128 * (dc + 1), 512 * oc:512 * (oc + 1)],
                              sem_wp[b])
                        p.wait("tensor", sem_wp[b])
                        for t in range(8):
                            E("tensor", lambda e, ps=psums[t], b=b, dc=dc, t=t:
                                e.matmul(ps[:],
                                         qxnf[:, 1024 * dc + 128 * t:
                                              1024 * dc + 128 * (t + 1)],
                                         wld[b][:], start=(dc == 0),
                                         stop=(dc == 15)),
                              inc=(sem_t if t == 7 else None))
                        lmark[b] = p.val(sem_t)
                        idx += 1
                    W("scalar", sem_t)
                    if phase == 0:
                        for t in range(8):
                            E("scalar", lambda e, t=t, ps=psums[t]: e.activation(
                                gbuf[:, 512 * t:512 * (t + 1)], ps[:], AF.Silu,
                                scale=gu[:, t:t + 1]), inc=sem_s)
                        p.wait("tensor", sem_s)
                    else:
                        W("vector", sem_t)
                        for t in range(8):
                            tg = 8 * half + t
                            bp = t % 2
                            if hmark[bp]:
                                p.wait("vector", sem_gp[bp], hmark[bp])
                            if ubmark[bp]:
                                p.wait("scalar", sem_v, ubmark[bp])
                            E("scalar", lambda e, t=t, ps=psums[t], bp=bp:
                                e.activation(ub[bp][:], ps[:], AF.Identity,
                                             scale=gu[:, 8 + t:9 + t]), inc=sem_s)
                            W("vector", sem_s)
                            E("vector", lambda e, t=t, bp=bp: e.tensor_tensor(
                                hb[bp][:], gbuf[:, 512 * t:512 * (t + 1)],
                                ub[bp][:], ALU.mult), inc=sem_v)
                            ubmark[bp] = p.val(sem_v)
                            VF()
                            W("scalar", sem_v)
                            E("scalar", lambda e, bp=bp: e.activation(
                                dumq[:], hb[bp][:], AF.Square,
                                accum_out=colh[bp][:]), inc=sem_s)
                            E("vector", lambda e, bp=bp, oc=oc: e.tensor_tensor(
                                hwb[:], hb[bp][:],
                                wlnb[:, 512 * oc:512 * (oc + 1)], ALU.mult))
                            VF()
                            E("vector", lambda e, bp=bp: e.reduce_max(
                                colm[bp][:], hwb[:], AX.X,
                                apply_absolute_value=True))
                            VF()
                            VF()
                            VF()
                            E("vector", lambda e, tg=tg, bp=bp: e.tensor_tensor(
                                hmax[:, tg:tg + 1], hmax[:, tg:tg + 1],
                                colm[bp][:], ALU.max))
                            W("vector", sem_s)
                            E("vector", lambda e, tg=tg, bp=bp: e.tensor_tensor(
                                hstat[:, tg:tg + 1], hstat[:, tg:tg + 1],
                                colh[bp][:], ALU.add), inc=sem_v)
                            W("gpsimd", sem_v)
                            p.dma("gpsimd", h_sc[tg, :, 512 * oc:512 * (oc + 1)],
                                  hb[bp][:], sem_gp[bp])
                            hmark[bp] = p.val(sem_gp[bp])
                        if oc < 15:
                            p.wait("tensor", sem_s)
                            p.wait("tensor", sem_v)
                        else:
                            p.barrier()

    def proj_half(half, xnf, ph):
        s1f = ph.enter_context(nc.sbuf_tensor("s1f", [128, 16 * 1024], BF16, allow_name_mangling=True))
        sqs2 = ph.enter_context(nc.sbuf_tensor("sqs2", [128, 512], F32, allow_name_mangling=True))
        st8 = ph.enter_context(nc.sbuf_tensor("st8", [128, 8], F32, allow_name_mangling=True))
        wstgs = [ph.enter_context(nc.sbuf_tensor(f"pw{i}", [128, 512], F32,
                 allow_name_mangling=True)) for i in range(2)]
        wbfs = [ph.enter_context(nc.sbuf_tensor(f"pwb{i}", [128, 512], BF16,
                allow_name_mangling=True)) for i in range(2)]
        evacs = [ph.enter_context(nc.sbuf_tensor(f"pev{i}", [128, 512], BF16,
                 allow_name_mangling=True)) for i in range(2)]
        with ExitStack() as ph2:
            psums = [ph2.enter_context(nc.psum_tensor(f"pj{t}", [128, 512], F32,
                     allow_name_mangling=True)) for t in range(8)]
            for m in range(3):
                p.dma("sync", cvec[0:1, 0:D], b1r[m], sem_w)
                p.dma("sync", cvec[0:1, D:D + DL], bl1[m], sem_w)
                p.barrier()
                s1mark = {0: 0, 1: 0}
                s2mark = {0: 0, 1: 0}
                s2vm = {0: 0, 1: 0}

                def load_s1(oc, dc, wf, sem, m=m):
                    p.dma("sync", wf[:],
                          wk1[m][128 * dc:128 * (dc + 1), 512 * oc:512 * (oc + 1)],
                          sem)

                def lhsT_xnf(dc, t):
                    return xnf[:, 1024 * dc + 128 * t:1024 * dc + 128 * (t + 1)]

                def bias_s1(oc, t, ps):
                    E("tensor", lambda e, ps=ps, oc=oc: e.matmul(
                        ps[:], ones_f[:], cvec[0:1, 512 * oc:512 * (oc + 1)],
                        start=False, stop=True), inc=sem_t)

                def post_s1(oc, t, ps, s1mark=s1mark):
                    b = t % 2
                    ev = evacs[b]
                    if s1mark[b]:
                        p.wait("scalar", sem_w, s1mark[b])
                    E("scalar", lambda e, ev=ev, ps=ps: e.copy(ev[:], ps[:]),
                      inc=sem_s)
                    W("sync", sem_s)
                    for j in range(4):
                        mc = 4 * oc + j
                        p.dma("sync",
                              s1f[:, 1024 * mc + 128 * t:1024 * mc + 128 * (t + 1)],
                              ev[:, 128 * j:128 * (j + 1)], sem_w, transpose=True)
                    s1mark[b] = p.val(sem_w)

                stream_mm(half, 4, 16, load_s1, lhsT_xnf, psums, post_s1,
                          bias_fn=bias_s1, wstage=wstgs, wbf=wbfs)

                def load_s2(oc, dc, wf, sem, m=m):
                    p.dma("sync", wf[:],
                          wl1[m][128 * dc:128 * (dc + 1), 512 * oc:512 * (oc + 1)],
                          sem)

                def lhsT_s1f(dc, t):
                    return s1f[:, 1024 * dc + 128 * t:1024 * dc + 128 * (t + 1)]

                def bias_s2(oc, t, ps):
                    E("tensor", lambda e, ps=ps, oc=oc: e.matmul(
                        ps[:], ones_f[:], cvec[0:1, D + 512 * oc:D + 512 * (oc + 1)],
                        start=False, stop=True), inc=sem_t)

                def post_s2(oc, t, ps, m=m, half=half, s2mark=s2mark,
                            s2vm=s2vm):
                    tg = 8 * half + t
                    b = t % 2
                    ev = evacs[b]
                    if s2mark[b]:
                        p.wait("scalar", sem_gp[b], s2mark[b])
                    if s2vm[b]:
                        p.wait("scalar", sem_v, s2vm[b])
                    E("scalar", lambda e, ev=ev, ps=ps: e.copy(ev[:], ps[:]),
                      inc=sem_s)
                    if m < 2:
                        W("vector", sem_s)
                        E("vector", lambda e, ev=ev: e.tensor_tensor(
                            sqs2[:], ev[:], ev[:], ALU.mult))
                        VF()
                        E("vector", lambda e: e.tensor_reduce(
                            st8[:].rearrange("p (a o) -> p a o", o=1),
                            sqs2[:].rearrange("p (a c) -> p a c", c=64),
                            AX.X, ALU.add))
                        VF()
                        VF()
                        VF()
                        hb0 = (16 if m == 0 else 0) + 8 * oc
                        E("vector", lambda e, hb0=hb0: e.tensor_tensor(
                            nrm[:, hb0:hb0 + 8], nrm[:, hb0:hb0 + 8], st8[:],
                            ALU.max), inc=sem_v)
                        s2vm[b] = p.val(sem_v)
                        W("gpsimd", sem_v)
                    else:
                        W("gpsimd", sem_s)
                    dst = [k1t, q1t, v1_sc][m]
                    p.dma("gpsimd", dst[tg, :, 512 * oc:512 * (oc + 1)], ev[:],
                          sem_gp[b])
                    s2mark[b] = p.val(sem_gp[b])

                stream_mm(half, 2, 16, load_s2, lhsT_s1f, psums, post_s2,
                          bias_fn=bias_s2, wstage=wstgs, wbf=wbfs)

    def ffn_down():
        with ExitStack() as ph:
            qhf = ph.enter_context(nc.sbuf_tensor("qhf", [128, 64 * 512], BF16, allow_name_mangling=True))
            wlnb = ph.enter_context(nc.sbuf_tensor("wlnb2", [128, IF], BF16, allow_name_mangling=True))
            hld = [ph.enter_context(nc.sbuf_tensor(f"dhl{i}", [128, 512], BF16,
                   allow_name_mangling=True)) for i in range(2)]
            hwq = ph.enter_context(nc.sbuf_tensor("hwq", [128, 512], BF16, allow_name_mangling=True))
            tqd = ph.enter_context(nc.sbuf_tensor("tqd", [128, 512], F32, allow_name_mangling=True))
            qfd = ph.enter_context(nc.sbuf_tensor("qfd", [128, 512], F32, allow_name_mangling=True))
            dqd = ph.enter_context(nc.sbuf_tensor("dqd", [128, 512], F32, allow_name_mangling=True))
            qi8d = ph.enter_context(nc.sbuf_tensor("qi8d", [128, 512], I8, allow_name_mangling=True))
            qbf = [ph.enter_context(nc.sbuf_tensor(f"qbf{i}", [128, 512], BF16,
                   allow_name_mangling=True)) for i in range(2)]
            wld = [ph.enter_context(nc.sbuf_tensor(f"dwl{i}", [128, 512], BF16,
                   allow_name_mangling=True)) for i in range(2)]
            dbf = [ph.enter_context(nc.sbuf_tensor(f"dbf{i}", [128, 512], F32,
                   allow_name_mangling=True)) for i in range(2)]
            with ExitStack() as ph2:
                psums = [ph2.enter_context(nc.psum_tensor(f"pd{t}", [128, 512],
                         F32, allow_name_mangling=True)) for t in range(4)]
                pwl = ph2.enter_context(nc.psum_tensor("pwl2", [128, 512], F32, allow_name_mangling=True))
                for hh in range(2):
                    p.dma("sync", cvec[0:1, 0:4096], w_ln[0:1, 4096 * hh:4096 * (hh + 1)], sem_w)
                    p.barrier()
                    for j in range(8):
                        E("tensor", lambda e, j=j: e.matmul(
                            pwl[:], ones_f[:], cvec[0:1, 512 * j:512 * (j + 1)],
                            start=True, stop=True), inc=sem_t)
                        W("scalar", sem_t)
                        E("scalar", lambda e, j=j, hh=hh: e.copy(
                            wlnb[:, 4096 * hh + 512 * j:4096 * hh + 512 * (j + 1)],
                            pwl[:]), inc=sem_s)
                        p.barrier()
                for grp in range(4):
                    qmark = {0: 0, 1: 0}
                    idx = 0
                    for tl in range(4):
                        tg = 4 * grp + tl
                        for oc in range(16):
                            b = idx % 2
                            if idx >= 2:
                                p.wait("scalar", sem_v, qmark[b])
                            p.dma("scalar", hld[b][:],
                                  h_sc[tg, :, 512 * oc:512 * (oc + 1)], sem_cp[b])
                            p.wait("vector", sem_cp[b])
                            W("vector", sem_s)
                            E("vector", lambda e, b=b, oc=oc: e.tensor_tensor(
                                hwq[:], hld[b][:],
                                wlnb[:, 512 * oc:512 * (oc + 1)], ALU.mult),
                                inc=sem_v)
                            qmark[b] = p.val(sem_v)
                            W("scalar", sem_v)
                            qb = qbf[idx % 2]
                            if idx >= 2:
                                p.wait("vector", sem_w, tmark2[idx % 2])
                            quant_chain(hwq[:], q2s[:, tg:tg + 1], tqd, qfd,
                                        dqd, qi8d, qb)
                            W("sync", sem_v)
                            for j in range(4):
                                c = 4 * oc + j
                                p.dma("sync",
                                      qhf[:, 512 * c + 128 * tl:
                                          512 * c + 128 * (tl + 1)],
                                      qb[:, 128 * j:128 * (j + 1)], sem_w,
                                      transpose=True)
                            tmark2[idx % 2] = p.val(sem_w)
                            idx += 1
                    p.barrier()
                    idx = 0
                    lmark = {0: 0, 1: 0}
                    dmark = {0: 0, 1: 0}
                    for oc in range(4):
                        for dc in range(64):
                            b = idx % 2
                            if idx >= 2:
                                p.wait("sync", sem_t, lmark[b])
                            p.dma("sync", wld[b][:],
                                  wd_t[128 * dc:128 * (dc + 1),
                                       512 * oc:512 * (oc + 1)], sem_wp[b])
                            p.wait("tensor", sem_wp[b])
                            for tl in range(4):
                                E("tensor", lambda e, ps=psums[tl], b=b, dc=dc,
                                    tl=tl: e.matmul(
                                        ps[:],
                                        qhf[:, 512 * dc + 128 * tl:
                                            512 * dc + 128 * (tl + 1)],
                                        wld[b][:], start=(dc == 0),
                                        stop=(dc == 63)),
                                  inc=(sem_t if tl == 3 else None))
                            lmark[b] = p.val(sem_t)
                            idx += 1
                        W("scalar", sem_t)
                        for tl in range(4):
                            tg = 4 * grp + tl
                            bp = tl % 2
                            if dmark[bp]:
                                p.wait("scalar", sem_v, dmark[bp])
                            E("scalar", lambda e, tl=tl, tg=tg, bp=bp,
                                ps=psums[tl]: e.activation(
                                    dbf[bp][:], ps[:], AF.Identity,
                                    scale=dss[:, tg:tg + 1]), inc=sem_s)
                            W("vector", sem_s)
                            E("vector", lambda e, tg=tg, oc=oc, bp=bp:
                                e.tensor_tensor(
                                    x_sb[:, D * tg + 512 * oc:
                                         D * tg + 512 * (oc + 1)],
                                    x_sb[:, D * tg + 512 * oc:
                                         D * tg + 512 * (oc + 1)],
                                    dbf[bp][:], ALU.add), inc=sem_v)
                            dmark[bp] = p.val(sem_v)
                        if oc < 3:
                            p.wait("tensor", sem_s)
                            p.wait("tensor", sem_v)
                        else:
                            p.barrier()

    def rms_finalize():
        E("vector", lambda e: e.tensor_scalar(
            rcq[:], hstat[:], 1.0 / IF, EPS_RMS, ALU.mult, ALU.add))
        VF()
        VF()
        VF()
        E("vector", lambda e: e.reciprocal(rr2[:], rcq[:]), inc=sem_v)
        W("scalar", sem_v)
        E("scalar", lambda e: e.activation(rr2[:], rr2[:], AF.Sqrt), inc=sem_s)
        W("vector", sem_s)
        E("vector", lambda e: e.tensor_tensor(hmax[:], hmax[:], rr2[:], ALU.mult))
        VF()
        VF()
        VF()
        E("vector", lambda e: e.tensor_scalar(
            hmax[:], hmax[:], EPS_Q, None, ALU.max))
        VF()
        VF()
        VF()
        E("vector", lambda e: e.reciprocal(rcq[:], hmax[:]))
        VF()
        VF()
        VF()
        E("vector", lambda e: e.tensor_tensor(q2s[:], rcq[:], rr2[:], ALU.mult))
        VF()
        VF()
        VF()
        E("vector", lambda e: e.tensor_scalar(
            q2s[:], q2s[:], 127.0, None, ALU.mult))
        E("vector", lambda e: e.tensor_scalar(
            dss[:], hmax[:], 1.0 / 127.0, None, ALU.mult), inc=sem_v)
        W("scalar", sem_v)
        E("scalar", lambda e: e.activation(
            dss[:], dss[:], AF.Copy, scale=wmc[:, 8:9]), inc=sem_s)
        p.barrier()
        p.dma("sync", dbg4[:, 0:16], sxi[:], sem_st)
        p.dma("sync", dbg4[:, 16:32], hstat[:], sem_st)
        p.dma("sync", dbg4[:, 32:48], hmax[:], sem_st)
        p.dma("sync", dbg4[:, 48:64], q2s[:], sem_st)
        p.dma("sync", dbg4[:, 64:80], dss[:], sem_st)
        p.dma("sync", dbg4[:, 80:96], rr2[:], sem_st)
        p.dma("sync", dbg4[:, 96:108], wmc[:], sem_st)
        p.dma("sync", dbg4[:, 108:124], gu[:], sem_st)
        p.barrier()

    def layer1():
        E("vector", lambda e: e.memset(nrm[:], 0.0))
        E("vector", lambda e: e.memset(hstat[:], 0.0))
        E("vector", lambda e: e.memset(hmax[:], 0.0), inc=sem_v)
        p.barrier()
        wstats_tern()
        for half in range(2):
            with ExitStack() as phq:
                qxnf = phq.enter_context(nc.sbuf_tensor("qxnf", [128, 16 * 1024], BF16, allow_name_mangling=True))
                with ExitStack() as ph:
                    xnt = ph.enter_context(nc.sbuf_tensor("xnt1", [128, D], BF16, allow_name_mangling=True))
                    dumA = ph.enter_context(nc.sbuf_tensor("dumA1", [128, D], F32, allow_name_mangling=True))
                    qfb = ph.enter_context(nc.sbuf_tensor("qfb", [128, D], F32, allow_name_mangling=True))
                    dqb = ph.enter_context(nc.sbuf_tensor("dqb", [128, D], F32, allow_name_mangling=True))
                    qi8 = ph.enter_context(nc.sbuf_tensor("qi8", [128, D], I8, allow_name_mangling=True))
                    qxt = ph.enter_context(nc.sbuf_tensor("qxt", [128, D], BF16, allow_name_mangling=True))
                    mxc = ph.enter_context(nc.sbuf_tensor("mxc", [128, 1], F32, allow_name_mangling=True))
                    stk = ph.enter_context(nc.sbuf_tensor("stk", [128, 1], F32, allow_name_mangling=True))
                    for tl in range(8):
                        t = 8 * half + tl
                        ln_stats_apply(t, xnt[:], dumA[:], dumA[:])
                        W("vector", sem_s)
                        E("vector", lambda e: e.reduce_max(
                            mxc[:], xnt[:], AX.X, apply_absolute_value=True))
                        VF()
                        VF()
                        VF()
                        E("vector", lambda e, t=t: e.tensor_scalar(
                            sxi[:, t:t + 1], mxc[:], EPS_Q, 1.0 / 127.0,
                            ALU.max, ALU.mult))
                        VF()
                        VF()
                        VF()
                        E("vector", lambda e, t=t: e.reciprocal(
                            stk[:], sxi[:, t:t + 1]), inc=sem_v)
                        W("scalar", sem_v)
                        quant_chain(xnt[:], stk[:], dumA, qfb, dqb, qi8, qxt)
                        W("sync", sem_v)
                        for dc in range(16):
                            p.dma("sync",
                                  qxnf[:, 1024 * dc + 128 * tl:
                                       1024 * dc + 128 * (tl + 1)],
                                  qxt[:, 128 * dc:128 * (dc + 1)], sem_w,
                                  transpose=True)
                        p.barrier()
                with ExitStack() as ph:
                    ffn_gateup(half, qxnf, ph)
            with ExitStack() as ph:
                xnf = ph.enter_context(nc.sbuf_tensor("xnf1", [128, 16 * 1024], BF16, allow_name_mangling=True))
                xnt = ph.enter_context(nc.sbuf_tensor("xnt2", [128, D], BF16, allow_name_mangling=True))
                for tl in range(8):
                    t = 8 * half + tl
                    E("scalar", lambda e, t=t: e.activation(
                        xnt[:], x_sb[:, D * t:D * (t + 1)], AF.Identity,
                        scale=ln_rs[:, t:t + 1], bias=ln_mu[:, t:t + 1]),
                        inc=sem_s)
                    W("sync", sem_s)
                    xn_to_xnf(tl, xnt, xnf)
                    p.barrier()
                proj_half(half, xnf, ph)
        finalize_bound(SQD1)
        attention(layer=1)
        rms_finalize()
        ffn_down()

    tmark = {0: 0, 1: 0}
    tmark2 = {0: 0, 1: 0}
    evmark = {0: 0, 1: 0}

    layer0()
    layer1()

    with nc.sbuf_tensor("oout", [128, 2 * D], F32, allow_name_mangling=True) as oout:
        p.barrier()
        for t in range(NT):
            buf = oout[:, (t % 2) * D:(t % 2) * D + D]
            if t >= 2:
                p.wait("scalar", sem_st)
            E("scalar", lambda e, b=buf, t=t: e.copy(
                b, x_sb[:, D * t:D * (t + 1)]), inc=sem_s)
            W("sync", sem_s)
            p.dma("sync", out[128 * t:128 * (t + 1), :], buf, sem_st)
        p.wait("sync", sem_st)

    with nc.Block() as block:
        block.tensor(lambda e: [f(e) for f in p.ops["tensor"]])
        block.vector(lambda e: [f(e) for f in p.ops["vector"]])
        block.scalar(lambda e: [f(e) for f in p.ops["scalar"]])
        block.sync(lambda e: [f(e) for f in p.ops["sync"]])
        block.gpsimd(lambda e: [f(e) for f in p.ops["gpsimd"]])

    es.close()
    return nc


def _prep_inputs(inputs):
    f32 = np.float32
    g = lambda k: np.ascontiguousarray(np.asarray(inputs[k], dtype=f32))

    wq = g("L0_attn_W_qkv")
    perm = np.concatenate([
        np.concatenate([np.arange(384 * h, 384 * h + 128) for h in range(16)]),
        np.concatenate([np.arange(384 * h + 128, 384 * h + 256) for h in range(16)]),
        np.concatenate([np.arange(384 * h + 256, 384 * h + 384) for h in range(16)]),
    ])
    w_qkv = np.ascontiguousarray(wq[:, perm])
    b_qkv = np.ascontiguousarray(g("L0_attn_b_qkv")[perm])[None, :]

    s = 2048
    inv_freq = (1.0 / (10000.0 ** (np.arange(0, 32, 2, dtype=f32) / 32.0))).astype(f32)
    t = np.arange(s, dtype=f32)
    freqs = np.outer(t, inv_freq).astype(f32)
    cos16 = np.cos(freqs).astype(f32)
    sin16 = np.sin(freqs).astype(f32)
    cos_r = np.ascontiguousarray(np.tile(cos16, (1, 4)))
    sin_r = np.ascontiguousarray(np.tile(sin16, (1, 4)))

    tri = (np.arange(128)[None, :] >= np.arange(128)[:, None]).astype(BF)

    base = {
        "w_qkv": w_qkv, "b_qkv": b_qkv,
        "w_dense": g("L0_attn_W_dense"), "b_dense": g("L0_attn_b_dense")[None, :],
        "w_ffn0": np.ascontiguousarray(g("L0_ffn_weight").T),
        "s_ffn0": g("L0_ffn_scale_weight")[None, :],
        "b_ffn0": g("L0_ffn_bias")[None, :],
        "wk1": g("L1_attn_wk"), "wq1": g("L1_attn_wq"), "wv1": g("L1_attn_wv"),
        "wkl": g("L1_attn_wkl"), "wql": g("L1_attn_wql"), "wvl": g("L1_attn_wvl"),
        "bk1": g("L1_attn_bk")[None, :], "bq1": g("L1_attn_bq")[None, :],
        "bv1": g("L1_attn_bv")[None, :],
        "bkl": g("L1_attn_bkl")[None, :], "bql": g("L1_attn_bql")[None, :],
        "bvl": g("L1_attn_bvl")[None, :],
        "wo": g("L1_attn_wo"), "bo": g("L1_attn_bo")[None, :],
        "w_gT": np.ascontiguousarray(g("L1_ffn_w_gate").T),
        "w_uT": np.ascontiguousarray(g("L1_ffn_w_up").T),
        "w_dT": np.ascontiguousarray(g("L1_ffn_w_down").T),
        "w_ln": g("L1_ffn_ln_weight")[None, :],
        "identity": np.eye(128, dtype=BF),
        "trimask": tri,
        "cos_r": cos_r, "sin_r": sin_r,
    }
    x = np.asarray(inputs["L0_attn_hidden_states"], dtype=f32)
    maps = []
    for c in range(8):
        m = dict(base)
        m["x"] = np.ascontiguousarray(x[c % 4].reshape(TOK, D))
        maps.append(m)
    return maps


_NC_CACHE = {}
_LAST_DBG = None


def kernel(**inputs):
    maps = _prep_inputs(inputs)
    if "nc" not in _NC_CACHE:
        _NC_CACHE["nc"] = build_program()
    nc = _NC_CACHE["nc"]
    res = run_bass_kernel_spmd(nc, maps, core_ids=list(range(8)))
    kernel._LAST_RES = res
    global _LAST_DBG
    try:
        _LAST_DBG = np.stack([np.asarray(res.results[b]["dbg"]) for b in range(8)])
        np.save("/tmp/dev_dbg2.npy", np.asarray(res.results[0]["dbg2"], dtype=np.float32))
        np.save("/tmp/dev_dbg3.npy", np.asarray(res.results[0]["dbg3"], dtype=np.float32))
        np.save("/tmp/dev_dbg4.npy", np.asarray(res.results[0]["dbg4"], dtype=np.float32))
    except Exception:
        _LAST_DBG = None
    outs = [np.asarray(res.results[b]["out"], dtype=np.float32).reshape(1, TOK // D * 0 + 2048, D)
            for b in range(4)]
    return np.concatenate(outs, axis=0)


if __name__ == "__main__":
    nc = build_program()
    print("built ok")


# version 06
# speedup vs baseline: 1.0201x; 1.0201x over previous
import math
import numpy as np
import ml_dtypes
import concourse.bass as bass
import concourse.mybir as mybir
from concourse.bass_utils import run_bass_kernel_spmd
from contextlib import ExitStack

F32 = mybir.dt.float32
BF16 = mybir.dt.bfloat16
I8 = mybir.dt.int8
AF = mybir.ActivationFunctionType
ALU = mybir.AluOpType
AX = mybir.AxisListType

D = 2048
NT = 16
HT = 8
TOK = 128 * NT
H0, DH0 = 16, 128
H1, DH1 = 16, 64
DL = 1024
IF = 8192
EPS_LN = 1e-5
EPS_RMS = 1e-6
EPS_Q = 1e-5
SQD0 = 1.0 / math.sqrt(DH0)
SQD1 = 1.0 / math.sqrt(DH1)
BF = np.dtype(ml_dtypes.bfloat16)

DEPTH = 2
_DBG_SCRATCH = False


class Prog:
    ENGINES = ("tensor", "vector", "scalar", "sync", "gpsimd")

    def __init__(self, nc, es):
        self.nc = nc
        self.es = es
        self.ops = {e: [] for e in self.ENGINES}
        self.semvals = {}
        self.sems = {}

    def sem(self, name):
        s = self.es.enter_context(self.nc.semaphore(name))
        self.semvals[name] = 0
        self.sems[name] = s
        return s

    def emit(self, engine, fn, inc=None, inc_by=1):
        if inc is not None:
            self.semvals[inc.name] += inc_by

            def wrapped(eng, fn=fn, inc=inc, n=inc_by):
                fn(eng).then_inc(inc, n)
            self.ops[engine].append(wrapped)
        else:
            self.ops[engine].append(fn)

    def dma(self, engine, out, in_, sem, transpose=False, slow=False):
        self.emit(engine,
                  lambda e, o=out, i=in_, tr=transpose, sl=slow: e.dma_start(
                      out=o, in_=i, transpose=tr, allow_slow_non_contiguous=sl),
                  inc=sem, inc_by=16)

    def wait(self, engine, sem, val=None):
        v = self.semvals[sem.name] if val is None else val
        self.ops[engine].append(lambda e, s=sem, v=v: e.wait_ge(s, v))

    def val(self, sem):
        return self.semvals[sem.name]

    def barrier(self):
        vals = [(s, self.semvals[n]) for n, s in self.sems.items()
                if n != "sem_st"]
        for e in self.ENGINES:
            for s, v in vals:
                if v > 0:
                    self.ops[e].append(lambda eng, s=s, v=v: eng.wait_ge(s, v))


def build_program():
    nc = bass.Bass("TRN2", target_bir_lowering=False)
    es = ExitStack()
    p = Prog(nc, es)

    def din(name, shape, dtype=F32):
        return nc.dram_tensor(name, list(shape), dtype, kind="ExternalInput").ap()

    def dsc(name, shape, dtype=BF16):
        kind = "ExternalOutput" if _DBG_SCRATCH else "Internal"
        return nc.dram_tensor(name, list(shape), dtype, kind=kind).ap()

    x_in = din("x", (TOK, D))
    out = nc.dram_tensor("out", [TOK, D], F32, kind="ExternalOutput").ap()
    dbg = nc.dram_tensor("dbg", [128, 96], F32, kind="ExternalOutput").ap()
    dbg2 = nc.dram_tensor("dbg2", [128, 3 * D], BF16, kind="ExternalOutput").ap()
    dbg3 = nc.dram_tensor("dbg3", [128, 4 * D], BF16, kind="ExternalOutput").ap()
    dbg4 = nc.dram_tensor("dbg4", [128, 4 * 2064], F32, kind="ExternalOutput").ap()

    w_qkv = din("w_qkv", (16, 12, 128, 512))
    b_qkv = din("b_qkv", (1, 3 * D))
    w_dense = din("w_dense", (16, 4, 128, 512))
    b_dense = din("b_dense", (1, D))
    w_ffn0 = din("w_ffn0", (16, 4, 128, 512))
    s_ffn0 = din("s_ffn0", (1, D))
    b_ffn0 = din("b_ffn0", (1, D))
    wk1 = [din(f"w{m}1", (16, 4, 128, 512)) for m in "kqv"]
    wl1 = [din(f"w{m}l", (16, 2, 128, 512)) for m in "kqv"]
    b1r = [din(f"b{m}1", (1, D)) for m in "kqv"]
    bl1 = [din(f"b{m}l", (1, DL)) for m in "kqv"]
    wo = din("wo", (8, 4, 128, 512))
    bo = din("bo", (1, D))
    w_gT = din("w_gT", (16, 4, 128, 2048))
    w_uT = din("w_uT", (16, 4, 128, 2048))
    w_dT = din("w_dT", (64, 1, 128, 2048))
    w_ln = din("w_ln", (1, IF))

    identity = din("identity", (128, 128), BF16)
    trimask = din("trimask", (128, 128), BF16)
    cos_r = din("cos_r", (TOK, 64))
    sin_r = din("sin_r", (TOK, 64))

    q_sc = dsc("q_sc", (NT, 128, D))
    k_sc = dsc("k_sc", (NT, 128, D))
    v_sc = dsc("v_sc", (NT, 128, D))
    k1t = dsc("k1t", (NT, 128, DL))
    q1t = dsc("q1t", (NT, 128, DL))
    v1_sc = dsc("v1_sc", (NT, 128, DL))
    wg_t = dsc("wg_t", (16, 16, 128, 512))
    wu_t = dsc("wu_t", (16, 16, 128, 512))
    wd_t = dsc("wd_t", (64, 4, 128, 512))
    h_sc = dsc("h_sc", (NT, 16, 128, 512))
    misc_sc = dsc("misc_sc", (1, 4096), F32)

    def sba(name, shape, dtype=BF16):
        return es.enter_context(nc.sbuf_tensor(name, list(shape), dtype))

    x_sb = sba("x_sb", (128, NT * D))
    idn = sba("idn", (128, 128))
    trim = sba("trim", (128, 128))
    ones_f = sba("ones_f", (1, 128), F32)
    onecol = sba("onecol", (128, 1), F32)
    cvec = sba("cvec", (1, 3 * D), F32)
    ln_mu = sba("ln_mu", (128, NT), F32)
    ln_rs = sba("ln_rs", (128, NT), F32)
    nrm = sba("nrm", (128, 32), F32)
    negb = sba("negb", (128, 16), F32)
    hstat = sba("hstat", (128, NT), F32)
    hmax = sba("hmax", (128, NT), F32)
    hfac = sba("hfac", (128, NT), F32)
    beffc = sba("beffc", (128, 3 * 8), F32)
    colA = sba("colA", (128, 1), F32)
    colB = sba("colB", (128, 1), F32)
    colC = sba("colC", (128, 1), F32)
    colD = sba("colD", (128, 1), F32)
    r16 = sba("r16", (1, 64), F32)
    vpad = sba("vpad", (128, 1), F32)
    wmr = sba("wmr", (1, 16), F32)
    wmc = sba("wmc", (128, 12), F32)
    sxi = sba("sxi", (128, NT), F32)
    q2s = sba("q2s", (128, NT), F32)
    dss = sba("dss", (128, NT), F32)
    rr2 = sba("rr2", (128, NT), F32)
    rcq = sba("rcq", (128, NT), F32)
    gu = sba("gu", (128, 16), F32)

    sem_w = p.sem("sem_w")
    sem_wp = [p.sem("sem_w0"), p.sem("sem_w1")]
    sem_cp = [p.sem("sem_c0"), p.sem("sem_c1")]
    sem_gp = [p.sem("sem_g0"), p.sem("sem_g1")]
    sem_kp = [p.sem("sem_k0"), p.sem("sem_k1")]
    sem_up = [p.sem("sem_u0"), p.sem("sem_u1")]
    sem_c = p.sem("sem_c")
    sem_g = p.sem("sem_g")
    sem_t = p.sem("sem_t")
    sem_v = p.sem("sem_v")
    sem_s = p.sem("sem_s")
    sem_st = p.sem("sem_st")

    E = p.emit
    W = p.wait

    def VF():
        E("vector", lambda e: e.memset(vpad[:], 0.0))

    p.dma("sync", idn[:], identity, sem_w)
    p.dma("sync", trim[:], trimask, sem_w)

    with nc.sbuf_tensor("xld", [128, 2 * D], F32, allow_name_mangling=True) as xld:
        for t in range(NT):
            buf = xld[:, (t % 2) * D:(t % 2) * D + D]
            if t >= 2:
                p.wait("scalar", sem_v, p.val(sem_v) - 1)
            p.dma("scalar", buf, x_in[128 * t:128 * (t + 1), :], sem_cp[t % 2])
            p.wait("vector", sem_cp[t % 2])
            E("vector", lambda e, b=buf, t=t: e.tensor_copy(
                x_sb[:, D * t:D * (t + 1)], b), inc=sem_v)
    E("vector", lambda e: e.memset(ones_f[:], 1.0))
    E("vector", lambda e: e.memset(onecol[:], 1.0), inc=sem_v)
    p.barrier()

    def ln_stats_apply(t, xn_out, dumA, dumB):
        xt = x_sb[:, D * t:D * (t + 1)]
        E("scalar", lambda e, xt=xt, s=dumA: e.activation(
            s, xt, AF.Copy, accum_out=colA[:]))
        E("scalar", lambda e, xt=xt, s=dumB: e.activation(
            s, xt, AF.Square, accum_out=colB[:]), inc=sem_s)
        W("vector", sem_s)
        E("vector", lambda e: e.tensor_scalar(colA[:], colA[:], 1.0 / D, None, ALU.mult))
        E("vector", lambda e: e.tensor_scalar(colB[:], colB[:], 1.0 / D, None, ALU.mult))
        E("vector", lambda e: e.tensor_tensor(colC[:], colA[:], colA[:], ALU.mult))
        VF()
        E("vector", lambda e: e.tensor_tensor(colB[:], colB[:], colC[:], ALU.subtract))
        VF()
        E("vector", lambda e: e.tensor_scalar(colB[:], colB[:], EPS_LN, None, ALU.add))
        VF()
        E("vector", lambda e: e.reciprocal(colB[:], colB[:]), inc=sem_v)
        W("scalar", sem_v)
        E("scalar", lambda e, t=t: e.activation(ln_rs[:, t:t + 1], colB[:], AF.Sqrt), inc=sem_s)
        W("vector", sem_s)
        E("vector", lambda e, t=t: e.scalar_tensor_tensor(
            ln_mu[:, t:t + 1], colA[:], -1.0, ln_rs[:, t:t + 1],
            ALU.mult, ALU.mult), inc=sem_v)
        W("scalar", sem_v)
        E("scalar", lambda e, t=t, xo=xn_out, xt=xt: e.activation(
            xo, xt, AF.Identity, scale=ln_rs[:, t:t + 1], bias=ln_mu[:, t:t + 1]),
            inc=sem_s)

    def xn_to_xnf(t_local, xn_tile, xnf):
        for dc in range(16):
            p.dma("sync", xnf[:, 1024 * dc + 128 * t_local:1024 * dc + 128 * (t_local + 1)],
                  xn_tile[:, 128 * dc:128 * (dc + 1)], sem_w, transpose=True)

    def stream_mm(half, n_oc, n_dc, load_chunk, lhsT_fn, psums, post_fn,
                  bias_fn=None, cast_scale=None, wstage=None, wbf=None):
        idx = 0
        for oc in range(n_oc):
            for dc in range(n_dc):
                bsel = idx % 2
                wf = wstage[bsel]
                wb = wbf[bsel]
                if idx >= 2:
                    p.wait("sync", sem_t, tmark[bsel])
                load_chunk(oc, dc, wf, sem_wp[bsel])
                p.wait("vector", sem_wp[bsel])
                if cast_scale is None:
                    E("vector", lambda e, wf=wf, wb=wb: e.tensor_copy(wb[:], wf[:]),
                      inc=sem_v)
                else:
                    E("vector", lambda e, wf=wf, wb=wb, oc=oc: e.tensor_tensor(
                        wb[:], wf[:], cast_scale(oc), ALU.mult), inc=sem_v)
                p.wait("tensor", sem_v)
                for t in range(8):
                    st = (dc == 0)
                    sp = (dc == n_dc - 1) and (bias_fn is None)
                    E("tensor", lambda e, ps=psums[t], l=lhsT_fn(dc, t), wb=wb,
                        st=st, sp=sp: e.matmul(ps[:], l, wb[:], start=st, stop=sp),
                      inc=(sem_t if t == 7 else None))
                tmark[bsel] = p.val(sem_t)
                idx += 1
            if bias_fn is not None:
                for t in range(8):
                    bias_fn(oc, t, psums[t])
            W("scalar", sem_t)
            W("vector", sem_t)
            for t in range(8):
                post_fn(oc, t, psums[t])
            if oc < n_oc - 1:
                p.wait("tensor", sem_s)
                p.wait("tensor", sem_v)
            else:
                p.barrier()

    def layer0():
        with ExitStack() as ph:
            xnf = ph.enter_context(nc.sbuf_tensor("xnf", [128, 16 * 1024], BF16, allow_name_mangling=True))
            xnt = ph.enter_context(nc.sbuf_tensor("xnt", [128, D], BF16, allow_name_mangling=True))
            dumA = ph.enter_context(nc.sbuf_tensor("dumA", [128, D], F32, allow_name_mangling=True))
            dumB = ph.enter_context(nc.sbuf_tensor("dumB", [128, D], F32, allow_name_mangling=True))
            cosr = ph.enter_context(nc.sbuf_tensor("cosr", [128, NT * 64], F32, allow_name_mangling=True))
            sinr = ph.enter_context(nc.sbuf_tensor("sinr", [128, NT * 64], F32, allow_name_mangling=True))
            wstg0 = ph.enter_context(nc.sbuf_tensor("wstg0", [128, 512], F32, allow_name_mangling=True))
            wstg1 = ph.enter_context(nc.sbuf_tensor("wstg1", [128, 512], F32, allow_name_mangling=True))
            wbf0 = ph.enter_context(nc.sbuf_tensor("wbf0", [128, 512], BF16, allow_name_mangling=True))
            wbf1 = ph.enter_context(nc.sbuf_tensor("wbf1", [128, 512], BF16, allow_name_mangling=True))
            evac0 = ph.enter_context(nc.sbuf_tensor("evac0", [128, 512], BF16, allow_name_mangling=True))
            evac1 = ph.enter_context(nc.sbuf_tensor("evac1", [128, 512], BF16, allow_name_mangling=True))
            ropes = ph.enter_context(nc.sbuf_tensor("ropes", [128, 128], F32, allow_name_mangling=True))
            sqs = ph.enter_context(nc.sbuf_tensor("sqs", [128, 512], F32, allow_name_mangling=True))
            st16 = ph.enter_context(nc.sbuf_tensor("st16", [128, 16], F32, allow_name_mangling=True))
            s0bc = ph.enter_context(nc.sbuf_tensor("s0bc", [128, D], F32, allow_name_mangling=True))
            pool = ph.enter_context(nc.sbuf_tensor("pool", [1, 4 * D], F32, allow_name_mangling=True))
            p.dma("sync", cosr[:].rearrange("p (t c) -> p t c", c=64),
                  cos_r.rearrange("(t p) c -> p t c", p=128), sem_w)
            p.dma("sync", sinr[:].rearrange("p (t c) -> p t c", c=64),
                  sin_r.rearrange("(t p) c -> p t c", p=128), sem_w)
            E("vector", lambda e: e.memset(nrm[:], 0.0), inc=sem_v)
            with nc.psum_tensor("pb", [128, 512], F32, allow_name_mangling=True) as pb:
                p.dma("sync", pool[:, 0:D], s_ffn0, sem_w)
                p.barrier()
                for j in range(4):
                    E("tensor", lambda e, j=j, pb=pb: e.matmul(
                        pb[:], ones_f[:], pool[:, 512 * j:512 * (j + 1)],
                        start=True, stop=True), inc=sem_t)
                    W("scalar", sem_t)
                    E("scalar", lambda e, j=j, pb=pb: e.copy(
                        s0bc[:, 512 * j:512 * (j + 1)], pb[:]), inc=sem_s)
                    p.barrier()
            p.barrier()
            p.dma("sync", pool[:, 0:3 * D], b_qkv, sem_w)
            p.dma("sync", pool[:, 3 * D:4 * D], b_ffn0, sem_w)

            for half in range(2):
                for tl in range(8):
                    t = 8 * half + tl
                    ln_stats_apply(t, xnt[:], dumA[:], dumB[:])
                    W("sync", sem_s)
                    xn_to_xnf(tl, xnt, xnf)
                    p.barrier()

                if half == 0:
                    p.dma("sync", dbg3[:, 3 * D:4 * D], xnt[:], sem_st)
                    p.dma("sync", dbg4[:, 2 * 2064:2 * 2064 + 16], ln_rs[:, 0:16], sem_st)
                    p.dma("sync", dbg4[:, 2 * 2064 + 16:2 * 2064 + 32], ln_mu[:, 0:16], sem_st)
                    p.dma("sync", dbg3[:, 0:D], x_sb[:, 0:D], sem_st)
                    for dc in range(16):
                        p.dma("sync", dbg3[:, D + 128 * dc:D + 128 * (dc + 1)],
                              xnf[:, 1024 * dc:1024 * dc + 128], sem_st)
                    p.barrier()
                with ExitStack() as ph2:
                    p0 = ph2.enter_context(nc.psum_tensor("p0", [128, 512], F32, allow_name_mangling=True))
                    p1 = ph2.enter_context(nc.psum_tensor("p1", [128, 512], F32, allow_name_mangling=True))
                    p2 = ph2.enter_context(nc.psum_tensor("p2", [128, 512], F32, allow_name_mangling=True))
                    p3 = ph2.enter_context(nc.psum_tensor("p3", [128, 512], F32, allow_name_mangling=True))
                    p4 = ph2.enter_context(nc.psum_tensor("p4", [128, 512], F32, allow_name_mangling=True))
                    p5 = ph2.enter_context(nc.psum_tensor("p5", [128, 512], F32, allow_name_mangling=True))
                    p6 = ph2.enter_context(nc.psum_tensor("p6", [128, 512], F32, allow_name_mangling=True))
                    p7 = ph2.enter_context(nc.psum_tensor("p7", [128, 512], F32, allow_name_mangling=True))
                    psums = [p0, p1, p2, p3, p4, p5, p6, p7]

                    def load_qkv(oc, dc, wf, sem):
                        p.dma("sync", wf[:], w_qkv[dc, oc], sem)

                    def lhsT_xnf(dc, t):
                        return xnf[:, 1024 * dc + 128 * t:1024 * dc + 128 * (t + 1)]

                    def bias_qkv(oc, t, ps):
                        E("tensor", lambda e, ps=ps, oc=oc: e.matmul(
                            ps[:], ones_f[:], pool[:, 512 * oc:512 * (oc + 1)],
                            start=False, stop=True), inc=sem_t)

                    def post_qkv(oc, t, ps, half=half):
                        tg = 8 * half + t
                        ev = evac0 if t % 2 == 0 else evac1
                        if evmark[t % 2]:
                            p.wait("scalar", sem_gp[t % 2], evmark[t % 2])
                        E("scalar", lambda e, ev=ev, ps=ps: e.copy(ev[:], ps[:]),
                          inc=sem_s)
                        if oc == 0 and t == 0 and half == 0:
                            W("sync", sem_s)
                            p.dma("sync", dbg3[:, 2 * D:2 * D + 512], ev[:], sem_st)
                        if oc < 8:
                            W("vector", sem_s)
                            co = cosr[:, 64 * tg:64 * (tg + 1)].rearrange(
                                "p (j c) -> p j c", j=1)
                            si = sinr[:, 64 * tg:64 * (tg + 1)].rearrange(
                                "p (j c) -> p j c", j=1)
                            co4 = cosr[:, 64 * tg:64 * (tg + 1)].rearrange(
                                "p (c j) -> p c j", j=1)
                            Av = ev[:].rearrange("p (j c) -> p j c", j=4)[:, :, 0:16]
                            Bv = ev[:].rearrange("p (j c) -> p j c", j=4)[:, :, 16:32]
                            cv = cosr[:, 64 * tg:64 * (tg + 1)].rearrange(
                                "p (j c) -> p j c", j=4)
                            sv = sinr[:, 64 * tg:64 * (tg + 1)].rearrange(
                                "p (j c) -> p j c", j=4)
                            T1 = ropes[:, 0:64].rearrange("p (j c) -> p j c", j=4)
                            T2 = ropes[:, 64:128].rearrange("p (j c) -> p j c", j=4)
                            E("vector", lambda e, A=Av, s=sv, T=T1: e.tensor_tensor(T, A, s, ALU.mult))
                            E("vector", lambda e, B=Bv, s=sv, T=T2: e.tensor_tensor(T, B, s, ALU.mult))
                            E("vector", lambda e, A=Av, c=cv: e.tensor_tensor(A, A, c, ALU.mult))
                            E("vector", lambda e, B=Bv, c=cv: e.tensor_tensor(B, B, c, ALU.mult))
                            E("vector", lambda e, A=Av, T=T2: e.tensor_tensor(A, A, T, ALU.subtract))
                            E("vector", lambda e, B=Bv, T=T1: e.tensor_tensor(B, B, T, ALU.add))
                            VF()
                            E("vector", lambda e, ev=ev: e.tensor_tensor(
                                sqs[:], ev[:], ev[:], ALU.mult))
                            VF()
                            E("vector", lambda e: e.tensor_reduce(
                                st16[:, 0:4].rearrange("p (a o) -> p a o", o=1),
                                sqs[:].rearrange("p (a c) -> p a c", c=128),
                                AX.X, ALU.add))
                            VF()
                            hb = 4 * oc
                            E("vector", lambda e, hb=hb: e.tensor_tensor(
                                nrm[:, hb:hb + 4], nrm[:, hb:hb + 4],
                                st16[:, 0:4], ALU.max), inc=sem_v)
                            W("gpsimd", sem_v)
                            dst = (q_sc if oc < 4 else k_sc)
                            cb = 512 * (oc % 4)
                            p.dma("gpsimd", dst[tg, :, cb:cb + 512], ev[:],
                                  sem_gp[t % 2])
                        else:
                            W("gpsimd", sem_s)
                            cb = 512 * (oc - 8)
                            p.dma("gpsimd", v_sc[tg, :, cb:cb + 512], ev[:],
                                  sem_gp[t % 2])
                        evmark[t % 2] = p.val(sem_gp[t % 2])

                    tmark[0] = tmark[1] = 0
                    evmark[0] = evmark[1] = 0
                    stream_mm(half, 12, 16, load_qkv, lhsT_xnf, psums, post_qkv,
                              bias_fn=bias_qkv, wstage=[wstg0, wstg1],
                              wbf=[wbf0, wbf1])

                    def load_f0(oc, dc, wf, sem):
                        p.dma("sync", wf[:], w_ffn0[dc, oc], sem)

                    def bias_f0(oc, t, ps):
                        E("tensor", lambda e, ps=ps, oc=oc: e.matmul(
                            ps[:], ones_f[:], pool[:, 3 * D + 512 * oc:3 * D + 512 * (oc + 1)],
                            start=False, stop=True), inc=sem_t)

                    def post_f0(oc, t, ps, half=half):
                        tg = 8 * half + t
                        xs = x_sb[:, D * tg + 512 * oc:D * tg + 512 * (oc + 1)]
                        E("vector", lambda e, xs=xs, ps=ps: e.tensor_tensor(
                            xs, xs, ps[:], ALU.add), inc=sem_v)

                    def scale_f0(oc):
                        return s0bc[:, 512 * oc:512 * (oc + 1)]

                    stream_mm(half, 4, 16, load_f0, lhsT_xnf, psums, post_f0,
                              bias_fn=bias_f0, cast_scale=scale_f0,
                              wstage=[wstg0, wstg1], wbf=[wbf0, wbf1])

            finalize_bound(SQD0)
            p.dma("sync", dbg2[:, 0:D], q_sc[0].rearrange("p c -> p c"), sem_st)
            p.dma("sync", dbg2[:, D:2 * D], k_sc[0], sem_st)
            p.dma("sync", dbg2[:, 2 * D:3 * D], v_sc[0], sem_st)
            p.barrier()

        attention(layer=0)
        dense_proj(w_in=w_dense, b_row=b_dense, n_dc=16, layer=0)

    def finalize_bound(sqd):
        with ExitStack() as ph:
            ntp = ph.enter_context(nc.sbuf_tensor("ntp", [32, 128], F32, allow_name_mangling=True))
            nmx = ph.enter_context(nc.sbuf_tensor("nmx", [32, 1], F32, allow_name_mangling=True))
            pnb = ph.enter_context(nc.psum_tensor("pnb", [128, 512], F32, allow_name_mangling=True))
            p.barrier()
            W("sync", sem_v, p.val(sem_v))
            p.dma("sync", misc_sc[0:1, 0:4096].rearrange("a (p c) -> a p c", p=128),
                  nrm[:, 0:32].rearrange("p c -> p c"), sem_w)
            p.wait("sync", sem_w)
            p.dma("sync", ntp[:],
                  misc_sc[0:1, 0:4096].rearrange("a (p c) -> a c p", p=128)[0],
                  sem_w, slow=True)
            p.wait("vector", sem_w)
            E("vector", lambda e: e.reduce_max(
                nmx[:], ntp[:], AX.X), inc=sem_v)
            VF()
            W("sync", sem_v)
            p.dma("sync", misc_sc[0:1, 0:32].rearrange("a (p c) -> a p c", c=1),
                  nmx[:], sem_w)
            p.wait("sync", sem_w)
            p.dma("sync", r16[:, 0:32], misc_sc[0:1, 0:32], sem_w)
            p.wait("scalar", sem_w)
            E("scalar", lambda e: e.activation(r16[:, 0:32], r16[:, 0:32], AF.Sqrt),
              inc=sem_s)
            W("vector", sem_s)
            E("vector", lambda e, sqd=sqd: e.scalar_tensor_tensor(
                r16[:, 32:48], r16[:, 0:16], -sqd, r16[:, 16:32],
                ALU.mult, ALU.mult), inc=sem_v)
            p.wait("tensor", sem_v)
            E("tensor", lambda e: e.matmul(
                pnb[:, 0:16], ones_f[:], r16[:, 32:48], start=True, stop=True),
                inc=sem_t)
            W("scalar", sem_t)
            E("scalar", lambda e: e.copy(negb[:], pnb[:, 0:16]), inc=sem_s)
            p.barrier()
            p.dma("sync", dbg[:, 0:32], nrm[:, 0:32], sem_st)
            p.dma("sync", dbg[:, 32:48], negb[:, 0:16], sem_st)
            p.emit("sync", lambda e: e.dma_start(out=dbg[0:32, 48:49], in_=ntp[0:32, 0:1],
                   allow_slow_non_contiguous=True), inc=sem_st, inc_by=16)
            p.dma("sync", dbg[0:1, 49:49+32], r16[:, 0:32], sem_st)
            p.barrier()

    def attention(layer):
        if layer == 0:
            nh, dv, sqd = H0, DH0, SQD0
        else:
            nh, dv, sqd = H1, DH1, SQD1
        dva = dv + 1
        ACCW = nh * dva
        with ExitStack() as pha:
            acc = pha.enter_context(nc.sbuf_tensor("acc", [128, 8 * ACCW], F32, allow_name_mangling=True))
            rcp = pha.enter_context(nc.sbuf_tensor("rcp", [128, 16], F32, allow_name_mangling=True))
            evn = pha.enter_context(nc.sbuf_tensor("evn", [128, 128], BF16, allow_name_mangling=True))
            for qh in range(2):
                with ExitStack() as ph:
                    qbuf = ph.enter_context(nc.sbuf_tensor("qbuf", [128, nh * 1024], BF16, allow_name_mangling=True))
                    kslot0 = ph.enter_context(nc.sbuf_tensor("kslot0", [128, nh * 128], BF16, allow_name_mangling=True))
                    kslot1 = ph.enter_context(nc.sbuf_tensor("kslot1", [128, nh * 128], BF16, allow_name_mangling=True))
                    vaug0 = ph.enter_context(nc.sbuf_tensor("vaug0", [128, nh * dva], BF16, allow_name_mangling=True))
                    vaug1 = ph.enter_context(nc.sbuf_tensor("vaug1", [128, nh * dva], BF16, allow_name_mangling=True))
                    pt0 = ph.enter_context(nc.sbuf_tensor("pt0", [128, 1024], BF16, allow_name_mangling=True))
                    pt1 = ph.enter_context(nc.sbuf_tensor("pt1", [128, 1024], BF16, allow_name_mangling=True))
                    ps0 = ph.enter_context(nc.psum_tensor("ps0", [128, 512], F32, allow_name_mangling=True))
                    ps1 = ph.enter_context(nc.psum_tensor("ps1", [128, 512], F32, allow_name_mangling=True))
                    ps2 = ph.enter_context(nc.psum_tensor("ps2", [128, 512], F32, allow_name_mangling=True))
                    ps3 = ph.enter_context(nc.psum_tensor("ps3", [128, 512], F32, allow_name_mangling=True))
                    pv0 = ph.enter_context(nc.psum_tensor("pv0", [128, dva], F32, allow_name_mangling=True))
                    pv1 = ph.enter_context(nc.psum_tensor("pv1", [128, dva], F32, allow_name_mangling=True))
                    pscore = [(ps0, ps1), (ps2, ps3)]
                    pv = [pv0, pv1]
                    ptile = [pt0, pt1]
                    kslot = [kslot0, kslot1]
                    vaug = [vaug0, vaug1]
                    E("vector", lambda e, v=vaug0: e.memset(v[:], 1.0))
                    E("vector", lambda e, v=vaug1: e.memset(v[:], 1.0), inc=sem_v)
                    p.barrier()

                    if layer == 0:
                        for h in range(nh):
                            for qs in range(8):
                                p.dma("sync",
                                      qbuf[:, 1024 * h + 128 * qs:1024 * h + 128 * (qs + 1)],
                                      q_sc[8 * qh + qs, :, 128 * h:128 * (h + 1)],
                                      sem_w, transpose=True)
                    else:
                        for j in range(8):
                            for qs in range(8):
                                p.dma("sync",
                                      qbuf[:, 1024 * j + 128 * qs:1024 * j + 128 * (qs + 1)],
                                      q1t[8 * qh + qs, :, 128 * j:128 * (j + 1)],
                                      sem_w, transpose=True)
                    p.barrier()

                    nslots = 8 if qh == 0 else 16
                    kmark = {0: 0, 1: 0}
                    vmark = {0: 0, 1: 0}
                    for sl in range(nslots):
                        bsel = sl % 2
                        ks, va = kslot[bsel], vaug[bsel]
                        if sl >= 2:
                            p.wait("sync", sem_t, kmark[bsel])
                            p.wait("gpsimd", sem_t, kmark[bsel])
                        if layer == 0:
                            for h in range(nh):
                                p.dma("sync", ks[:, 128 * h:128 * (h + 1)],
                                      k_sc[sl, :, 128 * h:128 * (h + 1)],
                                      sem_kp[bsel], transpose=True)
                            p.dma("gpsimd",
                                  va[:].rearrange("p (h c) -> p h c", h=nh)[:, :, 0:dv],
                                  v_sc[sl].rearrange("p (h c) -> p h c", h=nh),
                                  sem_up[bsel])
                        else:
                            for j in range(8):
                                p.dma("sync", ks[:, 128 * j:128 * (j + 1)],
                                      k1t[sl, :, 128 * j:128 * (j + 1)],
                                      sem_kp[bsel], transpose=True)
                            p.dma("gpsimd",
                                  va[:].rearrange("p (h c) -> p h c", h=nh)[:, :, 0:dv],
                                  v1_sc[sl].rearrange("p (h c) -> p h c", h=nh),
                                  sem_up[bsel])
                        wv = p.val(sem_kp[bsel])
                        gv = p.val(sem_up[bsel])

                        qstart = 128 * sl if qh == 0 else max(0, 128 * sl - 1024)
                        w_ = 1024 - qstart
                        has_diag = (qh == 0) or (sl >= 8)

                        for h in range(nh):
                            psc = pscore[(sl * nh + h) % 2]
                            pte = ptile[(sl * nh + h) % 2]
                            if layer == 0:
                                lk = ks[:, 128 * h:128 * (h + 1)]
                            else:
                                j, b = h // 2, (h % 2) * 64
                                lk = ks[b:b + 64, 128 * j:128 * (j + 1)]
                            if h == 0:
                                p.wait("tensor", sem_kp[bsel], wv)
                            nmm = (w_ + 511) // 512
                            for m in range(nmm):
                                c0 = 512 * m
                                cw = min(512, w_ - c0)
                                if layer == 0:
                                    qs_ap = qbuf[:, 1024 * h + qstart + c0:
                                                 1024 * h + qstart + c0 + cw]
                                else:
                                    b = (h % 2) * 64
                                    qs_ap = qbuf[b:b + 64,
                                                 1024 * (h // 2) + qstart + c0:
                                                 1024 * (h // 2) + qstart + c0 + cw]
                                E("tensor", lambda e, ps=psc[m], lk=lk, q=qs_ap,
                                    cw=cw: e.matmul(ps[:, 0:cw], lk, q,
                                                    start=True, stop=True),
                                  inc=(sem_t if m == nmm - 1 else None))
                            W("scalar", sem_t)
                            for m in range(nmm):
                                c0 = 512 * m
                                cw = min(512, w_ - c0)
                                E("scalar", lambda e, pt=pte, ps=psc[m], c0=c0, cw=cw,
                                    h=h, sqd=sqd: e.activation(
                                        pt[:, c0:c0 + cw], ps[:, 0:cw], AF.Exp,
                                        scale=sqd, bias=negb[:, h:h + 1]),
                                  inc=(sem_s if m == nmm - 1 else None))
                            if has_diag:
                                W("vector", sem_s)
                                E("vector", lambda e, pt=pte: e.tensor_tensor(
                                    pt[:, 0:128], pt[:, 0:128], trim[:], ALU.mult),
                                  inc=sem_v)
                                W("tensor", sem_v)
                            else:
                                W("tensor", sem_s)
                            if h == 0:
                                p.wait("tensor", sem_up[bsel], gv)
                            nqc = w_ // 128
                            for c in range(nqc):
                                pvb = pv[c % 2]
                                lt = (qstart + 128 * c) // 128
                                accsl = acc[:, ACCW * lt + dva * h:
                                            ACCW * lt + dva * (h + 1)]
                                E("tensor", lambda e, pvb=pvb, pte=pte, c=c, va=va,
                                    h=h: e.matmul(pvb[:],
                                                  pte[:, 128 * c:128 * (c + 1)],
                                                  va[:, dva * h:dva * (h + 1)],
                                                  start=True, stop=True), inc=sem_t)
                                W("vector", sem_t)
                                if sl == 0:
                                    E("vector", lambda e, a=accsl, pvb=pvb:
                                        e.tensor_copy(a, pvb[:]), inc=sem_v)
                                else:
                                    E("vector", lambda e, a=accsl, pvb=pvb:
                                        e.tensor_tensor(a, a, pvb[:], ALU.add),
                                        inc=sem_v)
                                p.wait("tensor", sem_v)
                        kmark[bsel] = p.val(sem_t)
                        vmark[bsel] = p.val(sem_v)
                    p.barrier()
                    if layer == 0 and qh == 0:
                        p.dma("sync", dbg4[:, 0:ACCW], acc[:, 0:ACCW], sem_st)
                        p.dma("sync", dbg4[:, ACCW:2 * ACCW],
                              acc[:, 7 * ACCW:8 * ACCW], sem_st)
                        p.barrier()

                with ExitStack() as ph:
                    nfc = (nh * dv // 128)
                    attnf = ph.enter_context(
                        nc.sbuf_tensor("attnf", [128, nfc * 1024], BF16, allow_name_mangling=True))
                    for lt in range(8):
                        lview = acc[:, ACCW * lt:ACCW * (lt + 1)].rearrange(
                            "p (h c) -> p h c", c=dva)[:, :, dv:dv + 1]
                        if lt > 0:
                            W("vector", sem_s)
                        E("vector", lambda e, lv=lview, nh=nh: e.reciprocal(
                            rcp[:, 0:nh].rearrange("p (h c) -> p h c", c=1), lv),
                            inc=sem_v)
                        W("scalar", sem_v)
                        for h in range(nh):
                            if layer == 0:
                                a = acc[:, ACCW * lt + dva * h:ACCW * lt + dva * h + dv]
                                E("scalar", lambda e, a=a, h=h, dv=dv: e.activation(
                                    evn[:, 0:dv], a, AF.Identity, scale=rcp[:, h:h + 1]),
                                    inc=sem_s)
                                W("sync", sem_s)
                                dst = attnf[:, 1024 * h + 128 * lt:
                                            1024 * h + 128 * (lt + 1)]
                                p.dma("sync", dst, evn[:, 0:dv], sem_w, transpose=True)
                                p.wait("scalar", sem_w)
                            else:
                                a = acc[:, ACCW * lt + dva * h:ACCW * lt + dva * h + dv]
                                eo = (h % 2) * dv
                                E("scalar", lambda e, a=a, h=h, dv=dv, eo=eo:
                                    e.activation(evn[:, eo:eo + dv], a, AF.Identity,
                                                 scale=rcp[:, h:h + 1]),
                                    inc=sem_s)
                                if h % 2 == 1:
                                    W("sync", sem_s)
                                    dst = attnf[:, 1024 * (h // 2) + 128 * lt:
                                                1024 * (h // 2) + 128 * (lt + 1)]
                                    p.dma("sync", dst, evn[:, 0:128], sem_w,
                                          transpose=True)
                                    p.wait("scalar", sem_w)
                    p.barrier()
                    dense_half(layer, qh, attnf)
            p.barrier()

    def dense_half(layer, half, attnf):
        w_in = w_dense if layer == 0 else wo
        b_row = b_dense if layer == 0 else bo
        n_dc = 16 if layer == 0 else 8
        with ExitStack() as ph:
            wstg0 = ph.enter_context(nc.sbuf_tensor("wstgd0", [128, 512], F32, allow_name_mangling=True))
            wstg1 = ph.enter_context(nc.sbuf_tensor("wstgd1", [128, 512], F32, allow_name_mangling=True))
            wbf0 = ph.enter_context(nc.sbuf_tensor("wbfd0", [128, 512], BF16, allow_name_mangling=True))
            wbf1 = ph.enter_context(nc.sbuf_tensor("wbfd1", [128, 512], BF16, allow_name_mangling=True))
            d0 = ph.enter_context(nc.psum_tensor("d0", [128, 512], F32, allow_name_mangling=True))
            d1 = ph.enter_context(nc.psum_tensor("d1", [128, 512], F32, allow_name_mangling=True))
            d2 = ph.enter_context(nc.psum_tensor("d2", [128, 512], F32, allow_name_mangling=True))
            d3 = ph.enter_context(nc.psum_tensor("d3", [128, 512], F32, allow_name_mangling=True))
            d4 = ph.enter_context(nc.psum_tensor("d4", [128, 512], F32, allow_name_mangling=True))
            d5 = ph.enter_context(nc.psum_tensor("d5", [128, 512], F32, allow_name_mangling=True))
            d6 = ph.enter_context(nc.psum_tensor("d6", [128, 512], F32, allow_name_mangling=True))
            d7 = ph.enter_context(nc.psum_tensor("d7", [128, 512], F32, allow_name_mangling=True))
            psums = [d0, d1, d2, d3, d4, d5, d6, d7]
            brow = ph.enter_context(nc.sbuf_tensor("brow", [1, D], F32, allow_name_mangling=True))
            p.dma("sync", brow[:, 0:D], b_row, sem_w)
            p.barrier()

            def load_w(oc, dc, wf, sem):
                p.dma("sync", wf[:], w_in[dc, oc], sem)

            def lhsT(dc, t):
                return attnf[:, 1024 * dc + 128 * t:1024 * dc + 128 * (t + 1)]

            def bias(oc, t, ps):
                E("tensor", lambda e, ps=ps, oc=oc: e.matmul(
                    ps[:], ones_f[:], brow[:, 512 * oc:512 * (oc + 1)],
                    start=False, stop=True), inc=sem_t)

            def post(oc, t, ps, half=half):
                tg = 8 * half + t
                xs = x_sb[:, D * tg + 512 * oc:D * tg + 512 * (oc + 1)]
                E("vector", lambda e, xs=xs, ps=ps: e.tensor_tensor(
                    xs, xs, ps[:], ALU.add), inc=sem_v)

            tmark[0] = tmark[1] = 0
            stream_mm(half, 4, n_dc, load_w, lhsT, psums, post, bias_fn=bias,
                      wstage=[wstg0, wstg1], wbf=[wbf0, wbf1])

    def dense_proj(w_in, b_row, n_dc, layer):
        pass

    def quant_chain(src_bf, scol, tq, qf, dq, qi8, qxt):
        E("scalar", lambda e: e.activation(tq[:], src_bf, AF.Copy, scale=scol),
          inc=sem_s)
        W("vector", sem_s)
        E("vector", lambda e: e.tensor_copy(qi8[:], tq[:]))
        VF()
        E("vector", lambda e: e.tensor_copy(qf[:], qi8[:]))
        VF()
        E("vector", lambda e: e.tensor_tensor(dq[:], tq[:], qf[:], ALU.subtract))
        VF()
        E("vector", lambda e: e.scalar_tensor_tensor(
            tq[:], dq[:], 0.5, qf[:], ALU.is_ge, ALU.add))
        VF()
        E("vector", lambda e: e.scalar_tensor_tensor(
            qf[:], dq[:], -0.5, tq[:], ALU.is_ge, ALU.add))
        VF()
        E("vector", lambda e: e.tensor_scalar(qxt[:], qf[:], -1.0, None, ALU.add),
          inc=sem_v)

    def wstats_tern():
        mats = [(w_gT, wg_t, D, IF), (w_uT, wu_t, D, IF), (w_dT, wd_t, IF, D)]
        with ExitStack() as ph:
            stg = [ph.enter_context(nc.sbuf_tensor(f"tstg{i}", [128, 2048], F32,
                   allow_name_mangling=True)) for i in range(2)]
            tob = [ph.enter_context(nc.sbuf_tensor(f"ttob{i}", [128, 2048], BF16,
                   allow_name_mangling=True)) for i in range(2)]
            tb1 = ph.enter_context(nc.sbuf_tensor("tb1", [128, 2048], F32, allow_name_mangling=True))
            tb2 = ph.enter_context(nc.sbuf_tensor("tb2", [128, 2048], F32, allow_name_mangling=True))
            dumw = ph.enter_context(nc.sbuf_tensor("dumw", [128, 2048], F32, allow_name_mangling=True))
            accw = ph.enter_context(nc.sbuf_tensor("accw", [128, 1], F32, allow_name_mangling=True))
            colw = ph.enter_context(nc.sbuf_tensor("colw", [128, 1], F32, allow_name_mangling=True))
            pws = ph.enter_context(nc.psum_tensor("pws", [1, 8], F32, allow_name_mangling=True))
            pwb = ph.enter_context(nc.psum_tensor("pwb", [128, 8], F32, allow_name_mangling=True))
            for m, (wsrc, wdst, R, C) in enumerate(mats):
                E("vector", lambda e: e.memset(accw[:], 0.0), inc=sem_v)
                W("scalar", sem_v)
                idx = 0
                smark = {0: 0, 1: 0}
                for r in range(R // 128):
                    for c in range(C // 2048):
                        b = idx % 2
                        if idx >= 2:
                            p.wait("sync", sem_s, smark[b])
                        p.dma("sync", stg[b][:], wsrc[r, c], sem_wp[b])
                        p.wait("scalar", sem_wp[b])
                        E("scalar", lambda e, b=b: e.activation(
                            dumw[:], stg[b][:], AF.Abs, accum_out=colw[:]),
                            inc=sem_s)
                        smark[b] = p.val(sem_s)
                        W("vector", sem_s)
                        E("vector", lambda e: e.tensor_tensor(
                            accw[:], accw[:], colw[:], ALU.add))
                        VF()
                        idx += 1
                E("vector", lambda e: e.memset(vpad[:], 0.0), inc=sem_v)
                p.wait("tensor", sem_v)
                E("tensor", lambda e, m=m: e.matmul(
                    pws[:, m:m + 1], accw[:], onecol[:], start=True, stop=True),
                    inc=sem_t)
                W("scalar", sem_t)
                E("scalar", lambda e, m=m: e.copy(wmr[:, m:m + 1], pws[:, m:m + 1]),
                  inc=sem_s)
                W("vector", sem_s)
                E("vector", lambda e, m=m, N=R * C: e.tensor_scalar(
                    wmr[:, 4 + m:5 + m], wmr[:, m:m + 1], 1.0 / N, EPS_Q,
                    ALU.mult, ALU.max), inc=sem_v)
                p.barrier()
            for m in range(3):
                E("tensor", lambda e, m=m: e.matmul(
                    pwb[:, m:m + 1], ones_f[:], wmr[:, 4 + m:5 + m],
                    start=True, stop=True), inc=sem_t)
            W("scalar", sem_t)
            for m in range(3):
                E("scalar", lambda e, m=m: e.copy(wmc[:, 6 + m:7 + m],
                                                  pwb[:, m:m + 1]), inc=sem_s)
            W("vector", sem_s)
            for m in range(3):
                E("vector", lambda e, m=m: e.tensor_scalar(
                    wmc[:, m:m + 1], wmc[:, 6 + m:7 + m], 0.5, None, ALU.mult))
                E("vector", lambda e, m=m: e.tensor_scalar(
                    wmc[:, 3 + m:4 + m], wmc[:, 6 + m:7 + m], -0.5, None, ALU.mult))
            E("vector", lambda e: e.memset(vpad[:], 0.0), inc=sem_v)
            p.barrier()
            for m, (wsrc, wdst, R, C) in enumerate(mats):
                idx = 0
                vmark = {0: 0, 1: 0}
                gmark = {0: 0, 1: 0}
                for r in range(R // 128):
                    for c in range(C // 2048):
                        b = idx % 2
                        if idx >= 2:
                            p.wait("sync", sem_v, vmark[b])
                            p.wait("vector", sem_gp[b], gmark[b])
                        p.dma("sync", stg[b][:], wsrc[r, c], sem_wp[b])
                        p.wait("vector", sem_wp[b])
                        E("vector", lambda e, b=b, m=m: e.tensor_scalar(
                            tb1[:], stg[b][:], wmc[:, m:m + 1], None, ALU.is_gt))
                        E("vector", lambda e, b=b, m=m: e.tensor_scalar(
                            tb2[:], stg[b][:], wmc[:, 3 + m:4 + m], None, ALU.is_ge),
                            inc=sem_v)
                        vmark[b] = p.val(sem_v)
                        VF()
                        E("vector", lambda e: e.tensor_tensor(
                            tb1[:], tb1[:], tb2[:], ALU.add))
                        VF()
                        E("vector", lambda e, b=b: e.tensor_scalar(
                            tob[b][:], tb1[:], -1.0, None, ALU.add), inc=sem_v)
                        W("gpsimd", sem_v)
                        for j in range(4):
                            p.dma("gpsimd", wdst[r, 4 * c + j],
                                  tob[b][:, 512 * j:512 * (j + 1)], sem_gp[b])
                        gmark[b] = p.val(sem_gp[b])
                        idx += 1
                p.barrier()

    def ffn_gateup(half, qxnf, ph):
        wld = [ph.enter_context(nc.sbuf_tensor(f"guw{i}", [128, 512], BF16,
               allow_name_mangling=True)) for i in range(2)]
        gbuf = ph.enter_context(nc.sbuf_tensor("gbuf", [128, 8 * 512], BF16, allow_name_mangling=True))
        ub = [ph.enter_context(nc.sbuf_tensor(f"ub{i}", [128, 512], BF16,
              allow_name_mangling=True)) for i in range(2)]
        hb = [ph.enter_context(nc.sbuf_tensor(f"hbb{i}", [128, 512], BF16,
              allow_name_mangling=True)) for i in range(2)]
        hwb = ph.enter_context(nc.sbuf_tensor("hwb", [128, 512], BF16, allow_name_mangling=True))
        wlnb = ph.enter_context(nc.sbuf_tensor("wlnb", [128, IF], BF16, allow_name_mangling=True))
        wrow = ph.enter_context(nc.sbuf_tensor("wrow", [1, IF], F32, allow_name_mangling=True))
        colh = [ph.enter_context(nc.sbuf_tensor(f"colh{i}", [128, 1], F32,
                allow_name_mangling=True)) for i in range(2)]
        colm = [ph.enter_context(nc.sbuf_tensor(f"colm{i}", [128, 1], F32,
                allow_name_mangling=True)) for i in range(2)]
        dumq = ph.enter_context(nc.sbuf_tensor("dumq", [128, 512], F32, allow_name_mangling=True))
        with nc.psum_tensor("pwl", [128, 512], F32, allow_name_mangling=True) as pwl:
            p.dma("sync", wrow[:], w_ln, sem_w)
            p.barrier()
            for j in range(IF // 512):
                E("tensor", lambda e, j=j, pwl=pwl: e.matmul(
                    pwl[:], ones_f[:], wrow[:, 512 * j:512 * (j + 1)],
                    start=True, stop=True), inc=sem_t)
                W("scalar", sem_t)
                E("scalar", lambda e, j=j, pwl=pwl: e.copy(
                    wlnb[:, 512 * j:512 * (j + 1)], pwl[:]), inc=sem_s)
                p.barrier()
        with ExitStack() as ph2:
            psums = [ph2.enter_context(nc.psum_tensor(f"gup{t}", [128, 512], F32,
                     allow_name_mangling=True)) for t in range(8)]
            W("scalar", sem_v)
            E("scalar", lambda e, half=half: e.activation(
                gu[:, 0:8], sxi[:, 8 * half:8 * half + 8], AF.Copy,
                scale=wmc[:, 6:7]))
            E("scalar", lambda e, half=half: e.activation(
                gu[:, 8:16], sxi[:, 8 * half:8 * half + 8], AF.Copy,
                scale=wmc[:, 7:8]), inc=sem_s)
            p.barrier()
            idx = 0
            lmark = {0: 0, 1: 0}
            hmark = {0: 0, 1: 0}
            ubmark = {0: 0, 1: 0}
            for oc in range(16):
                for phase in range(2):
                    wsrc = wg_t if phase == 0 else wu_t
                    for dc in range(16):
                        b = idx % 2
                        if idx >= 2:
                            p.wait("sync", sem_t, lmark[b])
                        p.dma("sync", wld[b][:], wsrc[dc, oc], sem_wp[b])
                        p.wait("tensor", sem_wp[b])
                        for t in range(8):
                            E("tensor", lambda e, ps=psums[t], b=b, dc=dc, t=t:
                                e.matmul(ps[:],
                                         qxnf[:, 1024 * dc + 128 * t:
                                              1024 * dc + 128 * (t + 1)],
                                         wld[b][:], start=(dc == 0),
                                         stop=(dc == 15)),
                              inc=(sem_t if t == 7 else None))
                        lmark[b] = p.val(sem_t)
                        idx += 1
                    W("scalar", sem_t)
                    if phase == 0:
                        for t in range(8):
                            E("scalar", lambda e, t=t, ps=psums[t]: e.activation(
                                gbuf[:, 512 * t:512 * (t + 1)], ps[:], AF.Silu,
                                scale=gu[:, t:t + 1]), inc=sem_s)
                        p.wait("tensor", sem_s)
                    else:
                        W("vector", sem_t)
                        for t in range(8):
                            tg = 8 * half + t
                            bp = t % 2
                            if hmark[bp]:
                                p.wait("vector", sem_gp[bp], hmark[bp])
                            if ubmark[bp]:
                                p.wait("scalar", sem_v, ubmark[bp])
                            E("scalar", lambda e, t=t, ps=psums[t], bp=bp:
                                e.activation(ub[bp][:], ps[:], AF.Identity,
                                             scale=gu[:, 8 + t:9 + t]), inc=sem_s)
                            W("vector", sem_s)
                            E("vector", lambda e, t=t, bp=bp: e.tensor_tensor(
                                hb[bp][:], gbuf[:, 512 * t:512 * (t + 1)],
                                ub[bp][:], ALU.mult), inc=sem_v)
                            ubmark[bp] = p.val(sem_v)
                            VF()
                            W("scalar", sem_v)
                            E("scalar", lambda e, bp=bp: e.activation(
                                dumq[:], hb[bp][:], AF.Square,
                                accum_out=colh[bp][:]), inc=sem_s)
                            E("vector", lambda e, bp=bp, oc=oc: e.tensor_tensor(
                                hwb[:], hb[bp][:],
                                wlnb[:, 512 * oc:512 * (oc + 1)], ALU.mult))
                            VF()
                            E("vector", lambda e, bp=bp: e.reduce_max(
                                colm[bp][:], hwb[:], AX.X,
                                apply_absolute_value=True))
                            VF()
                            VF()
                            VF()
                            E("vector", lambda e, tg=tg, bp=bp: e.tensor_tensor(
                                hmax[:, tg:tg + 1], hmax[:, tg:tg + 1],
                                colm[bp][:], ALU.max))
                            W("vector", sem_s)
                            E("vector", lambda e, tg=tg, bp=bp: e.tensor_tensor(
                                hstat[:, tg:tg + 1], hstat[:, tg:tg + 1],
                                colh[bp][:], ALU.add), inc=sem_v)
                            W("gpsimd", sem_v)
                            p.dma("gpsimd", h_sc[tg, oc], hb[bp][:], sem_gp[bp])
                            hmark[bp] = p.val(sem_gp[bp])
                        if oc < 15:
                            p.wait("tensor", sem_s)
                            p.wait("tensor", sem_v)
                        else:
                            p.barrier()

    def proj_half(half, xnf, ph):
        s1f = ph.enter_context(nc.sbuf_tensor("s1f", [128, 16 * 1024], BF16, allow_name_mangling=True))
        sqs2 = ph.enter_context(nc.sbuf_tensor("sqs2", [128, 512], F32, allow_name_mangling=True))
        st8 = ph.enter_context(nc.sbuf_tensor("st8", [128, 8], F32, allow_name_mangling=True))
        wstgs = [ph.enter_context(nc.sbuf_tensor(f"pw{i}", [128, 512], F32,
                 allow_name_mangling=True)) for i in range(2)]
        wbfs = [ph.enter_context(nc.sbuf_tensor(f"pwb{i}", [128, 512], BF16,
                allow_name_mangling=True)) for i in range(2)]
        evacs = [ph.enter_context(nc.sbuf_tensor(f"pev{i}", [128, 512], BF16,
                 allow_name_mangling=True)) for i in range(2)]
        with ExitStack() as ph2:
            psums = [ph2.enter_context(nc.psum_tensor(f"pj{t}", [128, 512], F32,
                     allow_name_mangling=True)) for t in range(8)]
            for m in range(3):
                p.dma("sync", cvec[0:1, 0:D], b1r[m], sem_w)
                p.dma("sync", cvec[0:1, D:D + DL], bl1[m], sem_w)
                p.barrier()
                s1mark = {0: 0, 1: 0}
                s2mark = {0: 0, 1: 0}
                s2vm = {0: 0, 1: 0}

                def load_s1(oc, dc, wf, sem, m=m):
                    p.dma("sync", wf[:], wk1[m][dc, oc], sem)

                def lhsT_xnf(dc, t):
                    return xnf[:, 1024 * dc + 128 * t:1024 * dc + 128 * (t + 1)]

                def bias_s1(oc, t, ps):
                    E("tensor", lambda e, ps=ps, oc=oc: e.matmul(
                        ps[:], ones_f[:], cvec[0:1, 512 * oc:512 * (oc + 1)],
                        start=False, stop=True), inc=sem_t)

                def post_s1(oc, t, ps, s1mark=s1mark):
                    b = t % 2
                    ev = evacs[b]
                    if s1mark[b]:
                        p.wait("scalar", sem_w, s1mark[b])
                    E("scalar", lambda e, ev=ev, ps=ps: e.copy(ev[:], ps[:]),
                      inc=sem_s)
                    W("sync", sem_s)
                    for j in range(4):
                        mc = 4 * oc + j
                        p.dma("sync",
                              s1f[:, 1024 * mc + 128 * t:1024 * mc + 128 * (t + 1)],
                              ev[:, 128 * j:128 * (j + 1)], sem_w, transpose=True)
                    s1mark[b] = p.val(sem_w)

                stream_mm(half, 4, 16, load_s1, lhsT_xnf, psums, post_s1,
                          bias_fn=bias_s1, wstage=wstgs, wbf=wbfs)

                def load_s2(oc, dc, wf, sem, m=m):
                    p.dma("sync", wf[:], wl1[m][dc, oc], sem)

                def lhsT_s1f(dc, t):
                    return s1f[:, 1024 * dc + 128 * t:1024 * dc + 128 * (t + 1)]

                def bias_s2(oc, t, ps):
                    E("tensor", lambda e, ps=ps, oc=oc: e.matmul(
                        ps[:], ones_f[:], cvec[0:1, D + 512 * oc:D + 512 * (oc + 1)],
                        start=False, stop=True), inc=sem_t)

                def post_s2(oc, t, ps, m=m, half=half, s2mark=s2mark,
                            s2vm=s2vm):
                    tg = 8 * half + t
                    b = t % 2
                    ev = evacs[b]
                    if s2mark[b]:
                        p.wait("scalar", sem_gp[b], s2mark[b])
                    if s2vm[b]:
                        p.wait("scalar", sem_v, s2vm[b])
                    E("scalar", lambda e, ev=ev, ps=ps: e.copy(ev[:], ps[:]),
                      inc=sem_s)
                    if m < 2:
                        W("vector", sem_s)
                        E("vector", lambda e, ev=ev: e.tensor_tensor(
                            sqs2[:], ev[:], ev[:], ALU.mult))
                        VF()
                        E("vector", lambda e: e.tensor_reduce(
                            st8[:].rearrange("p (a o) -> p a o", o=1),
                            sqs2[:].rearrange("p (a c) -> p a c", c=64),
                            AX.X, ALU.add))
                        VF()
                        VF()
                        VF()
                        hb0 = (16 if m == 0 else 0) + 8 * oc
                        E("vector", lambda e, hb0=hb0: e.tensor_tensor(
                            nrm[:, hb0:hb0 + 8], nrm[:, hb0:hb0 + 8], st8[:],
                            ALU.max), inc=sem_v)
                        s2vm[b] = p.val(sem_v)
                        W("gpsimd", sem_v)
                    else:
                        W("gpsimd", sem_s)
                    dst = [k1t, q1t, v1_sc][m]
                    p.dma("gpsimd", dst[tg, :, 512 * oc:512 * (oc + 1)], ev[:],
                          sem_gp[b])
                    s2mark[b] = p.val(sem_gp[b])

                stream_mm(half, 2, 16, load_s2, lhsT_s1f, psums, post_s2,
                          bias_fn=bias_s2, wstage=wstgs, wbf=wbfs)

    def ffn_down():
        with ExitStack() as ph:
            qhf = ph.enter_context(nc.sbuf_tensor("qhf", [128, 64 * 512], BF16, allow_name_mangling=True))
            wlnb = ph.enter_context(nc.sbuf_tensor("wlnb2", [128, IF], BF16, allow_name_mangling=True))
            hld = [ph.enter_context(nc.sbuf_tensor(f"dhl{i}", [128, 512], BF16,
                   allow_name_mangling=True)) for i in range(2)]
            hwq = ph.enter_context(nc.sbuf_tensor("hwq", [128, 512], BF16, allow_name_mangling=True))
            tqd = ph.enter_context(nc.sbuf_tensor("tqd", [128, 512], F32, allow_name_mangling=True))
            qfd = ph.enter_context(nc.sbuf_tensor("qfd", [128, 512], F32, allow_name_mangling=True))
            dqd = ph.enter_context(nc.sbuf_tensor("dqd", [128, 512], F32, allow_name_mangling=True))
            qi8d = ph.enter_context(nc.sbuf_tensor("qi8d", [128, 512], I8, allow_name_mangling=True))
            qbf = [ph.enter_context(nc.sbuf_tensor(f"qbf{i}", [128, 512], BF16,
                   allow_name_mangling=True)) for i in range(2)]
            wld = [ph.enter_context(nc.sbuf_tensor(f"dwl{i}", [128, 512], BF16,
                   allow_name_mangling=True)) for i in range(2)]
            dbf = [ph.enter_context(nc.sbuf_tensor(f"dbf{i}", [128, 512], F32,
                   allow_name_mangling=True)) for i in range(2)]
            with ExitStack() as ph2:
                psums = [ph2.enter_context(nc.psum_tensor(f"pd{t}", [128, 512],
                         F32, allow_name_mangling=True)) for t in range(4)]
                pwl = ph2.enter_context(nc.psum_tensor("pwl2", [128, 512], F32, allow_name_mangling=True))
                for hh in range(2):
                    p.dma("sync", cvec[0:1, 0:4096], w_ln[0:1, 4096 * hh:4096 * (hh + 1)], sem_w)
                    p.barrier()
                    for j in range(8):
                        E("tensor", lambda e, j=j: e.matmul(
                            pwl[:], ones_f[:], cvec[0:1, 512 * j:512 * (j + 1)],
                            start=True, stop=True), inc=sem_t)
                        W("scalar", sem_t)
                        E("scalar", lambda e, j=j, hh=hh: e.copy(
                            wlnb[:, 4096 * hh + 512 * j:4096 * hh + 512 * (j + 1)],
                            pwl[:]), inc=sem_s)
                        p.barrier()
                for grp in range(4):
                    qmark = {0: 0, 1: 0}
                    idx = 0
                    for tl in range(4):
                        tg = 4 * grp + tl
                        for oc in range(16):
                            b = idx % 2
                            if idx >= 2:
                                p.wait("scalar", sem_v, qmark[b])
                            p.dma("scalar", hld[b][:], h_sc[tg, oc], sem_cp[b])
                            p.wait("vector", sem_cp[b])
                            W("vector", sem_s)
                            E("vector", lambda e, b=b, oc=oc: e.tensor_tensor(
                                hwq[:], hld[b][:],
                                wlnb[:, 512 * oc:512 * (oc + 1)], ALU.mult),
                                inc=sem_v)
                            qmark[b] = p.val(sem_v)
                            W("scalar", sem_v)
                            qb = qbf[idx % 2]
                            if idx >= 2:
                                p.wait("vector", sem_w, tmark2[idx % 2])
                            quant_chain(hwq[:], q2s[:, tg:tg + 1], tqd, qfd,
                                        dqd, qi8d, qb)
                            W("sync", sem_v)
                            for j in range(4):
                                c = 4 * oc + j
                                p.dma("sync",
                                      qhf[:, 512 * c + 128 * tl:
                                          512 * c + 128 * (tl + 1)],
                                      qb[:, 128 * j:128 * (j + 1)], sem_w,
                                      transpose=True)
                            tmark2[idx % 2] = p.val(sem_w)
                            idx += 1
                    p.barrier()
                    idx = 0
                    lmark = {0: 0, 1: 0}
                    dmark = {0: 0, 1: 0}
                    for oc in range(4):
                        for dc in range(64):
                            b = idx % 2
                            if idx >= 2:
                                p.wait("sync", sem_t, lmark[b])
                            p.dma("sync", wld[b][:], wd_t[dc, oc], sem_wp[b])
                            p.wait("tensor", sem_wp[b])
                            for tl in range(4):
                                E("tensor", lambda e, ps=psums[tl], b=b, dc=dc,
                                    tl=tl: e.matmul(
                                        ps[:],
                                        qhf[:, 512 * dc + 128 * tl:
                                            512 * dc + 128 * (tl + 1)],
                                        wld[b][:], start=(dc == 0),
                                        stop=(dc == 63)),
                                  inc=(sem_t if tl == 3 else None))
                            lmark[b] = p.val(sem_t)
                            idx += 1
                        W("scalar", sem_t)
                        for tl in range(4):
                            tg = 4 * grp + tl
                            bp = tl % 2
                            if dmark[bp]:
                                p.wait("scalar", sem_v, dmark[bp])
                            E("scalar", lambda e, tl=tl, tg=tg, bp=bp,
                                ps=psums[tl]: e.activation(
                                    dbf[bp][:], ps[:], AF.Identity,
                                    scale=dss[:, tg:tg + 1]), inc=sem_s)
                            W("vector", sem_s)
                            E("vector", lambda e, tg=tg, oc=oc, bp=bp:
                                e.tensor_tensor(
                                    x_sb[:, D * tg + 512 * oc:
                                         D * tg + 512 * (oc + 1)],
                                    x_sb[:, D * tg + 512 * oc:
                                         D * tg + 512 * (oc + 1)],
                                    dbf[bp][:], ALU.add), inc=sem_v)
                            dmark[bp] = p.val(sem_v)
                        if oc < 3:
                            p.wait("tensor", sem_s)
                            p.wait("tensor", sem_v)
                        else:
                            p.barrier()

    def rms_finalize():
        E("vector", lambda e: e.tensor_scalar(
            rcq[:], hstat[:], 1.0 / IF, EPS_RMS, ALU.mult, ALU.add))
        VF()
        VF()
        VF()
        E("vector", lambda e: e.reciprocal(rr2[:], rcq[:]), inc=sem_v)
        W("scalar", sem_v)
        E("scalar", lambda e: e.activation(rr2[:], rr2[:], AF.Sqrt), inc=sem_s)
        W("vector", sem_s)
        E("vector", lambda e: e.tensor_tensor(hmax[:], hmax[:], rr2[:], ALU.mult))
        VF()
        VF()
        VF()
        E("vector", lambda e: e.tensor_scalar(
            hmax[:], hmax[:], EPS_Q, None, ALU.max))
        VF()
        VF()
        VF()
        E("vector", lambda e: e.reciprocal(rcq[:], hmax[:]))
        VF()
        VF()
        VF()
        E("vector", lambda e: e.tensor_tensor(q2s[:], rcq[:], rr2[:], ALU.mult))
        VF()
        VF()
        VF()
        E("vector", lambda e: e.tensor_scalar(
            q2s[:], q2s[:], 127.0, None, ALU.mult))
        E("vector", lambda e: e.tensor_scalar(
            dss[:], hmax[:], 1.0 / 127.0, None, ALU.mult), inc=sem_v)
        W("scalar", sem_v)
        E("scalar", lambda e: e.activation(
            dss[:], dss[:], AF.Copy, scale=wmc[:, 8:9]), inc=sem_s)
        p.barrier()
        p.dma("sync", dbg4[:, 0:16], sxi[:], sem_st)
        p.dma("sync", dbg4[:, 16:32], hstat[:], sem_st)
        p.dma("sync", dbg4[:, 32:48], hmax[:], sem_st)
        p.dma("sync", dbg4[:, 48:64], q2s[:], sem_st)
        p.dma("sync", dbg4[:, 64:80], dss[:], sem_st)
        p.dma("sync", dbg4[:, 80:96], rr2[:], sem_st)
        p.dma("sync", dbg4[:, 96:108], wmc[:], sem_st)
        p.dma("sync", dbg4[:, 108:124], gu[:], sem_st)
        p.barrier()

    def layer1():
        E("vector", lambda e: e.memset(nrm[:], 0.0))
        E("vector", lambda e: e.memset(hstat[:], 0.0))
        E("vector", lambda e: e.memset(hmax[:], 0.0), inc=sem_v)
        p.barrier()
        wstats_tern()
        for half in range(2):
            with ExitStack() as phq:
                qxnf = phq.enter_context(nc.sbuf_tensor("qxnf", [128, 16 * 1024], BF16, allow_name_mangling=True))
                with ExitStack() as ph:
                    xnt = ph.enter_context(nc.sbuf_tensor("xnt1", [128, D], BF16, allow_name_mangling=True))
                    dumA = ph.enter_context(nc.sbuf_tensor("dumA1", [128, D], F32, allow_name_mangling=True))
                    qfb = ph.enter_context(nc.sbuf_tensor("qfb", [128, D], F32, allow_name_mangling=True))
                    dqb = ph.enter_context(nc.sbuf_tensor("dqb", [128, D], F32, allow_name_mangling=True))
                    qi8 = ph.enter_context(nc.sbuf_tensor("qi8", [128, D], I8, allow_name_mangling=True))
                    qxt = ph.enter_context(nc.sbuf_tensor("qxt", [128, D], BF16, allow_name_mangling=True))
                    mxc = ph.enter_context(nc.sbuf_tensor("mxc", [128, 1], F32, allow_name_mangling=True))
                    stk = ph.enter_context(nc.sbuf_tensor("stk", [128, 1], F32, allow_name_mangling=True))
                    for tl in range(8):
                        t = 8 * half + tl
                        ln_stats_apply(t, xnt[:], dumA[:], dumA[:])
                        W("vector", sem_s)
                        E("vector", lambda e: e.reduce_max(
                            mxc[:], xnt[:], AX.X, apply_absolute_value=True))
                        VF()
                        VF()
                        VF()
                        E("vector", lambda e, t=t: e.tensor_scalar(
                            sxi[:, t:t + 1], mxc[:], EPS_Q, 1.0 / 127.0,
                            ALU.max, ALU.mult))
                        VF()
                        VF()
                        VF()
                        E("vector", lambda e, t=t: e.reciprocal(
                            stk[:], sxi[:, t:t + 1]), inc=sem_v)
                        W("scalar", sem_v)
                        quant_chain(xnt[:], stk[:], dumA, qfb, dqb, qi8, qxt)
                        W("sync", sem_v)
                        for dc in range(16):
                            p.dma("sync",
                                  qxnf[:, 1024 * dc + 128 * tl:
                                       1024 * dc + 128 * (tl + 1)],
                                  qxt[:, 128 * dc:128 * (dc + 1)], sem_w,
                                  transpose=True)
                        p.barrier()
                with ExitStack() as ph:
                    ffn_gateup(half, qxnf, ph)
            with ExitStack() as ph:
                xnf = ph.enter_context(nc.sbuf_tensor("xnf1", [128, 16 * 1024], BF16, allow_name_mangling=True))
                xnt = ph.enter_context(nc.sbuf_tensor("xnt2", [128, D], BF16, allow_name_mangling=True))
                for tl in range(8):
                    t = 8 * half + tl
                    E("scalar", lambda e, t=t: e.activation(
                        xnt[:], x_sb[:, D * t:D * (t + 1)], AF.Identity,
                        scale=ln_rs[:, t:t + 1], bias=ln_mu[:, t:t + 1]),
                        inc=sem_s)
                    W("sync", sem_s)
                    xn_to_xnf(tl, xnt, xnf)
                    p.barrier()
                proj_half(half, xnf, ph)
        finalize_bound(SQD1)
        attention(layer=1)
        rms_finalize()
        ffn_down()

    tmark = {0: 0, 1: 0}
    tmark2 = {0: 0, 1: 0}
    evmark = {0: 0, 1: 0}

    layer0()
    layer1()

    with nc.sbuf_tensor("oout", [128, 2 * D], F32, allow_name_mangling=True) as oout:
        p.barrier()
        for t in range(NT):
            buf = oout[:, (t % 2) * D:(t % 2) * D + D]
            if t >= 2:
                p.wait("scalar", sem_st)
            E("scalar", lambda e, b=buf, t=t: e.copy(
                b, x_sb[:, D * t:D * (t + 1)]), inc=sem_s)
            W("sync", sem_s)
            p.dma("sync", out[128 * t:128 * (t + 1), :], buf, sem_st)
        p.wait("sync", sem_st)

    with nc.Block() as block:
        block.tensor(lambda e: [f(e) for f in p.ops["tensor"]])
        block.vector(lambda e: [f(e) for f in p.ops["vector"]])
        block.scalar(lambda e: [f(e) for f in p.ops["scalar"]])
        block.sync(lambda e: [f(e) for f in p.ops["sync"]])
        block.gpsimd(lambda e: [f(e) for f in p.ops["gpsimd"]])

    es.close()
    return nc


def _prep_inputs(inputs):
    f32 = np.float32
    g = lambda k: np.ascontiguousarray(np.asarray(inputs[k], dtype=f32))

    def tile(w, cc=512):
        R, C = w.shape
        return np.ascontiguousarray(
            w.reshape(R // 128, 128, C // cc, cc).transpose(0, 2, 1, 3))

    wq = g("L0_attn_W_qkv")
    perm = np.concatenate([
        np.concatenate([np.arange(384 * h, 384 * h + 128) for h in range(16)]),
        np.concatenate([np.arange(384 * h + 128, 384 * h + 256) for h in range(16)]),
        np.concatenate([np.arange(384 * h + 256, 384 * h + 384) for h in range(16)]),
    ])
    w_qkv = np.ascontiguousarray(wq[:, perm])
    b_qkv = np.ascontiguousarray(g("L0_attn_b_qkv")[perm])[None, :]

    s = 2048
    inv_freq = (1.0 / (10000.0 ** (np.arange(0, 32, 2, dtype=f32) / 32.0))).astype(f32)
    t = np.arange(s, dtype=f32)
    freqs = np.outer(t, inv_freq).astype(f32)
    cos16 = np.cos(freqs).astype(f32)
    sin16 = np.sin(freqs).astype(f32)
    cos_r = np.ascontiguousarray(np.tile(cos16, (1, 4)))
    sin_r = np.ascontiguousarray(np.tile(sin16, (1, 4)))

    tri = (np.arange(128)[None, :] >= np.arange(128)[:, None]).astype(BF)

    base = {
        "w_qkv": tile(w_qkv), "b_qkv": b_qkv,
        "w_dense": tile(g("L0_attn_W_dense")),
        "b_dense": g("L0_attn_b_dense")[None, :],
        "w_ffn0": tile(np.ascontiguousarray(g("L0_ffn_weight").T)),
        "s_ffn0": g("L0_ffn_scale_weight")[None, :],
        "b_ffn0": g("L0_ffn_bias")[None, :],
        "wk1": tile(g("L1_attn_wk")), "wq1": tile(g("L1_attn_wq")),
        "wv1": tile(g("L1_attn_wv")),
        "wkl": tile(g("L1_attn_wkl")), "wql": tile(g("L1_attn_wql")),
        "wvl": tile(g("L1_attn_wvl")),
        "bk1": g("L1_attn_bk")[None, :], "bq1": g("L1_attn_bq")[None, :],
        "bv1": g("L1_attn_bv")[None, :],
        "bkl": g("L1_attn_bkl")[None, :], "bql": g("L1_attn_bql")[None, :],
        "bvl": g("L1_attn_bvl")[None, :],
        "wo": tile(g("L1_attn_wo")), "bo": g("L1_attn_bo")[None, :],
        "w_gT": tile(np.ascontiguousarray(g("L1_ffn_w_gate").T), 2048),
        "w_uT": tile(np.ascontiguousarray(g("L1_ffn_w_up").T), 2048),
        "w_dT": tile(np.ascontiguousarray(g("L1_ffn_w_down").T), 2048),
        "w_ln": g("L1_ffn_ln_weight")[None, :],
        "identity": np.eye(128, dtype=BF),
        "trimask": tri,
        "cos_r": cos_r, "sin_r": sin_r,
    }
    x = np.asarray(inputs["L0_attn_hidden_states"], dtype=f32)
    maps = []
    for c in range(8):
        m = dict(base)
        m["x"] = np.ascontiguousarray(x[c % 4].reshape(TOK, D))
        maps.append(m)
    return maps


_NC_CACHE = {}
_LAST_DBG = None


def kernel(**inputs):
    maps = _prep_inputs(inputs)
    if "nc" not in _NC_CACHE:
        _NC_CACHE["nc"] = build_program()
    nc = _NC_CACHE["nc"]
    res = run_bass_kernel_spmd(nc, maps, core_ids=list(range(8)))
    kernel._LAST_RES = res
    global _LAST_DBG
    try:
        _LAST_DBG = np.stack([np.asarray(res.results[b]["dbg"]) for b in range(8)])
        np.save("/tmp/dev_dbg2.npy", np.asarray(res.results[0]["dbg2"], dtype=np.float32))
        np.save("/tmp/dev_dbg3.npy", np.asarray(res.results[0]["dbg3"], dtype=np.float32))
        np.save("/tmp/dev_dbg4.npy", np.asarray(res.results[0]["dbg4"], dtype=np.float32))
    except Exception:
        _LAST_DBG = None
    outs = [np.asarray(res.results[b]["out"], dtype=np.float32).reshape(1, TOK // D * 0 + 2048, D)
            for b in range(4)]
    return np.concatenate(outs, axis=0)


if __name__ == "__main__":
    nc = build_program()
    print("built ok")


# version 07
# speedup vs baseline: 1.1449x; 1.1223x over previous
import math
import numpy as np
import ml_dtypes
import concourse.bass as bass
import concourse.mybir as mybir
from concourse.bass_utils import run_bass_kernel_spmd
from contextlib import ExitStack

F32 = mybir.dt.float32
BF16 = mybir.dt.bfloat16
I8 = mybir.dt.int8
AF = mybir.ActivationFunctionType
ALU = mybir.AluOpType
AX = mybir.AxisListType

D = 2048
NT = 16
HT = 8
TOK = 128 * NT
H0, DH0 = 16, 128
H1, DH1 = 16, 64
DL = 1024
IF = 8192
EPS_LN = 1e-5
EPS_RMS = 1e-6
EPS_Q = 1e-5
SQD0 = 1.0 / math.sqrt(DH0)
SQD1 = 1.0 / math.sqrt(DH1)
BF = np.dtype(ml_dtypes.bfloat16)

DEPTH = 2
_DBG_SCRATCH = False


class Prog:
    ENGINES = ("tensor", "vector", "scalar", "sync", "gpsimd")

    def __init__(self, nc, es):
        self.nc = nc
        self.es = es
        self.ops = {e: [] for e in self.ENGINES}
        self.semvals = {}
        self.sems = {}

    def sem(self, name):
        s = self.es.enter_context(self.nc.semaphore(name))
        self.semvals[name] = 0
        self.sems[name] = s
        return s

    def emit(self, engine, fn, inc=None, inc_by=1):
        if inc is not None:
            self.semvals[inc.name] += inc_by

            def wrapped(eng, fn=fn, inc=inc, n=inc_by):
                fn(eng).then_inc(inc, n)
            self.ops[engine].append(wrapped)
        else:
            self.ops[engine].append(fn)

    def dma(self, engine, out, in_, sem, transpose=False, slow=False):
        self.emit(engine,
                  lambda e, o=out, i=in_, tr=transpose, sl=slow: e.dma_start(
                      out=o, in_=i, transpose=tr, allow_slow_non_contiguous=sl),
                  inc=sem, inc_by=16)

    def wait(self, engine, sem, val=None):
        v = self.semvals[sem.name] if val is None else val
        self.ops[engine].append(lambda e, s=sem, v=v: e.wait_ge(s, v))

    def val(self, sem):
        return self.semvals[sem.name]

    def barrier(self):
        vals = [(s, self.semvals[n]) for n, s in self.sems.items()
                if n != "sem_st"]
        for e in self.ENGINES:
            for s, v in vals:
                if v > 0:
                    self.ops[e].append(lambda eng, s=s, v=v: eng.wait_ge(s, v))


def build_program():
    nc = bass.Bass("TRN2", target_bir_lowering=False)
    es = ExitStack()
    p = Prog(nc, es)

    def din(name, shape, dtype=F32):
        return nc.dram_tensor(name, list(shape), dtype, kind="ExternalInput").ap()

    def dsc(name, shape, dtype=BF16):
        kind = "ExternalOutput" if _DBG_SCRATCH else "Internal"
        return nc.dram_tensor(name, list(shape), dtype, kind=kind).ap()

    x_in = din("x", (TOK, D))
    out = nc.dram_tensor("out", [TOK, D], F32, kind="ExternalOutput").ap()
    dbg = nc.dram_tensor("dbg", [128, 96], F32, kind="ExternalOutput").ap()
    dbg2 = nc.dram_tensor("dbg2", [128, 3 * D], BF16, kind="ExternalOutput").ap()
    dbg3 = nc.dram_tensor("dbg3", [128, 4 * D], BF16, kind="ExternalOutput").ap()
    dbg4 = nc.dram_tensor("dbg4", [128, 4 * 2064], F32, kind="ExternalOutput").ap()

    w_qkv = din("w_qkv", (16, 12, 128, 512))
    b_qkv = din("b_qkv", (1, 3 * D))
    w_dense = din("w_dense", (16, 4, 128, 512))
    b_dense = din("b_dense", (1, D))
    w_ffn0 = din("w_ffn0", (16, 4, 128, 512))
    s_ffn0 = din("s_ffn0", (1, D))
    b_ffn0 = din("b_ffn0", (1, D))
    wk1 = [din(f"w{m}1", (16, 4, 128, 512)) for m in "kqv"]
    wl1 = [din(f"w{m}l", (16, 2, 128, 512)) for m in "kqv"]
    b1r = [din(f"b{m}1", (1, D)) for m in "kqv"]
    bl1 = [din(f"b{m}l", (1, DL)) for m in "kqv"]
    wo = din("wo", (8, 4, 128, 512))
    bo = din("bo", (1, D))
    w_gT = din("w_gT", (16, 4, 128, 2048))
    w_uT = din("w_uT", (16, 4, 128, 2048))
    w_dT = din("w_dT", (64, 1, 128, 2048))
    w_ln = din("w_ln", (1, IF))

    identity = din("identity", (128, 128), BF16)
    trimask = din("trimask", (128, 128), BF16)
    cos_r = din("cos_r", (TOK, 64))
    sin_r = din("sin_r", (TOK, 64))

    q_sc = dsc("q_sc", (NT, 128, D))
    k_sc = dsc("k_sc", (NT, 128, D))
    v_sc = dsc("v_sc", (NT, 128, D))
    k1t = dsc("k1t", (NT, 128, DL))
    q1t = dsc("q1t", (NT, 128, DL))
    v1_sc = dsc("v1_sc", (NT, 128, DL))
    wg_t = dsc("wg_t", (16, 16, 128, 512))
    wu_t = dsc("wu_t", (16, 16, 128, 512))
    wd_t = dsc("wd_t", (64, 4, 128, 512))
    h_sc = dsc("h_sc", (NT, 16, 128, 512))
    misc_sc = dsc("misc_sc", (1, 4096), F32)

    def sba(name, shape, dtype=BF16):
        return es.enter_context(nc.sbuf_tensor(name, list(shape), dtype))

    x_sb = sba("x_sb", (128, NT * D))
    idn = sba("idn", (128, 128))
    trim = sba("trim", (128, 128))
    ones_f = sba("ones_f", (1, 128), F32)
    onecol = sba("onecol", (128, 1), F32)
    cvec = sba("cvec", (1, 3 * D), F32)
    ln_mu = sba("ln_mu", (128, NT), F32)
    ln_rs = sba("ln_rs", (128, NT), F32)
    nrm = sba("nrm", (128, 32), F32)
    negb = sba("negb", (128, 16), F32)
    hstat = sba("hstat", (128, NT), F32)
    hmax = sba("hmax", (128, NT), F32)
    hfac = sba("hfac", (128, NT), F32)
    beffc = sba("beffc", (128, 3 * 8), F32)
    colA = sba("colA", (128, 1), F32)
    colB = sba("colB", (128, 1), F32)
    colC = sba("colC", (128, 1), F32)
    colD = sba("colD", (128, 1), F32)
    r16 = sba("r16", (1, 64), F32)
    vpad = sba("vpad", (128, 1), F32)
    wmr = sba("wmr", (1, 16), F32)
    wmc = sba("wmc", (128, 12), F32)
    sxi = sba("sxi", (128, NT), F32)
    q2s = sba("q2s", (128, NT), F32)
    dss = sba("dss", (128, NT), F32)
    rr2 = sba("rr2", (128, NT), F32)
    rcq = sba("rcq", (128, NT), F32)
    gu = sba("gu", (128, 16), F32)

    sem_w = p.sem("sem_w")
    sem_wp = [p.sem("sem_w0"), p.sem("sem_w1")]
    sem_cp = [p.sem("sem_c0"), p.sem("sem_c1")]
    sem_gp = [p.sem("sem_g0"), p.sem("sem_g1")]
    sem_kp = [p.sem("sem_k0"), p.sem("sem_k1")]
    sem_up = [p.sem("sem_u0"), p.sem("sem_u1")]
    sem_c = p.sem("sem_c")
    sem_g = p.sem("sem_g")
    sem_t = p.sem("sem_t")
    sem_v = p.sem("sem_v")
    sem_s = p.sem("sem_s")
    sem_st = p.sem("sem_st")

    E = p.emit
    W = p.wait

    def VF():
        E("vector", lambda e: e.memset(vpad[:], 0.0))

    p.dma("sync", idn[:], identity, sem_w)
    p.dma("sync", trim[:], trimask, sem_w)

    with nc.sbuf_tensor("xld", [128, 2 * D], F32, allow_name_mangling=True) as xld:
        for t in range(NT):
            buf = xld[:, (t % 2) * D:(t % 2) * D + D]
            if t >= 2:
                p.wait("scalar", sem_v, p.val(sem_v) - 1)
            p.dma("scalar", buf, x_in[128 * t:128 * (t + 1), :], sem_cp[t % 2])
            p.wait("vector", sem_cp[t % 2])
            E("vector", lambda e, b=buf, t=t: e.tensor_copy(
                x_sb[:, D * t:D * (t + 1)], b), inc=sem_v)
    E("vector", lambda e: e.memset(ones_f[:], 1.0))
    E("vector", lambda e: e.memset(onecol[:], 1.0), inc=sem_v)
    p.barrier()

    def ln_stats_apply(t, xn_out, dumA, dumB):
        xt = x_sb[:, D * t:D * (t + 1)]
        E("scalar", lambda e, xt=xt, s=dumA: e.activation(
            s, xt, AF.Copy, accum_out=colA[:]))
        E("scalar", lambda e, xt=xt, s=dumB: e.activation(
            s, xt, AF.Square, accum_out=colB[:]), inc=sem_s)
        W("vector", sem_s)
        E("vector", lambda e: e.tensor_scalar(colA[:], colA[:], 1.0 / D, None, ALU.mult))
        E("vector", lambda e: e.tensor_scalar(colB[:], colB[:], 1.0 / D, None, ALU.mult))
        E("vector", lambda e: e.tensor_tensor(colC[:], colA[:], colA[:], ALU.mult))
        VF()
        E("vector", lambda e: e.tensor_tensor(colB[:], colB[:], colC[:], ALU.subtract))
        VF()
        E("vector", lambda e: e.tensor_scalar(colB[:], colB[:], EPS_LN, None, ALU.add))
        VF()
        E("vector", lambda e: e.reciprocal(colB[:], colB[:]), inc=sem_v)
        W("scalar", sem_v)
        E("scalar", lambda e, t=t: e.activation(ln_rs[:, t:t + 1], colB[:], AF.Sqrt), inc=sem_s)
        W("vector", sem_s)
        E("vector", lambda e, t=t: e.scalar_tensor_tensor(
            ln_mu[:, t:t + 1], colA[:], -1.0, ln_rs[:, t:t + 1],
            ALU.mult, ALU.mult), inc=sem_v)
        W("scalar", sem_v)
        E("scalar", lambda e, t=t, xo=xn_out, xt=xt: e.activation(
            xo, xt, AF.Identity, scale=ln_rs[:, t:t + 1], bias=ln_mu[:, t:t + 1]),
            inc=sem_s)

    def xn_to_xnf(t_local, xn_tile, xnf):
        for dc in range(16):
            p.dma("sync", xnf[:, 1024 * dc + 128 * t_local:1024 * dc + 128 * (t_local + 1)],
                  xn_tile[:, 128 * dc:128 * (dc + 1)], sem_w, transpose=True)

    def stream_mm(half, n_oc, n_dc, load_chunk, lhsT_fn, psums, post_fn,
                  bias_fn=None, cast_scale=None, wstage=None, wbf=None):
        idx = 0
        depth = len(wstage)
        marks = {i: 0 for i in range(depth)}
        for oc in range(n_oc):
            for dc in range(n_dc):
                bsel = idx % depth
                wf = wstage[bsel]
                wb = wbf[bsel]
                if idx >= depth:
                    p.wait("sync", sem_t, marks[bsel])
                load_chunk(oc, dc, wf, sem_wp[bsel % 2])
                p.wait("vector", sem_wp[bsel % 2])
                if cast_scale is None:
                    E("vector", lambda e, wf=wf, wb=wb: e.tensor_copy(wb[:], wf[:]),
                      inc=sem_v)
                else:
                    E("vector", lambda e, wf=wf, wb=wb, oc=oc: e.tensor_tensor(
                        wb[:], wf[:], cast_scale(oc), ALU.mult), inc=sem_v)
                p.wait("tensor", sem_v)
                for t in range(8):
                    st = (dc == 0)
                    sp = (dc == n_dc - 1) and (bias_fn is None)
                    E("tensor", lambda e, ps=psums[t], l=lhsT_fn(dc, t), wb=wb,
                        st=st, sp=sp: e.matmul(ps[:], l, wb[:], start=st, stop=sp),
                      inc=(sem_t if t == 7 else None))
                marks[bsel] = p.val(sem_t)
                idx += 1
            if bias_fn is not None:
                for t in range(8):
                    bias_fn(oc, t, psums[t])
            W("scalar", sem_t)
            W("vector", sem_t)
            for t in range(8):
                post_fn(oc, t, psums[t])
            if oc < n_oc - 1:
                p.wait("tensor", sem_s)
                p.wait("tensor", sem_v)
            else:
                p.barrier()

    def layer0():
        with ExitStack() as ph:
            xnf = ph.enter_context(nc.sbuf_tensor("xnf", [128, 16 * 1024], BF16, allow_name_mangling=True))
            xnt = ph.enter_context(nc.sbuf_tensor("xnt", [128, D], BF16, allow_name_mangling=True))
            dumA = ph.enter_context(nc.sbuf_tensor("dumA", [128, D], F32, allow_name_mangling=True))
            dumB = ph.enter_context(nc.sbuf_tensor("dumB", [128, D], F32, allow_name_mangling=True))
            cosr = ph.enter_context(nc.sbuf_tensor("cosr", [128, NT * 64], F32, allow_name_mangling=True))
            sinr = ph.enter_context(nc.sbuf_tensor("sinr", [128, NT * 64], F32, allow_name_mangling=True))
            wstgs = [ph.enter_context(nc.sbuf_tensor(f"wstg{i}", [128, 512], F32, allow_name_mangling=True)) for i in range(4)]
            wbfs = [ph.enter_context(nc.sbuf_tensor(f"wbf{i}", [128, 512], BF16, allow_name_mangling=True)) for i in range(4)]
            evac0 = ph.enter_context(nc.sbuf_tensor("evac0", [128, 512], BF16, allow_name_mangling=True))
            evac1 = ph.enter_context(nc.sbuf_tensor("evac1", [128, 512], BF16, allow_name_mangling=True))
            ropes = ph.enter_context(nc.sbuf_tensor("ropes", [128, 128], F32, allow_name_mangling=True))
            sqs = ph.enter_context(nc.sbuf_tensor("sqs", [128, 512], F32, allow_name_mangling=True))
            st16 = ph.enter_context(nc.sbuf_tensor("st16", [128, 16], F32, allow_name_mangling=True))
            s0bc = ph.enter_context(nc.sbuf_tensor("s0bc", [128, D], F32, allow_name_mangling=True))
            pool = ph.enter_context(nc.sbuf_tensor("pool", [1, 4 * D], F32, allow_name_mangling=True))
            p.dma("sync", cosr[:].rearrange("p (t c) -> p t c", c=64),
                  cos_r.rearrange("(t p) c -> p t c", p=128), sem_w)
            p.dma("sync", sinr[:].rearrange("p (t c) -> p t c", c=64),
                  sin_r.rearrange("(t p) c -> p t c", p=128), sem_w)
            E("vector", lambda e: e.memset(nrm[:], 0.0), inc=sem_v)
            with nc.psum_tensor("pb", [128, 512], F32, allow_name_mangling=True) as pb:
                p.dma("sync", pool[:, 0:D], s_ffn0, sem_w)
                p.barrier()
                for j in range(4):
                    E("tensor", lambda e, j=j, pb=pb: e.matmul(
                        pb[:], ones_f[:], pool[:, 512 * j:512 * (j + 1)],
                        start=True, stop=True), inc=sem_t)
                    W("scalar", sem_t)
                    E("scalar", lambda e, j=j, pb=pb: e.copy(
                        s0bc[:, 512 * j:512 * (j + 1)], pb[:]), inc=sem_s)
                    p.barrier()
            p.barrier()
            p.dma("sync", pool[:, 0:3 * D], b_qkv, sem_w)
            p.dma("sync", pool[:, 3 * D:4 * D], b_ffn0, sem_w)

            for half in range(2):
                for tl in range(8):
                    t = 8 * half + tl
                    ln_stats_apply(t, xnt[:], dumA[:], dumB[:])
                    W("sync", sem_s)
                    xn_to_xnf(tl, xnt, xnf)
                    p.barrier()

                if half == 0:
                    p.dma("sync", dbg3[:, 3 * D:4 * D], xnt[:], sem_st)
                    p.dma("sync", dbg4[:, 2 * 2064:2 * 2064 + 16], ln_rs[:, 0:16], sem_st)
                    p.dma("sync", dbg4[:, 2 * 2064 + 16:2 * 2064 + 32], ln_mu[:, 0:16], sem_st)
                    p.dma("sync", dbg3[:, 0:D], x_sb[:, 0:D], sem_st)
                    for dc in range(16):
                        p.dma("sync", dbg3[:, D + 128 * dc:D + 128 * (dc + 1)],
                              xnf[:, 1024 * dc:1024 * dc + 128], sem_st)
                    p.barrier()
                with ExitStack() as ph2:
                    p0 = ph2.enter_context(nc.psum_tensor("p0", [128, 512], F32, allow_name_mangling=True))
                    p1 = ph2.enter_context(nc.psum_tensor("p1", [128, 512], F32, allow_name_mangling=True))
                    p2 = ph2.enter_context(nc.psum_tensor("p2", [128, 512], F32, allow_name_mangling=True))
                    p3 = ph2.enter_context(nc.psum_tensor("p3", [128, 512], F32, allow_name_mangling=True))
                    p4 = ph2.enter_context(nc.psum_tensor("p4", [128, 512], F32, allow_name_mangling=True))
                    p5 = ph2.enter_context(nc.psum_tensor("p5", [128, 512], F32, allow_name_mangling=True))
                    p6 = ph2.enter_context(nc.psum_tensor("p6", [128, 512], F32, allow_name_mangling=True))
                    p7 = ph2.enter_context(nc.psum_tensor("p7", [128, 512], F32, allow_name_mangling=True))
                    psums = [p0, p1, p2, p3, p4, p5, p6, p7]

                    def load_qkv(oc, dc, wf, sem):
                        p.dma("sync", wf[:], w_qkv[dc, oc], sem)

                    def lhsT_xnf(dc, t):
                        return xnf[:, 1024 * dc + 128 * t:1024 * dc + 128 * (t + 1)]

                    def bias_qkv(oc, t, ps):
                        E("tensor", lambda e, ps=ps, oc=oc: e.matmul(
                            ps[:], ones_f[:], pool[:, 512 * oc:512 * (oc + 1)],
                            start=False, stop=True), inc=sem_t)

                    def post_qkv(oc, t, ps, half=half):
                        tg = 8 * half + t
                        ev = evac0 if t % 2 == 0 else evac1
                        if evmark[t % 2]:
                            p.wait("scalar", sem_gp[t % 2], evmark[t % 2])
                        E("scalar", lambda e, ev=ev, ps=ps: e.copy(ev[:], ps[:]),
                          inc=sem_s)
                        if oc == 0 and t == 0 and half == 0:
                            W("sync", sem_s)
                            p.dma("sync", dbg3[:, 2 * D:2 * D + 512], ev[:], sem_st)
                        if oc < 8:
                            W("vector", sem_s)
                            co = cosr[:, 64 * tg:64 * (tg + 1)].rearrange(
                                "p (j c) -> p j c", j=1)
                            si = sinr[:, 64 * tg:64 * (tg + 1)].rearrange(
                                "p (j c) -> p j c", j=1)
                            co4 = cosr[:, 64 * tg:64 * (tg + 1)].rearrange(
                                "p (c j) -> p c j", j=1)
                            Av = ev[:].rearrange("p (j c) -> p j c", j=4)[:, :, 0:16]
                            Bv = ev[:].rearrange("p (j c) -> p j c", j=4)[:, :, 16:32]
                            cv = cosr[:, 64 * tg:64 * (tg + 1)].rearrange(
                                "p (j c) -> p j c", j=4)
                            sv = sinr[:, 64 * tg:64 * (tg + 1)].rearrange(
                                "p (j c) -> p j c", j=4)
                            T1 = ropes[:, 0:64].rearrange("p (j c) -> p j c", j=4)
                            T2 = ropes[:, 64:128].rearrange("p (j c) -> p j c", j=4)
                            E("vector", lambda e, A=Av, s=sv, T=T1: e.tensor_tensor(T, A, s, ALU.mult))
                            E("vector", lambda e, B=Bv, s=sv, T=T2: e.tensor_tensor(T, B, s, ALU.mult))
                            E("vector", lambda e, A=Av, c=cv: e.tensor_tensor(A, A, c, ALU.mult))
                            E("vector", lambda e, B=Bv, c=cv: e.tensor_tensor(B, B, c, ALU.mult))
                            E("vector", lambda e, A=Av, T=T2: e.tensor_tensor(A, A, T, ALU.subtract))
                            E("vector", lambda e, B=Bv, T=T1: e.tensor_tensor(B, B, T, ALU.add))
                            VF()
                            E("vector", lambda e, ev=ev: e.tensor_tensor(
                                sqs[:], ev[:], ev[:], ALU.mult))
                            VF()
                            E("vector", lambda e: e.tensor_reduce(
                                st16[:, 0:4].rearrange("p (a o) -> p a o", o=1),
                                sqs[:].rearrange("p (a c) -> p a c", c=128),
                                AX.X, ALU.add))
                            VF()
                            hb = 4 * oc
                            E("vector", lambda e, hb=hb: e.tensor_tensor(
                                nrm[:, hb:hb + 4], nrm[:, hb:hb + 4],
                                st16[:, 0:4], ALU.max), inc=sem_v)
                            W("gpsimd", sem_v)
                            dst = (q_sc if oc < 4 else k_sc)
                            cb = 512 * (oc % 4)
                            p.dma("gpsimd", dst[tg, :, cb:cb + 512], ev[:],
                                  sem_gp[t % 2])
                        else:
                            W("gpsimd", sem_s)
                            cb = 512 * (oc - 8)
                            p.dma("gpsimd", v_sc[tg, :, cb:cb + 512], ev[:],
                                  sem_gp[t % 2])
                        evmark[t % 2] = p.val(sem_gp[t % 2])

                    evmark[0] = evmark[1] = 0
                    stream_mm(half, 12, 16, load_qkv, lhsT_xnf, psums, post_qkv,
                              bias_fn=bias_qkv, wstage=wstgs, wbf=wbfs)

                    def load_f0(oc, dc, wf, sem):
                        p.dma("sync", wf[:], w_ffn0[dc, oc], sem)

                    def bias_f0(oc, t, ps):
                        E("tensor", lambda e, ps=ps, oc=oc: e.matmul(
                            ps[:], ones_f[:], pool[:, 3 * D + 512 * oc:3 * D + 512 * (oc + 1)],
                            start=False, stop=True), inc=sem_t)

                    def post_f0(oc, t, ps, half=half):
                        tg = 8 * half + t
                        xs = x_sb[:, D * tg + 512 * oc:D * tg + 512 * (oc + 1)]
                        E("vector", lambda e, xs=xs, ps=ps: e.tensor_tensor(
                            xs, xs, ps[:], ALU.add), inc=sem_v)

                    def scale_f0(oc):
                        return s0bc[:, 512 * oc:512 * (oc + 1)]

                    stream_mm(half, 4, 16, load_f0, lhsT_xnf, psums, post_f0,
                              bias_fn=bias_f0, cast_scale=scale_f0,
                              wstage=wstgs, wbf=wbfs)

            finalize_bound(SQD0)
            p.dma("sync", dbg2[:, 0:D], q_sc[0].rearrange("p c -> p c"), sem_st)
            p.dma("sync", dbg2[:, D:2 * D], k_sc[0], sem_st)
            p.dma("sync", dbg2[:, 2 * D:3 * D], v_sc[0], sem_st)
            p.barrier()

        attention(layer=0)
        dense_proj(w_in=w_dense, b_row=b_dense, n_dc=16, layer=0)

    def finalize_bound(sqd):
        with ExitStack() as ph:
            ntp = ph.enter_context(nc.sbuf_tensor("ntp", [32, 128], F32, allow_name_mangling=True))
            nmx = ph.enter_context(nc.sbuf_tensor("nmx", [32, 1], F32, allow_name_mangling=True))
            pnb = ph.enter_context(nc.psum_tensor("pnb", [128, 512], F32, allow_name_mangling=True))
            p.barrier()
            W("sync", sem_v, p.val(sem_v))
            p.dma("sync", misc_sc[0:1, 0:4096].rearrange("a (p c) -> a p c", p=128),
                  nrm[:, 0:32].rearrange("p c -> p c"), sem_w)
            p.wait("sync", sem_w)
            p.dma("sync", ntp[:],
                  misc_sc[0:1, 0:4096].rearrange("a (p c) -> a c p", p=128)[0],
                  sem_w, slow=True)
            p.wait("vector", sem_w)
            E("vector", lambda e: e.reduce_max(
                nmx[:], ntp[:], AX.X), inc=sem_v)
            VF()
            W("sync", sem_v)
            p.dma("sync", misc_sc[0:1, 0:32].rearrange("a (p c) -> a p c", c=1),
                  nmx[:], sem_w)
            p.wait("sync", sem_w)
            p.dma("sync", r16[:, 0:32], misc_sc[0:1, 0:32], sem_w)
            p.wait("scalar", sem_w)
            E("scalar", lambda e: e.activation(r16[:, 0:32], r16[:, 0:32], AF.Sqrt),
              inc=sem_s)
            W("vector", sem_s)
            E("vector", lambda e, sqd=sqd: e.scalar_tensor_tensor(
                r16[:, 32:48], r16[:, 0:16], -sqd, r16[:, 16:32],
                ALU.mult, ALU.mult), inc=sem_v)
            p.wait("tensor", sem_v)
            E("tensor", lambda e: e.matmul(
                pnb[:, 0:16], ones_f[:], r16[:, 32:48], start=True, stop=True),
                inc=sem_t)
            W("scalar", sem_t)
            E("scalar", lambda e: e.copy(negb[:], pnb[:, 0:16]), inc=sem_s)
            p.barrier()
            p.dma("sync", dbg[:, 0:32], nrm[:, 0:32], sem_st)
            p.dma("sync", dbg[:, 32:48], negb[:, 0:16], sem_st)
            p.emit("sync", lambda e: e.dma_start(out=dbg[0:32, 48:49], in_=ntp[0:32, 0:1],
                   allow_slow_non_contiguous=True), inc=sem_st, inc_by=16)
            p.dma("sync", dbg[0:1, 49:49+32], r16[:, 0:32], sem_st)
            p.barrier()

    def attention(layer):
        if layer == 0:
            nh, dv, sqd = H0, DH0, SQD0
        else:
            nh, dv, sqd = H1, DH1, SQD1
        dva = dv + 1
        ACCW = nh * dva
        with ExitStack() as pha:
            acc = pha.enter_context(nc.sbuf_tensor("acc", [128, 8 * ACCW], F32, allow_name_mangling=True))
            rcp = pha.enter_context(nc.sbuf_tensor("rcp", [128, 16], F32, allow_name_mangling=True))
            evn = pha.enter_context(nc.sbuf_tensor("evn", [128, 128], BF16, allow_name_mangling=True))
            for qh in range(2):
                with ExitStack() as ph:
                    qbuf = ph.enter_context(nc.sbuf_tensor("qbuf", [128, nh * 1024], BF16, allow_name_mangling=True))
                    kslot0 = ph.enter_context(nc.sbuf_tensor("kslot0", [128, nh * 128], BF16, allow_name_mangling=True))
                    kslot1 = ph.enter_context(nc.sbuf_tensor("kslot1", [128, nh * 128], BF16, allow_name_mangling=True))
                    vaug0 = ph.enter_context(nc.sbuf_tensor("vaug0", [128, nh * dva], BF16, allow_name_mangling=True))
                    vaug1 = ph.enter_context(nc.sbuf_tensor("vaug1", [128, nh * dva], BF16, allow_name_mangling=True))
                    pt0 = ph.enter_context(nc.sbuf_tensor("pt0", [128, 1024], BF16, allow_name_mangling=True))
                    pt1 = ph.enter_context(nc.sbuf_tensor("pt1", [128, 1024], BF16, allow_name_mangling=True))
                    ps0 = ph.enter_context(nc.psum_tensor("ps0", [128, 512], F32, allow_name_mangling=True))
                    ps1 = ph.enter_context(nc.psum_tensor("ps1", [128, 512], F32, allow_name_mangling=True))
                    ps2 = ph.enter_context(nc.psum_tensor("ps2", [128, 512], F32, allow_name_mangling=True))
                    ps3 = ph.enter_context(nc.psum_tensor("ps3", [128, 512], F32, allow_name_mangling=True))
                    pv0 = ph.enter_context(nc.psum_tensor("pv0", [128, dva], F32, allow_name_mangling=True))
                    pv1 = ph.enter_context(nc.psum_tensor("pv1", [128, dva], F32, allow_name_mangling=True))
                    pscore = [(ps0, ps1), (ps2, ps3)]
                    pv = [pv0, pv1]
                    ptile = [pt0, pt1]
                    kslot = [kslot0, kslot1]
                    vaug = [vaug0, vaug1]
                    E("vector", lambda e, v=vaug0: e.memset(v[:], 1.0))
                    E("vector", lambda e, v=vaug1: e.memset(v[:], 1.0), inc=sem_v)
                    p.barrier()

                    if layer == 0:
                        for h in range(nh):
                            for qs in range(8):
                                p.dma("sync",
                                      qbuf[:, 1024 * h + 128 * qs:1024 * h + 128 * (qs + 1)],
                                      q_sc[8 * qh + qs, :, 128 * h:128 * (h + 1)],
                                      sem_w, transpose=True)
                    else:
                        for j in range(8):
                            for qs in range(8):
                                p.dma("sync",
                                      qbuf[:, 1024 * j + 128 * qs:1024 * j + 128 * (qs + 1)],
                                      q1t[8 * qh + qs, :, 128 * j:128 * (j + 1)],
                                      sem_w, transpose=True)
                    p.barrier()

                    nslots = 8 if qh == 0 else 16
                    kmark = {0: 0, 1: 0}
                    vmark = {0: 0, 1: 0}
                    for sl in range(nslots):
                        bsel = sl % 2
                        ks, va = kslot[bsel], vaug[bsel]
                        if sl >= 2:
                            p.wait("sync", sem_t, kmark[bsel])
                            p.wait("gpsimd", sem_t, kmark[bsel])
                        if layer == 0:
                            for h in range(nh):
                                p.dma("sync", ks[:, 128 * h:128 * (h + 1)],
                                      k_sc[sl, :, 128 * h:128 * (h + 1)],
                                      sem_kp[bsel], transpose=True)
                            p.dma("gpsimd",
                                  va[:].rearrange("p (h c) -> p h c", h=nh)[:, :, 0:dv],
                                  v_sc[sl].rearrange("p (h c) -> p h c", h=nh),
                                  sem_up[bsel])
                        else:
                            for j in range(8):
                                p.dma("sync", ks[:, 128 * j:128 * (j + 1)],
                                      k1t[sl, :, 128 * j:128 * (j + 1)],
                                      sem_kp[bsel], transpose=True)
                            p.dma("gpsimd",
                                  va[:].rearrange("p (h c) -> p h c", h=nh)[:, :, 0:dv],
                                  v1_sc[sl].rearrange("p (h c) -> p h c", h=nh),
                                  sem_up[bsel])
                        wv = p.val(sem_kp[bsel])
                        gv = p.val(sem_up[bsel])

                        qstart = 128 * sl if qh == 0 else max(0, 128 * sl - 1024)
                        w_ = 1024 - qstart
                        has_diag = (qh == 0) or (sl >= 8)

                        for h in range(nh):
                            psc = pscore[(sl * nh + h) % 2]
                            pte = ptile[(sl * nh + h) % 2]
                            if layer == 0:
                                lk = ks[:, 128 * h:128 * (h + 1)]
                            else:
                                j, b = h // 2, (h % 2) * 64
                                lk = ks[b:b + 64, 128 * j:128 * (j + 1)]
                            if h == 0:
                                p.wait("tensor", sem_kp[bsel], wv)
                            nmm = (w_ + 511) // 512
                            for m in range(nmm):
                                c0 = 512 * m
                                cw = min(512, w_ - c0)
                                if layer == 0:
                                    qs_ap = qbuf[:, 1024 * h + qstart + c0:
                                                 1024 * h + qstart + c0 + cw]
                                else:
                                    b = (h % 2) * 64
                                    qs_ap = qbuf[b:b + 64,
                                                 1024 * (h // 2) + qstart + c0:
                                                 1024 * (h // 2) + qstart + c0 + cw]
                                E("tensor", lambda e, ps=psc[m], lk=lk, q=qs_ap,
                                    cw=cw: e.matmul(ps[:, 0:cw], lk, q,
                                                    start=True, stop=True),
                                  inc=(sem_t if m == nmm - 1 else None))
                            W("scalar", sem_t)
                            for m in range(nmm):
                                c0 = 512 * m
                                cw = min(512, w_ - c0)
                                E("scalar", lambda e, pt=pte, ps=psc[m], c0=c0, cw=cw,
                                    h=h, sqd=sqd: e.activation(
                                        pt[:, c0:c0 + cw], ps[:, 0:cw], AF.Exp,
                                        scale=sqd, bias=negb[:, h:h + 1]),
                                  inc=(sem_s if m == nmm - 1 else None))
                            if has_diag:
                                W("vector", sem_s)
                                E("vector", lambda e, pt=pte: e.tensor_tensor(
                                    pt[:, 0:128], pt[:, 0:128], trim[:], ALU.mult),
                                  inc=sem_v)
                                W("tensor", sem_v)
                            else:
                                W("tensor", sem_s)
                            if h == 0:
                                p.wait("tensor", sem_up[bsel], gv)
                            nqc = w_ // 128
                            for c in range(nqc):
                                pvb = pv[c % 2]
                                lt = (qstart + 128 * c) // 128
                                accsl = acc[:, ACCW * lt + dva * h:
                                            ACCW * lt + dva * (h + 1)]
                                E("tensor", lambda e, pvb=pvb, pte=pte, c=c, va=va,
                                    h=h: e.matmul(pvb[:],
                                                  pte[:, 128 * c:128 * (c + 1)],
                                                  va[:, dva * h:dva * (h + 1)],
                                                  start=True, stop=True), inc=sem_t)
                                W("vector", sem_t)
                                if sl == 0:
                                    E("vector", lambda e, a=accsl, pvb=pvb:
                                        e.tensor_copy(a, pvb[:]), inc=sem_v)
                                else:
                                    E("vector", lambda e, a=accsl, pvb=pvb:
                                        e.tensor_tensor(a, a, pvb[:], ALU.add),
                                        inc=sem_v)
                                p.wait("tensor", sem_v)
                        kmark[bsel] = p.val(sem_t)
                        vmark[bsel] = p.val(sem_v)
                    p.barrier()
                    if layer == 0 and qh == 0:
                        p.dma("sync", dbg4[:, 0:ACCW], acc[:, 0:ACCW], sem_st)
                        p.dma("sync", dbg4[:, ACCW:2 * ACCW],
                              acc[:, 7 * ACCW:8 * ACCW], sem_st)
                        p.barrier()

                with ExitStack() as ph:
                    nfc = (nh * dv // 128)
                    attnf = ph.enter_context(
                        nc.sbuf_tensor("attnf", [128, nfc * 1024], BF16, allow_name_mangling=True))
                    for lt in range(8):
                        lview = acc[:, ACCW * lt:ACCW * (lt + 1)].rearrange(
                            "p (h c) -> p h c", c=dva)[:, :, dv:dv + 1]
                        if lt > 0:
                            W("vector", sem_s)
                        E("vector", lambda e, lv=lview, nh=nh: e.reciprocal(
                            rcp[:, 0:nh].rearrange("p (h c) -> p h c", c=1), lv),
                            inc=sem_v)
                        W("scalar", sem_v)
                        for h in range(nh):
                            if layer == 0:
                                a = acc[:, ACCW * lt + dva * h:ACCW * lt + dva * h + dv]
                                E("scalar", lambda e, a=a, h=h, dv=dv: e.activation(
                                    evn[:, 0:dv], a, AF.Identity, scale=rcp[:, h:h + 1]),
                                    inc=sem_s)
                                W("sync", sem_s)
                                dst = attnf[:, 1024 * h + 128 * lt:
                                            1024 * h + 128 * (lt + 1)]
                                p.dma("sync", dst, evn[:, 0:dv], sem_w, transpose=True)
                                p.wait("scalar", sem_w)
                            else:
                                a = acc[:, ACCW * lt + dva * h:ACCW * lt + dva * h + dv]
                                eo = (h % 2) * dv
                                E("scalar", lambda e, a=a, h=h, dv=dv, eo=eo:
                                    e.activation(evn[:, eo:eo + dv], a, AF.Identity,
                                                 scale=rcp[:, h:h + 1]),
                                    inc=sem_s)
                                if h % 2 == 1:
                                    W("sync", sem_s)
                                    dst = attnf[:, 1024 * (h // 2) + 128 * lt:
                                                1024 * (h // 2) + 128 * (lt + 1)]
                                    p.dma("sync", dst, evn[:, 0:128], sem_w,
                                          transpose=True)
                                    p.wait("scalar", sem_w)
                    p.barrier()
                    dense_half(layer, qh, attnf)
            p.barrier()

    def dense_half(layer, half, attnf):
        w_in = w_dense if layer == 0 else wo
        b_row = b_dense if layer == 0 else bo
        n_dc = 16 if layer == 0 else 8
        with ExitStack() as ph:
            wstgs = [ph.enter_context(nc.sbuf_tensor(f"wstgd{i}", [128, 512], F32, allow_name_mangling=True)) for i in range(4)]
            wbfs = [ph.enter_context(nc.sbuf_tensor(f"wbfd{i}", [128, 512], BF16, allow_name_mangling=True)) for i in range(4)]
            d0 = ph.enter_context(nc.psum_tensor("d0", [128, 512], F32, allow_name_mangling=True))
            d1 = ph.enter_context(nc.psum_tensor("d1", [128, 512], F32, allow_name_mangling=True))
            d2 = ph.enter_context(nc.psum_tensor("d2", [128, 512], F32, allow_name_mangling=True))
            d3 = ph.enter_context(nc.psum_tensor("d3", [128, 512], F32, allow_name_mangling=True))
            d4 = ph.enter_context(nc.psum_tensor("d4", [128, 512], F32, allow_name_mangling=True))
            d5 = ph.enter_context(nc.psum_tensor("d5", [128, 512], F32, allow_name_mangling=True))
            d6 = ph.enter_context(nc.psum_tensor("d6", [128, 512], F32, allow_name_mangling=True))
            d7 = ph.enter_context(nc.psum_tensor("d7", [128, 512], F32, allow_name_mangling=True))
            psums = [d0, d1, d2, d3, d4, d5, d6, d7]
            brow = ph.enter_context(nc.sbuf_tensor("brow", [1, D], F32, allow_name_mangling=True))
            p.dma("sync", brow[:, 0:D], b_row, sem_w)
            p.barrier()

            def load_w(oc, dc, wf, sem):
                p.dma("sync", wf[:], w_in[dc, oc], sem)

            def lhsT(dc, t):
                return attnf[:, 1024 * dc + 128 * t:1024 * dc + 128 * (t + 1)]

            def bias(oc, t, ps):
                E("tensor", lambda e, ps=ps, oc=oc: e.matmul(
                    ps[:], ones_f[:], brow[:, 512 * oc:512 * (oc + 1)],
                    start=False, stop=True), inc=sem_t)

            def post(oc, t, ps, half=half):
                tg = 8 * half + t
                xs = x_sb[:, D * tg + 512 * oc:D * tg + 512 * (oc + 1)]
                E("vector", lambda e, xs=xs, ps=ps: e.tensor_tensor(
                    xs, xs, ps[:], ALU.add), inc=sem_v)

            stream_mm(half, 4, n_dc, load_w, lhsT, psums, post, bias_fn=bias,
                      wstage=wstgs, wbf=wbfs)

    def dense_proj(w_in, b_row, n_dc, layer):
        pass

    def quant_chain(src_bf, scol, tq, qf, dq, qi8, qxt):
        E("scalar", lambda e: e.activation(tq[:], src_bf, AF.Copy, scale=scol),
          inc=sem_s)
        W("vector", sem_s)
        E("vector", lambda e: e.tensor_copy(qi8[:], tq[:]))
        VF()
        E("vector", lambda e: e.tensor_copy(qf[:], qi8[:]))
        VF()
        E("vector", lambda e: e.tensor_tensor(dq[:], tq[:], qf[:], ALU.subtract))
        VF()
        E("vector", lambda e: e.scalar_tensor_tensor(
            tq[:], dq[:], 0.5, qf[:], ALU.is_ge, ALU.add))
        VF()
        E("vector", lambda e: e.scalar_tensor_tensor(
            qf[:], dq[:], -0.5, tq[:], ALU.is_ge, ALU.add))
        VF()
        E("vector", lambda e: e.tensor_scalar(qxt[:], qf[:], -1.0, None, ALU.add),
          inc=sem_v)

    def wstats_tern():
        mats = [(w_gT, wg_t, D, IF), (w_uT, wu_t, D, IF), (w_dT, wd_t, IF, D)]
        with ExitStack() as ph:
            stg = [ph.enter_context(nc.sbuf_tensor(f"tstg{i}", [128, 2048], F32,
                   allow_name_mangling=True)) for i in range(2)]
            tob = [ph.enter_context(nc.sbuf_tensor(f"ttob{i}", [128, 2048], BF16,
                   allow_name_mangling=True)) for i in range(2)]
            tb1 = ph.enter_context(nc.sbuf_tensor("tb1", [128, 2048], F32, allow_name_mangling=True))
            tb2 = ph.enter_context(nc.sbuf_tensor("tb2", [128, 2048], F32, allow_name_mangling=True))
            dumw = ph.enter_context(nc.sbuf_tensor("dumw", [128, 2048], F32, allow_name_mangling=True))
            accw = ph.enter_context(nc.sbuf_tensor("accw", [128, 1], F32, allow_name_mangling=True))
            colw = ph.enter_context(nc.sbuf_tensor("colw", [128, 1], F32, allow_name_mangling=True))
            pws = ph.enter_context(nc.psum_tensor("pws", [1, 8], F32, allow_name_mangling=True))
            pwb = ph.enter_context(nc.psum_tensor("pwb", [128, 8], F32, allow_name_mangling=True))
            for m, (wsrc, wdst, R, C) in enumerate(mats):
                E("vector", lambda e: e.memset(accw[:], 0.0), inc=sem_v)
                W("scalar", sem_v)
                idx = 0
                smark = {0: 0, 1: 0}
                for r in range(R // 128):
                    for c in range(C // 2048):
                        b = idx % 2
                        if idx >= 2:
                            p.wait("sync", sem_s, smark[b])
                        p.dma("sync", stg[b][:], wsrc[r, c], sem_wp[b])
                        p.wait("scalar", sem_wp[b])
                        E("scalar", lambda e, b=b: e.activation(
                            dumw[:], stg[b][:], AF.Abs, accum_out=colw[:]),
                            inc=sem_s)
                        smark[b] = p.val(sem_s)
                        W("vector", sem_s)
                        E("vector", lambda e: e.tensor_tensor(
                            accw[:], accw[:], colw[:], ALU.add))
                        VF()
                        idx += 1
                E("vector", lambda e: e.memset(vpad[:], 0.0), inc=sem_v)
                p.wait("tensor", sem_v)
                E("tensor", lambda e, m=m: e.matmul(
                    pws[:, m:m + 1], accw[:], onecol[:], start=True, stop=True),
                    inc=sem_t)
                W("scalar", sem_t)
                E("scalar", lambda e, m=m: e.copy(wmr[:, m:m + 1], pws[:, m:m + 1]),
                  inc=sem_s)
                W("vector", sem_s)
                E("vector", lambda e, m=m, N=R * C: e.tensor_scalar(
                    wmr[:, 4 + m:5 + m], wmr[:, m:m + 1], 1.0 / N, EPS_Q,
                    ALU.mult, ALU.max), inc=sem_v)
                p.barrier()
            for m in range(3):
                E("tensor", lambda e, m=m: e.matmul(
                    pwb[:, m:m + 1], ones_f[:], wmr[:, 4 + m:5 + m],
                    start=True, stop=True), inc=sem_t)
            W("scalar", sem_t)
            for m in range(3):
                E("scalar", lambda e, m=m: e.copy(wmc[:, 6 + m:7 + m],
                                                  pwb[:, m:m + 1]), inc=sem_s)
            W("vector", sem_s)
            for m in range(3):
                E("vector", lambda e, m=m: e.tensor_scalar(
                    wmc[:, m:m + 1], wmc[:, 6 + m:7 + m], 0.5, None, ALU.mult))
                E("vector", lambda e, m=m: e.tensor_scalar(
                    wmc[:, 3 + m:4 + m], wmc[:, 6 + m:7 + m], -0.5, None, ALU.mult))
            E("vector", lambda e: e.memset(vpad[:], 0.0), inc=sem_v)
            p.barrier()
            for m, (wsrc, wdst, R, C) in enumerate(mats):
                idx = 0
                vmark = {0: 0, 1: 0}
                gmark = {0: 0, 1: 0}
                for r in range(R // 128):
                    for c in range(C // 2048):
                        b = idx % 2
                        if idx >= 2:
                            p.wait("sync", sem_v, vmark[b])
                            p.wait("vector", sem_gp[b], gmark[b])
                        p.dma("sync", stg[b][:], wsrc[r, c], sem_wp[b])
                        p.wait("vector", sem_wp[b])
                        E("vector", lambda e, b=b, m=m: e.tensor_scalar(
                            tb1[:], stg[b][:], wmc[:, m:m + 1], None, ALU.is_gt))
                        E("vector", lambda e, b=b, m=m: e.tensor_scalar(
                            tb2[:], stg[b][:], wmc[:, 3 + m:4 + m], None, ALU.is_ge),
                            inc=sem_v)
                        vmark[b] = p.val(sem_v)
                        VF()
                        E("vector", lambda e: e.tensor_tensor(
                            tb1[:], tb1[:], tb2[:], ALU.add))
                        VF()
                        E("vector", lambda e, b=b: e.tensor_scalar(
                            tob[b][:], tb1[:], -1.0, None, ALU.add), inc=sem_v)
                        W("gpsimd", sem_v)
                        for j in range(4):
                            p.dma("gpsimd", wdst[r, 4 * c + j],
                                  tob[b][:, 512 * j:512 * (j + 1)], sem_gp[b])
                        gmark[b] = p.val(sem_gp[b])
                        idx += 1
                p.barrier()

    def ffn_gateup(half, qxnf, ph):
        wld = [ph.enter_context(nc.sbuf_tensor(f"guw{i}", [128, 512], BF16,
               allow_name_mangling=True)) for i in range(4)]
        gbuf = ph.enter_context(nc.sbuf_tensor("gbuf", [128, 8 * 512], BF16, allow_name_mangling=True))
        ub = [ph.enter_context(nc.sbuf_tensor(f"ub{i}", [128, 512], BF16,
              allow_name_mangling=True)) for i in range(2)]
        hb = [ph.enter_context(nc.sbuf_tensor(f"hbb{i}", [128, 512], BF16,
              allow_name_mangling=True)) for i in range(2)]
        hwb = ph.enter_context(nc.sbuf_tensor("hwb", [128, 512], BF16, allow_name_mangling=True))
        wlnb = ph.enter_context(nc.sbuf_tensor("wlnb", [128, IF], BF16, allow_name_mangling=True))
        wrow = ph.enter_context(nc.sbuf_tensor("wrow", [1, IF], F32, allow_name_mangling=True))
        colh = [ph.enter_context(nc.sbuf_tensor(f"colh{i}", [128, 1], F32,
                allow_name_mangling=True)) for i in range(2)]
        colm = [ph.enter_context(nc.sbuf_tensor(f"colm{i}", [128, 1], F32,
                allow_name_mangling=True)) for i in range(2)]
        dumq = ph.enter_context(nc.sbuf_tensor("dumq", [128, 512], F32, allow_name_mangling=True))
        with nc.psum_tensor("pwl", [128, 512], F32, allow_name_mangling=True) as pwl:
            p.dma("sync", wrow[:], w_ln, sem_w)
            p.barrier()
            for j in range(IF // 512):
                E("tensor", lambda e, j=j, pwl=pwl: e.matmul(
                    pwl[:], ones_f[:], wrow[:, 512 * j:512 * (j + 1)],
                    start=True, stop=True), inc=sem_t)
                W("scalar", sem_t)
                E("scalar", lambda e, j=j, pwl=pwl: e.copy(
                    wlnb[:, 512 * j:512 * (j + 1)], pwl[:]), inc=sem_s)
                p.barrier()
        with ExitStack() as ph2:
            psums = [ph2.enter_context(nc.psum_tensor(f"gup{t}", [128, 512], F32,
                     allow_name_mangling=True)) for t in range(8)]
            W("scalar", sem_v)
            E("scalar", lambda e, half=half: e.activation(
                gu[:, 0:8], sxi[:, 8 * half:8 * half + 8], AF.Copy,
                scale=wmc[:, 6:7]))
            E("scalar", lambda e, half=half: e.activation(
                gu[:, 8:16], sxi[:, 8 * half:8 * half + 8], AF.Copy,
                scale=wmc[:, 7:8]), inc=sem_s)
            p.barrier()
            idx = 0
            lmark = {i: 0 for i in range(4)}
            hmark = {0: 0, 1: 0}
            ubmark = {0: 0, 1: 0}
            for oc in range(16):
                for phase in range(2):
                    wsrc = wg_t if phase == 0 else wu_t
                    for dc in range(16):
                        b = idx % 4
                        if idx >= 4:
                            p.wait("sync", sem_t, lmark[b])
                        p.dma("sync", wld[b][:], wsrc[dc, oc], sem_wp[idx % 2])
                        p.wait("tensor", sem_wp[idx % 2])
                        for t in range(8):
                            E("tensor", lambda e, ps=psums[t], b=b, dc=dc, t=t:
                                e.matmul(ps[:],
                                         qxnf[:, 1024 * dc + 128 * t:
                                              1024 * dc + 128 * (t + 1)],
                                         wld[b][:], start=(dc == 0),
                                         stop=(dc == 15)),
                              inc=(sem_t if t == 7 else None))
                        lmark[b] = p.val(sem_t)
                        idx += 1
                    W("scalar", sem_t)
                    if phase == 0:
                        for t in range(8):
                            E("scalar", lambda e, t=t, ps=psums[t]: e.activation(
                                gbuf[:, 512 * t:512 * (t + 1)], ps[:], AF.Silu,
                                scale=gu[:, t:t + 1]), inc=sem_s)
                        p.wait("tensor", sem_s)
                    else:
                        W("vector", sem_t)
                        for t in range(8):
                            tg = 8 * half + t
                            bp = t % 2
                            if hmark[bp]:
                                p.wait("vector", sem_gp[bp], hmark[bp])
                            if ubmark[bp]:
                                p.wait("scalar", sem_v, ubmark[bp])
                            E("scalar", lambda e, t=t, ps=psums[t], bp=bp:
                                e.activation(ub[bp][:], ps[:], AF.Identity,
                                             scale=gu[:, 8 + t:9 + t]), inc=sem_s)
                            W("vector", sem_s)
                            E("vector", lambda e, t=t, bp=bp: e.tensor_tensor(
                                hb[bp][:], gbuf[:, 512 * t:512 * (t + 1)],
                                ub[bp][:], ALU.mult), inc=sem_v)
                            ubmark[bp] = p.val(sem_v)
                            VF()
                            W("scalar", sem_v)
                            E("scalar", lambda e, bp=bp: e.activation(
                                dumq[:], hb[bp][:], AF.Square,
                                accum_out=colh[bp][:]), inc=sem_s)
                            E("vector", lambda e, bp=bp, oc=oc: e.tensor_tensor(
                                hwb[:], hb[bp][:],
                                wlnb[:, 512 * oc:512 * (oc + 1)], ALU.mult))
                            VF()
                            E("vector", lambda e, bp=bp: e.reduce_max(
                                colm[bp][:], hwb[:], AX.X,
                                apply_absolute_value=True))
                            VF()
                            VF()
                            VF()
                            E("vector", lambda e, tg=tg, bp=bp: e.tensor_tensor(
                                hmax[:, tg:tg + 1], hmax[:, tg:tg + 1],
                                colm[bp][:], ALU.max))
                            W("vector", sem_s)
                            E("vector", lambda e, tg=tg, bp=bp: e.tensor_tensor(
                                hstat[:, tg:tg + 1], hstat[:, tg:tg + 1],
                                colh[bp][:], ALU.add), inc=sem_v)
                            W("gpsimd", sem_v)
                            p.dma("gpsimd", h_sc[tg, oc], hb[bp][:], sem_gp[bp])
                            hmark[bp] = p.val(sem_gp[bp])
                        if oc < 15:
                            p.wait("tensor", sem_s)
                            p.wait("tensor", sem_v)
                        else:
                            p.barrier()

    def proj_half(half, xnf, ph):
        s1f = ph.enter_context(nc.sbuf_tensor("s1f", [128, 16 * 1024], BF16, allow_name_mangling=True))
        sqs2 = ph.enter_context(nc.sbuf_tensor("sqs2", [128, 512], F32, allow_name_mangling=True))
        st8 = ph.enter_context(nc.sbuf_tensor("st8", [128, 8], F32, allow_name_mangling=True))
        wstgs = [ph.enter_context(nc.sbuf_tensor(f"pw{i}", [128, 512], F32,
                 allow_name_mangling=True)) for i in range(4)]
        wbfs = [ph.enter_context(nc.sbuf_tensor(f"pwb{i}", [128, 512], BF16,
                allow_name_mangling=True)) for i in range(4)]
        evacs = [ph.enter_context(nc.sbuf_tensor(f"pev{i}", [128, 512], BF16,
                 allow_name_mangling=True)) for i in range(2)]
        with ExitStack() as ph2:
            psums = [ph2.enter_context(nc.psum_tensor(f"pj{t}", [128, 512], F32,
                     allow_name_mangling=True)) for t in range(8)]
            for m in range(3):
                p.dma("sync", cvec[0:1, 0:D], b1r[m], sem_w)
                p.dma("sync", cvec[0:1, D:D + DL], bl1[m], sem_w)
                p.barrier()
                s1mark = {0: 0, 1: 0}
                s2mark = {0: 0, 1: 0}
                s2vm = {0: 0, 1: 0}

                def load_s1(oc, dc, wf, sem, m=m):
                    p.dma("sync", wf[:], wk1[m][dc, oc], sem)

                def lhsT_xnf(dc, t):
                    return xnf[:, 1024 * dc + 128 * t:1024 * dc + 128 * (t + 1)]

                def bias_s1(oc, t, ps):
                    E("tensor", lambda e, ps=ps, oc=oc: e.matmul(
                        ps[:], ones_f[:], cvec[0:1, 512 * oc:512 * (oc + 1)],
                        start=False, stop=True), inc=sem_t)

                def post_s1(oc, t, ps, s1mark=s1mark):
                    b = t % 2
                    ev = evacs[b]
                    if s1mark[b]:
                        p.wait("scalar", sem_w, s1mark[b])
                    E("scalar", lambda e, ev=ev, ps=ps: e.copy(ev[:], ps[:]),
                      inc=sem_s)
                    W("sync", sem_s)
                    for j in range(4):
                        mc = 4 * oc + j
                        p.dma("sync",
                              s1f[:, 1024 * mc + 128 * t:1024 * mc + 128 * (t + 1)],
                              ev[:, 128 * j:128 * (j + 1)], sem_w, transpose=True)
                    s1mark[b] = p.val(sem_w)

                stream_mm(half, 4, 16, load_s1, lhsT_xnf, psums, post_s1,
                          bias_fn=bias_s1, wstage=wstgs, wbf=wbfs)

                def load_s2(oc, dc, wf, sem, m=m):
                    p.dma("sync", wf[:], wl1[m][dc, oc], sem)

                def lhsT_s1f(dc, t):
                    return s1f[:, 1024 * dc + 128 * t:1024 * dc + 128 * (t + 1)]

                def bias_s2(oc, t, ps):
                    E("tensor", lambda e, ps=ps, oc=oc: e.matmul(
                        ps[:], ones_f[:], cvec[0:1, D + 512 * oc:D + 512 * (oc + 1)],
                        start=False, stop=True), inc=sem_t)

                def post_s2(oc, t, ps, m=m, half=half, s2mark=s2mark,
                            s2vm=s2vm):
                    tg = 8 * half + t
                    b = t % 2
                    ev = evacs[b]
                    if s2mark[b]:
                        p.wait("scalar", sem_gp[b], s2mark[b])
                    if s2vm[b]:
                        p.wait("scalar", sem_v, s2vm[b])
                    E("scalar", lambda e, ev=ev, ps=ps: e.copy(ev[:], ps[:]),
                      inc=sem_s)
                    if m < 2:
                        W("vector", sem_s)
                        E("vector", lambda e, ev=ev: e.tensor_tensor(
                            sqs2[:], ev[:], ev[:], ALU.mult))
                        VF()
                        E("vector", lambda e: e.tensor_reduce(
                            st8[:].rearrange("p (a o) -> p a o", o=1),
                            sqs2[:].rearrange("p (a c) -> p a c", c=64),
                            AX.X, ALU.add))
                        VF()
                        VF()
                        VF()
                        hb0 = (16 if m == 0 else 0) + 8 * oc
                        E("vector", lambda e, hb0=hb0: e.tensor_tensor(
                            nrm[:, hb0:hb0 + 8], nrm[:, hb0:hb0 + 8], st8[:],
                            ALU.max), inc=sem_v)
                        s2vm[b] = p.val(sem_v)
                        W("gpsimd", sem_v)
                    else:
                        W("gpsimd", sem_s)
                    dst = [k1t, q1t, v1_sc][m]
                    p.dma("gpsimd", dst[tg, :, 512 * oc:512 * (oc + 1)], ev[:],
                          sem_gp[b])
                    s2mark[b] = p.val(sem_gp[b])

                stream_mm(half, 2, 16, load_s2, lhsT_s1f, psums, post_s2,
                          bias_fn=bias_s2, wstage=wstgs, wbf=wbfs)

    def ffn_down():
        with ExitStack() as ph:
            qhf = ph.enter_context(nc.sbuf_tensor("qhf", [128, 64 * 512], BF16, allow_name_mangling=True))
            wlnb = ph.enter_context(nc.sbuf_tensor("wlnb2", [128, IF], BF16, allow_name_mangling=True))
            hld = [ph.enter_context(nc.sbuf_tensor(f"dhl{i}", [128, 512], BF16,
                   allow_name_mangling=True)) for i in range(2)]
            hwq = ph.enter_context(nc.sbuf_tensor("hwq", [128, 512], BF16, allow_name_mangling=True))
            tqd = ph.enter_context(nc.sbuf_tensor("tqd", [128, 512], F32, allow_name_mangling=True))
            qfd = ph.enter_context(nc.sbuf_tensor("qfd", [128, 512], F32, allow_name_mangling=True))
            dqd = ph.enter_context(nc.sbuf_tensor("dqd", [128, 512], F32, allow_name_mangling=True))
            qi8d = ph.enter_context(nc.sbuf_tensor("qi8d", [128, 512], I8, allow_name_mangling=True))
            qbf = [ph.enter_context(nc.sbuf_tensor(f"qbf{i}", [128, 512], BF16,
                   allow_name_mangling=True)) for i in range(2)]
            wld = [ph.enter_context(nc.sbuf_tensor(f"dwl{i}", [128, 512], BF16,
                   allow_name_mangling=True)) for i in range(4)]
            dbf = [ph.enter_context(nc.sbuf_tensor(f"dbf{i}", [128, 512], F32,
                   allow_name_mangling=True)) for i in range(2)]
            with ExitStack() as ph2:
                psums = [ph2.enter_context(nc.psum_tensor(f"pd{t}", [128, 512],
                         F32, allow_name_mangling=True)) for t in range(4)]
                pwl = ph2.enter_context(nc.psum_tensor("pwl2", [128, 512], F32, allow_name_mangling=True))
                for hh in range(2):
                    p.dma("sync", cvec[0:1, 0:4096], w_ln[0:1, 4096 * hh:4096 * (hh + 1)], sem_w)
                    p.barrier()
                    for j in range(8):
                        E("tensor", lambda e, j=j: e.matmul(
                            pwl[:], ones_f[:], cvec[0:1, 512 * j:512 * (j + 1)],
                            start=True, stop=True), inc=sem_t)
                        W("scalar", sem_t)
                        E("scalar", lambda e, j=j, hh=hh: e.copy(
                            wlnb[:, 4096 * hh + 512 * j:4096 * hh + 512 * (j + 1)],
                            pwl[:]), inc=sem_s)
                        p.barrier()
                for grp in range(4):
                    qmark = {0: 0, 1: 0}
                    idx = 0
                    for tl in range(4):
                        tg = 4 * grp + tl
                        for oc in range(16):
                            b = idx % 2
                            if idx >= 2:
                                p.wait("scalar", sem_v, qmark[b])
                            p.dma("scalar", hld[b][:], h_sc[tg, oc], sem_cp[b])
                            p.wait("vector", sem_cp[b])
                            W("vector", sem_s)
                            E("vector", lambda e, b=b, oc=oc: e.tensor_tensor(
                                hwq[:], hld[b][:],
                                wlnb[:, 512 * oc:512 * (oc + 1)], ALU.mult),
                                inc=sem_v)
                            qmark[b] = p.val(sem_v)
                            W("scalar", sem_v)
                            qb = qbf[idx % 2]
                            if idx >= 2:
                                p.wait("vector", sem_w, tmark2[idx % 2])
                            quant_chain(hwq[:], q2s[:, tg:tg + 1], tqd, qfd,
                                        dqd, qi8d, qb)
                            W("sync", sem_v)
                            for j in range(4):
                                c = 4 * oc + j
                                p.dma("sync",
                                      qhf[:, 512 * c + 128 * tl:
                                          512 * c + 128 * (tl + 1)],
                                      qb[:, 128 * j:128 * (j + 1)], sem_w,
                                      transpose=True)
                            tmark2[idx % 2] = p.val(sem_w)
                            idx += 1
                    p.barrier()
                    idx = 0
                    lmark = {i: 0 for i in range(4)}
                    dmark = {0: 0, 1: 0}
                    for oc in range(4):
                        for dc in range(64):
                            b = idx % 4
                            if idx >= 4:
                                p.wait("sync", sem_t, lmark[b])
                            p.dma("sync", wld[b][:], wd_t[dc, oc], sem_wp[idx % 2])
                            p.wait("tensor", sem_wp[idx % 2])
                            for tl in range(4):
                                E("tensor", lambda e, ps=psums[tl], b=b, dc=dc,
                                    tl=tl: e.matmul(
                                        ps[:],
                                        qhf[:, 512 * dc + 128 * tl:
                                            512 * dc + 128 * (tl + 1)],
                                        wld[b][:], start=(dc == 0),
                                        stop=(dc == 63)),
                                  inc=(sem_t if tl == 3 else None))
                            lmark[b] = p.val(sem_t)
                            idx += 1
                        W("scalar", sem_t)
                        for tl in range(4):
                            tg = 4 * grp + tl
                            bp = tl % 2
                            if dmark[bp]:
                                p.wait("scalar", sem_v, dmark[bp])
                            E("scalar", lambda e, tl=tl, tg=tg, bp=bp,
                                ps=psums[tl]: e.activation(
                                    dbf[bp][:], ps[:], AF.Identity,
                                    scale=dss[:, tg:tg + 1]), inc=sem_s)
                            W("vector", sem_s)
                            E("vector", lambda e, tg=tg, oc=oc, bp=bp:
                                e.tensor_tensor(
                                    x_sb[:, D * tg + 512 * oc:
                                         D * tg + 512 * (oc + 1)],
                                    x_sb[:, D * tg + 512 * oc:
                                         D * tg + 512 * (oc + 1)],
                                    dbf[bp][:], ALU.add), inc=sem_v)
                            dmark[bp] = p.val(sem_v)
                        if oc < 3:
                            p.wait("tensor", sem_s)
                            p.wait("tensor", sem_v)
                        else:
                            p.barrier()

    def rms_finalize():
        E("vector", lambda e: e.tensor_scalar(
            rcq[:], hstat[:], 1.0 / IF, EPS_RMS, ALU.mult, ALU.add))
        VF()
        VF()
        VF()
        E("vector", lambda e: e.reciprocal(rr2[:], rcq[:]), inc=sem_v)
        W("scalar", sem_v)
        E("scalar", lambda e: e.activation(rr2[:], rr2[:], AF.Sqrt), inc=sem_s)
        W("vector", sem_s)
        E("vector", lambda e: e.tensor_tensor(hmax[:], hmax[:], rr2[:], ALU.mult))
        VF()
        VF()
        VF()
        E("vector", lambda e: e.tensor_scalar(
            hmax[:], hmax[:], EPS_Q, None, ALU.max))
        VF()
        VF()
        VF()
        E("vector", lambda e: e.reciprocal(rcq[:], hmax[:]))
        VF()
        VF()
        VF()
        E("vector", lambda e: e.tensor_tensor(q2s[:], rcq[:], rr2[:], ALU.mult))
        VF()
        VF()
        VF()
        E("vector", lambda e: e.tensor_scalar(
            q2s[:], q2s[:], 127.0, None, ALU.mult))
        E("vector", lambda e: e.tensor_scalar(
            dss[:], hmax[:], 1.0 / 127.0, None, ALU.mult), inc=sem_v)
        W("scalar", sem_v)
        E("scalar", lambda e: e.activation(
            dss[:], dss[:], AF.Copy, scale=wmc[:, 8:9]), inc=sem_s)
        p.barrier()
        p.dma("sync", dbg4[:, 0:16], sxi[:], sem_st)
        p.dma("sync", dbg4[:, 16:32], hstat[:], sem_st)
        p.dma("sync", dbg4[:, 32:48], hmax[:], sem_st)
        p.dma("sync", dbg4[:, 48:64], q2s[:], sem_st)
        p.dma("sync", dbg4[:, 64:80], dss[:], sem_st)
        p.dma("sync", dbg4[:, 80:96], rr2[:], sem_st)
        p.dma("sync", dbg4[:, 96:108], wmc[:], sem_st)
        p.dma("sync", dbg4[:, 108:124], gu[:], sem_st)
        p.barrier()

    def layer1():
        E("vector", lambda e: e.memset(nrm[:], 0.0))
        E("vector", lambda e: e.memset(hstat[:], 0.0))
        E("vector", lambda e: e.memset(hmax[:], 0.0), inc=sem_v)
        p.barrier()
        wstats_tern()
        for half in range(2):
            with ExitStack() as phq:
                qxnf = phq.enter_context(nc.sbuf_tensor("qxnf", [128, 16 * 1024], BF16, allow_name_mangling=True))
                with ExitStack() as ph:
                    xnt = ph.enter_context(nc.sbuf_tensor("xnt1", [128, D], BF16, allow_name_mangling=True))
                    dumA = ph.enter_context(nc.sbuf_tensor("dumA1", [128, D], F32, allow_name_mangling=True))
                    qfb = ph.enter_context(nc.sbuf_tensor("qfb", [128, D], F32, allow_name_mangling=True))
                    dqb = ph.enter_context(nc.sbuf_tensor("dqb", [128, D], F32, allow_name_mangling=True))
                    qi8 = ph.enter_context(nc.sbuf_tensor("qi8", [128, D], I8, allow_name_mangling=True))
                    qxt = ph.enter_context(nc.sbuf_tensor("qxt", [128, D], BF16, allow_name_mangling=True))
                    mxc = ph.enter_context(nc.sbuf_tensor("mxc", [128, 1], F32, allow_name_mangling=True))
                    stk = ph.enter_context(nc.sbuf_tensor("stk", [128, 1], F32, allow_name_mangling=True))
                    for tl in range(8):
                        t = 8 * half + tl
                        ln_stats_apply(t, xnt[:], dumA[:], dumA[:])
                        W("vector", sem_s)
                        E("vector", lambda e: e.reduce_max(
                            mxc[:], xnt[:], AX.X, apply_absolute_value=True))
                        VF()
                        VF()
                        VF()
                        E("vector", lambda e, t=t: e.tensor_scalar(
                            sxi[:, t:t + 1], mxc[:], EPS_Q, 1.0 / 127.0,
                            ALU.max, ALU.mult))
                        VF()
                        VF()
                        VF()
                        E("vector", lambda e, t=t: e.reciprocal(
                            stk[:], sxi[:, t:t + 1]), inc=sem_v)
                        W("scalar", sem_v)
                        quant_chain(xnt[:], stk[:], dumA, qfb, dqb, qi8, qxt)
                        W("sync", sem_v)
                        for dc in range(16):
                            p.dma("sync",
                                  qxnf[:, 1024 * dc + 128 * tl:
                                       1024 * dc + 128 * (tl + 1)],
                                  qxt[:, 128 * dc:128 * (dc + 1)], sem_w,
                                  transpose=True)
                        p.barrier()
                with ExitStack() as ph:
                    ffn_gateup(half, qxnf, ph)
            with ExitStack() as ph:
                xnf = ph.enter_context(nc.sbuf_tensor("xnf1", [128, 16 * 1024], BF16, allow_name_mangling=True))
                xnt = ph.enter_context(nc.sbuf_tensor("xnt2", [128, D], BF16, allow_name_mangling=True))
                for tl in range(8):
                    t = 8 * half + tl
                    E("scalar", lambda e, t=t: e.activation(
                        xnt[:], x_sb[:, D * t:D * (t + 1)], AF.Identity,
                        scale=ln_rs[:, t:t + 1], bias=ln_mu[:, t:t + 1]),
                        inc=sem_s)
                    W("sync", sem_s)
                    xn_to_xnf(tl, xnt, xnf)
                    p.barrier()
                proj_half(half, xnf, ph)
        finalize_bound(SQD1)
        attention(layer=1)
        rms_finalize()
        ffn_down()

    tmark = {0: 0, 1: 0}
    tmark2 = {0: 0, 1: 0}
    evmark = {0: 0, 1: 0}

    layer0()
    layer1()

    with nc.sbuf_tensor("oout", [128, 2 * D], F32, allow_name_mangling=True) as oout:
        p.barrier()
        for t in range(NT):
            buf = oout[:, (t % 2) * D:(t % 2) * D + D]
            if t >= 2:
                p.wait("scalar", sem_st)
            E("scalar", lambda e, b=buf, t=t: e.copy(
                b, x_sb[:, D * t:D * (t + 1)]), inc=sem_s)
            W("sync", sem_s)
            p.dma("sync", out[128 * t:128 * (t + 1), :], buf, sem_st)
        p.wait("sync", sem_st)

    with nc.Block() as block:
        block.tensor(lambda e: [f(e) for f in p.ops["tensor"]])
        block.vector(lambda e: [f(e) for f in p.ops["vector"]])
        block.scalar(lambda e: [f(e) for f in p.ops["scalar"]])
        block.sync(lambda e: [f(e) for f in p.ops["sync"]])
        block.gpsimd(lambda e: [f(e) for f in p.ops["gpsimd"]])

    es.close()
    return nc


def _prep_inputs(inputs):
    f32 = np.float32
    g = lambda k: np.ascontiguousarray(np.asarray(inputs[k], dtype=f32))

    def tile(w, cc=512):
        R, C = w.shape
        return np.ascontiguousarray(
            w.reshape(R // 128, 128, C // cc, cc).transpose(0, 2, 1, 3))

    wq = g("L0_attn_W_qkv")
    perm = np.concatenate([
        np.concatenate([np.arange(384 * h, 384 * h + 128) for h in range(16)]),
        np.concatenate([np.arange(384 * h + 128, 384 * h + 256) for h in range(16)]),
        np.concatenate([np.arange(384 * h + 256, 384 * h + 384) for h in range(16)]),
    ])
    w_qkv = np.ascontiguousarray(wq[:, perm])
    b_qkv = np.ascontiguousarray(g("L0_attn_b_qkv")[perm])[None, :]

    s = 2048
    inv_freq = (1.0 / (10000.0 ** (np.arange(0, 32, 2, dtype=f32) / 32.0))).astype(f32)
    t = np.arange(s, dtype=f32)
    freqs = np.outer(t, inv_freq).astype(f32)
    cos16 = np.cos(freqs).astype(f32)
    sin16 = np.sin(freqs).astype(f32)
    cos_r = np.ascontiguousarray(np.tile(cos16, (1, 4)))
    sin_r = np.ascontiguousarray(np.tile(sin16, (1, 4)))

    tri = (np.arange(128)[None, :] >= np.arange(128)[:, None]).astype(BF)

    base = {
        "w_qkv": tile(w_qkv), "b_qkv": b_qkv,
        "w_dense": tile(g("L0_attn_W_dense")),
        "b_dense": g("L0_attn_b_dense")[None, :],
        "w_ffn0": tile(np.ascontiguousarray(g("L0_ffn_weight").T)),
        "s_ffn0": g("L0_ffn_scale_weight")[None, :],
        "b_ffn0": g("L0_ffn_bias")[None, :],
        "wk1": tile(g("L1_attn_wk")), "wq1": tile(g("L1_attn_wq")),
        "wv1": tile(g("L1_attn_wv")),
        "wkl": tile(g("L1_attn_wkl")), "wql": tile(g("L1_attn_wql")),
        "wvl": tile(g("L1_attn_wvl")),
        "bk1": g("L1_attn_bk")[None, :], "bq1": g("L1_attn_bq")[None, :],
        "bv1": g("L1_attn_bv")[None, :],
        "bkl": g("L1_attn_bkl")[None, :], "bql": g("L1_attn_bql")[None, :],
        "bvl": g("L1_attn_bvl")[None, :],
        "wo": tile(g("L1_attn_wo")), "bo": g("L1_attn_bo")[None, :],
        "w_gT": tile(np.ascontiguousarray(g("L1_ffn_w_gate").T), 2048),
        "w_uT": tile(np.ascontiguousarray(g("L1_ffn_w_up").T), 2048),
        "w_dT": tile(np.ascontiguousarray(g("L1_ffn_w_down").T), 2048),
        "w_ln": g("L1_ffn_ln_weight")[None, :],
        "identity": np.eye(128, dtype=BF),
        "trimask": tri,
        "cos_r": cos_r, "sin_r": sin_r,
    }
    x = np.asarray(inputs["L0_attn_hidden_states"], dtype=f32)
    maps = []
    for c in range(8):
        m = dict(base)
        m["x"] = np.ascontiguousarray(x[c % 4].reshape(TOK, D))
        maps.append(m)
    return maps


_NC_CACHE = {}
_LAST_DBG = None


def kernel(**inputs):
    maps = _prep_inputs(inputs)
    if "nc" not in _NC_CACHE:
        _NC_CACHE["nc"] = build_program()
    nc = _NC_CACHE["nc"]
    res = run_bass_kernel_spmd(nc, maps, core_ids=list(range(8)))
    kernel._LAST_RES = res
    global _LAST_DBG
    try:
        _LAST_DBG = np.stack([np.asarray(res.results[b]["dbg"]) for b in range(8)])
        np.save("/tmp/dev_dbg2.npy", np.asarray(res.results[0]["dbg2"], dtype=np.float32))
        np.save("/tmp/dev_dbg3.npy", np.asarray(res.results[0]["dbg3"], dtype=np.float32))
        np.save("/tmp/dev_dbg4.npy", np.asarray(res.results[0]["dbg4"], dtype=np.float32))
    except Exception:
        _LAST_DBG = None
    outs = [np.asarray(res.results[b]["out"], dtype=np.float32).reshape(1, TOK // D * 0 + 2048, D)
            for b in range(4)]
    return np.concatenate(outs, axis=0)


if __name__ == "__main__":
    nc = build_program()
    print("built ok")


# version 08 (final)
# speedup vs baseline: 1.1791x; 1.0299x over previous
import math
import numpy as np
import ml_dtypes
import concourse.bass as bass
import concourse.mybir as mybir
from concourse.bass_utils import run_bass_kernel_spmd
from contextlib import ExitStack

F32 = mybir.dt.float32
BF16 = mybir.dt.bfloat16
I8 = mybir.dt.int8
AF = mybir.ActivationFunctionType
ALU = mybir.AluOpType
AX = mybir.AxisListType

D = 2048
NT = 16
HT = 8
TOK = 128 * NT
H0, DH0 = 16, 128
H1, DH1 = 16, 64
DL = 1024
IF = 8192
EPS_LN = 1e-5
EPS_RMS = 1e-6
EPS_Q = 1e-5
SQD0 = 1.0 / math.sqrt(DH0)
SQD1 = 1.0 / math.sqrt(DH1)
BF = np.dtype(ml_dtypes.bfloat16)

DEPTH = 2
_DBG_SCRATCH = False


class Prog:
    ENGINES = ("tensor", "vector", "scalar", "sync", "gpsimd")

    def __init__(self, nc, es):
        self.nc = nc
        self.es = es
        self.ops = {e: [] for e in self.ENGINES}
        self.semvals = {}
        self.sems = {}

    def sem(self, name):
        s = self.es.enter_context(self.nc.semaphore(name))
        self.semvals[name] = 0
        self.sems[name] = s
        return s

    def emit(self, engine, fn, inc=None, inc_by=1):
        if inc is not None:
            self.semvals[inc.name] += inc_by

            def wrapped(eng, fn=fn, inc=inc, n=inc_by):
                fn(eng).then_inc(inc, n)
            self.ops[engine].append(wrapped)
        else:
            self.ops[engine].append(fn)

    def dma(self, engine, out, in_, sem, transpose=False, slow=False):
        self.emit(engine,
                  lambda e, o=out, i=in_, tr=transpose, sl=slow: e.dma_start(
                      out=o, in_=i, transpose=tr, allow_slow_non_contiguous=sl),
                  inc=sem, inc_by=16)

    def wait(self, engine, sem, val=None):
        v = self.semvals[sem.name] if val is None else val
        self.ops[engine].append(lambda e, s=sem, v=v: e.wait_ge(s, v))

    def val(self, sem):
        return self.semvals[sem.name]

    def barrier(self):
        vals = [(s, self.semvals[n]) for n, s in self.sems.items()
                if n != "sem_st"]
        for e in self.ENGINES:
            for s, v in vals:
                if v > 0:
                    self.ops[e].append(lambda eng, s=s, v=v: eng.wait_ge(s, v))


def build_program():
    nc = bass.Bass("TRN2", target_bir_lowering=False)
    es = ExitStack()
    p = Prog(nc, es)

    def din(name, shape, dtype=F32):
        return nc.dram_tensor(name, list(shape), dtype, kind="ExternalInput").ap()

    def dsc(name, shape, dtype=BF16):
        kind = "ExternalOutput" if _DBG_SCRATCH else "Internal"
        return nc.dram_tensor(name, list(shape), dtype, kind=kind).ap()

    x_in = din("x", (TOK, D))
    out = nc.dram_tensor("out", [TOK, D], F32, kind="ExternalOutput").ap()
    dbg = nc.dram_tensor("dbg", [128, 96], F32, kind="ExternalOutput").ap()
    dbg2 = nc.dram_tensor("dbg2", [128, 3 * D], BF16, kind="ExternalOutput").ap()
    dbg3 = nc.dram_tensor("dbg3", [128, 4 * D], BF16, kind="ExternalOutput").ap()
    dbg4 = nc.dram_tensor("dbg4", [128, 4 * 2064], F32, kind="ExternalOutput").ap()

    w_qkv = din("w_qkv", (16, 12, 128, 512))
    b_qkv = din("b_qkv", (1, 3 * D))
    w_dense = din("w_dense", (16, 4, 128, 512))
    b_dense = din("b_dense", (1, D))
    w_ffn0 = din("w_ffn0", (16, 4, 128, 512))
    s_ffn0 = din("s_ffn0", (1, D))
    b_ffn0 = din("b_ffn0", (1, D))
    wk1 = [din(f"w{m}1", (16, 4, 128, 512)) for m in "kqv"]
    wl1 = [din(f"w{m}l", (16, 2, 128, 512)) for m in "kqv"]
    b1r = [din(f"b{m}1", (1, D)) for m in "kqv"]
    bl1 = [din(f"b{m}l", (1, DL)) for m in "kqv"]
    wo = din("wo", (8, 4, 128, 512))
    bo = din("bo", (1, D))
    w_gT = din("w_gT", (16, 4, 128, 2048))
    w_uT = din("w_uT", (16, 4, 128, 2048))
    w_dT = din("w_dT", (64, 1, 128, 2048))
    w_ln = din("w_ln", (1, IF))

    identity = din("identity", (128, 128), BF16)
    trimask = din("trimask", (128, 128), BF16)
    cos_r = din("cos_r", (TOK, 64))
    sin_r = din("sin_r", (TOK, 64))

    q_sc = dsc("q_sc", (NT, 128, D))
    k_sc = dsc("k_sc", (NT, 128, D))
    v_sc = dsc("v_sc", (NT, 128, D))
    k1t = dsc("k1t", (NT, 128, DL))
    q1t = dsc("q1t", (NT, 128, DL))
    v1_sc = dsc("v1_sc", (NT, 128, DL))
    wg_t = dsc("wg_t", (16, 16, 128, 512))
    wu_t = dsc("wu_t", (16, 16, 128, 512))
    wd_t = dsc("wd_t", (64, 4, 128, 512))
    h_sc = dsc("h_sc", (NT, 16, 128, 512))
    misc_sc = dsc("misc_sc", (1, 4096), F32)

    def sba(name, shape, dtype=BF16):
        return es.enter_context(nc.sbuf_tensor(name, list(shape), dtype))

    x_sb = sba("x_sb", (128, NT * D))
    idn = sba("idn", (128, 128))
    trim = sba("trim", (128, 128))
    ones_f = sba("ones_f", (1, 128), F32)
    onecol = sba("onecol", (128, 1), F32)
    cvec = sba("cvec", (1, 3 * D), F32)
    ln_mu = sba("ln_mu", (128, NT), F32)
    ln_rs = sba("ln_rs", (128, NT), F32)
    nrm = sba("nrm", (128, 32), F32)
    negb = sba("negb", (128, 16), F32)
    hstat = sba("hstat", (128, NT), F32)
    hmax = sba("hmax", (128, NT), F32)
    hfac = sba("hfac", (128, NT), F32)
    beffc = sba("beffc", (128, 3 * 8), F32)
    colA = sba("colA", (128, 1), F32)
    colB = sba("colB", (128, 1), F32)
    colC = sba("colC", (128, 1), F32)
    colD = sba("colD", (128, 1), F32)
    r16 = sba("r16", (1, 64), F32)
    vpad = sba("vpad", (128, 1), F32)
    wmr = sba("wmr", (1, 16), F32)
    wmc = sba("wmc", (128, 12), F32)
    sxi = sba("sxi", (128, NT), F32)
    q2s = sba("q2s", (128, NT), F32)
    dss = sba("dss", (128, NT), F32)
    rr2 = sba("rr2", (128, NT), F32)
    rcq = sba("rcq", (128, NT), F32)
    gu = sba("gu", (128, 16), F32)

    sem_w = p.sem("sem_w")
    sem_wp = [p.sem("sem_w0"), p.sem("sem_w1")]
    sem_cp = [p.sem("sem_c0"), p.sem("sem_c1")]
    sem_gp = [p.sem("sem_g0"), p.sem("sem_g1")]
    sem_kp = [p.sem("sem_k0"), p.sem("sem_k1")]
    sem_up = [p.sem("sem_u0"), p.sem("sem_u1")]
    sem_c = p.sem("sem_c")
    sem_g = p.sem("sem_g")
    sem_t = p.sem("sem_t")
    sem_v = p.sem("sem_v")
    sem_s = p.sem("sem_s")
    sem_st = p.sem("sem_st")

    E = p.emit
    W = p.wait

    def VF():
        E("vector", lambda e: e.memset(vpad[:], 0.0))

    p.dma("sync", idn[:], identity, sem_w)
    p.dma("sync", trim[:], trimask, sem_w)

    with nc.sbuf_tensor("xld", [128, 2 * D], F32, allow_name_mangling=True) as xld:
        for t in range(NT):
            buf = xld[:, (t % 2) * D:(t % 2) * D + D]
            if t >= 2:
                p.wait("scalar", sem_v, p.val(sem_v) - 1)
            p.dma("scalar", buf, x_in[128 * t:128 * (t + 1), :], sem_cp[t % 2])
            p.wait("vector", sem_cp[t % 2])
            E("vector", lambda e, b=buf, t=t: e.tensor_copy(
                x_sb[:, D * t:D * (t + 1)], b), inc=sem_v)
    E("vector", lambda e: e.memset(ones_f[:], 1.0))
    E("vector", lambda e: e.memset(onecol[:], 1.0), inc=sem_v)
    p.barrier()

    def ln_stats_apply(t, xn_out, dumA, dumB):
        xt = x_sb[:, D * t:D * (t + 1)]
        E("scalar", lambda e, xt=xt, s=dumA: e.activation(
            s, xt, AF.Copy, accum_out=colA[:]))
        E("scalar", lambda e, xt=xt, s=dumB: e.activation(
            s, xt, AF.Square, accum_out=colB[:]), inc=sem_s)
        W("vector", sem_s)
        E("vector", lambda e: e.tensor_scalar(colA[:], colA[:], 1.0 / D, None, ALU.mult))
        E("vector", lambda e: e.tensor_scalar(colB[:], colB[:], 1.0 / D, None, ALU.mult))
        E("vector", lambda e: e.tensor_tensor(colC[:], colA[:], colA[:], ALU.mult))
        VF()
        E("vector", lambda e: e.tensor_tensor(colB[:], colB[:], colC[:], ALU.subtract))
        VF()
        E("vector", lambda e: e.tensor_scalar(colB[:], colB[:], EPS_LN, None, ALU.add))
        VF()
        E("vector", lambda e: e.reciprocal(colB[:], colB[:]), inc=sem_v)
        W("scalar", sem_v)
        E("scalar", lambda e, t=t: e.activation(ln_rs[:, t:t + 1], colB[:], AF.Sqrt), inc=sem_s)
        W("vector", sem_s)
        E("vector", lambda e, t=t: e.scalar_tensor_tensor(
            ln_mu[:, t:t + 1], colA[:], -1.0, ln_rs[:, t:t + 1],
            ALU.mult, ALU.mult), inc=sem_v)
        W("scalar", sem_v)
        E("scalar", lambda e, t=t, xo=xn_out, xt=xt: e.activation(
            xo, xt, AF.Identity, scale=ln_rs[:, t:t + 1], bias=ln_mu[:, t:t + 1]),
            inc=sem_s)

    def xn_to_xnf(t_local, xn_tile, xnf):
        for dc in range(16):
            p.dma("sync", xnf[:, 1024 * dc + 128 * t_local:1024 * dc + 128 * (t_local + 1)],
                  xn_tile[:, 128 * dc:128 * (dc + 1)], sem_w, transpose=True)

    def stream_mm(half, n_oc, n_dc, load_chunk, lhsT_fn, psums, post_fn,
                  bias_fn=None, cast_scale=None, wstage=None, wbf=None):
        idx = 0
        depth = len(wstage)
        marks = {i: 0 for i in range(depth)}
        for oc in range(n_oc):
            for dc in range(n_dc):
                bsel = idx % depth
                wf = wstage[bsel]
                wb = wbf[bsel]
                if idx >= depth:
                    p.wait("sync", sem_t, marks[bsel])
                load_chunk(oc, dc, wf, sem_wp[bsel % 2])
                p.wait("vector", sem_wp[bsel % 2])
                if cast_scale is None:
                    E("vector", lambda e, wf=wf, wb=wb: e.tensor_copy(wb[:], wf[:]),
                      inc=sem_v)
                else:
                    E("vector", lambda e, wf=wf, wb=wb, oc=oc: e.tensor_tensor(
                        wb[:], wf[:], cast_scale(oc), ALU.mult), inc=sem_v)
                p.wait("tensor", sem_v)
                for t in range(8):
                    st = (dc == 0)
                    sp = (dc == n_dc - 1) and (bias_fn is None)
                    E("tensor", lambda e, ps=psums[t], l=lhsT_fn(dc, t), wb=wb,
                        st=st, sp=sp: e.matmul(ps[:], l, wb[:], start=st, stop=sp),
                      inc=(sem_t if t == 7 else None))
                marks[bsel] = p.val(sem_t)
                idx += 1
            if bias_fn is not None:
                for t in range(8):
                    bias_fn(oc, t, psums[t])
            W("scalar", sem_t)
            W("vector", sem_t)
            for t in range(8):
                post_fn(oc, t, psums[t])
            if oc < n_oc - 1:
                p.wait("tensor", sem_s)
                p.wait("tensor", sem_v)
            else:
                p.barrier()

    def layer0():
        with ExitStack() as ph:
            xnf = ph.enter_context(nc.sbuf_tensor("xnf", [128, 16 * 1024], BF16, allow_name_mangling=True))
            xnt = ph.enter_context(nc.sbuf_tensor("xnt", [128, D], BF16, allow_name_mangling=True))
            dumA = ph.enter_context(nc.sbuf_tensor("dumA", [128, D], F32, allow_name_mangling=True))
            dumB = ph.enter_context(nc.sbuf_tensor("dumB", [128, D], F32, allow_name_mangling=True))
            cosr = ph.enter_context(nc.sbuf_tensor("cosr", [128, NT * 64], F32, allow_name_mangling=True))
            sinr = ph.enter_context(nc.sbuf_tensor("sinr", [128, NT * 64], F32, allow_name_mangling=True))
            wstgs = [ph.enter_context(nc.sbuf_tensor(f"wstg{i}", [128, 512], F32, allow_name_mangling=True)) for i in range(4)]
            wbfs = [ph.enter_context(nc.sbuf_tensor(f"wbf{i}", [128, 512], BF16, allow_name_mangling=True)) for i in range(4)]
            evac0 = ph.enter_context(nc.sbuf_tensor("evac0", [128, 512], BF16, allow_name_mangling=True))
            evac1 = ph.enter_context(nc.sbuf_tensor("evac1", [128, 512], BF16, allow_name_mangling=True))
            ropes = ph.enter_context(nc.sbuf_tensor("ropes", [128, 128], F32, allow_name_mangling=True))
            sqs = ph.enter_context(nc.sbuf_tensor("sqs", [128, 512], F32, allow_name_mangling=True))
            st16 = ph.enter_context(nc.sbuf_tensor("st16", [128, 16], F32, allow_name_mangling=True))
            s0bc = ph.enter_context(nc.sbuf_tensor("s0bc", [128, D], F32, allow_name_mangling=True))
            pool = ph.enter_context(nc.sbuf_tensor("pool", [1, 4 * D], F32, allow_name_mangling=True))
            p.dma("sync", cosr[:].rearrange("p (t c) -> p t c", c=64),
                  cos_r.rearrange("(t p) c -> p t c", p=128), sem_w)
            p.dma("sync", sinr[:].rearrange("p (t c) -> p t c", c=64),
                  sin_r.rearrange("(t p) c -> p t c", p=128), sem_w)
            E("vector", lambda e: e.memset(nrm[:], 0.0), inc=sem_v)
            with nc.psum_tensor("pb", [128, 512], F32, allow_name_mangling=True) as pb:
                p.dma("sync", pool[:, 0:D], s_ffn0, sem_w)
                p.barrier()
                for j in range(4):
                    E("tensor", lambda e, j=j, pb=pb: e.matmul(
                        pb[:], ones_f[:], pool[:, 512 * j:512 * (j + 1)],
                        start=True, stop=True), inc=sem_t)
                    W("scalar", sem_t)
                    E("scalar", lambda e, j=j, pb=pb: e.copy(
                        s0bc[:, 512 * j:512 * (j + 1)], pb[:]), inc=sem_s)
                    p.barrier()
            p.barrier()
            p.dma("sync", pool[:, 0:3 * D], b_qkv, sem_w)
            p.dma("sync", pool[:, 3 * D:4 * D], b_ffn0, sem_w)

            for half in range(2):
                for tl in range(8):
                    t = 8 * half + tl
                    ln_stats_apply(t, xnt[:], dumA[:], dumB[:])
                    W("sync", sem_s)
                    xn_to_xnf(tl, xnt, xnf)
                    p.barrier()

                if half == 0:
                    p.dma("sync", dbg3[:, 3 * D:4 * D], xnt[:], sem_st)
                    p.dma("sync", dbg4[:, 2 * 2064:2 * 2064 + 16], ln_rs[:, 0:16], sem_st)
                    p.dma("sync", dbg4[:, 2 * 2064 + 16:2 * 2064 + 32], ln_mu[:, 0:16], sem_st)
                    p.dma("sync", dbg3[:, 0:D], x_sb[:, 0:D], sem_st)
                    for dc in range(16):
                        p.dma("sync", dbg3[:, D + 128 * dc:D + 128 * (dc + 1)],
                              xnf[:, 1024 * dc:1024 * dc + 128], sem_st)
                    p.barrier()
                with ExitStack() as ph2:
                    p0 = ph2.enter_context(nc.psum_tensor("p0", [128, 512], F32, allow_name_mangling=True))
                    p1 = ph2.enter_context(nc.psum_tensor("p1", [128, 512], F32, allow_name_mangling=True))
                    p2 = ph2.enter_context(nc.psum_tensor("p2", [128, 512], F32, allow_name_mangling=True))
                    p3 = ph2.enter_context(nc.psum_tensor("p3", [128, 512], F32, allow_name_mangling=True))
                    p4 = ph2.enter_context(nc.psum_tensor("p4", [128, 512], F32, allow_name_mangling=True))
                    p5 = ph2.enter_context(nc.psum_tensor("p5", [128, 512], F32, allow_name_mangling=True))
                    p6 = ph2.enter_context(nc.psum_tensor("p6", [128, 512], F32, allow_name_mangling=True))
                    p7 = ph2.enter_context(nc.psum_tensor("p7", [128, 512], F32, allow_name_mangling=True))
                    psums = [p0, p1, p2, p3, p4, p5, p6, p7]

                    def load_qkv(oc, dc, wf, sem):
                        p.dma("sync", wf[:], w_qkv[dc, oc], sem)

                    def lhsT_xnf(dc, t):
                        return xnf[:, 1024 * dc + 128 * t:1024 * dc + 128 * (t + 1)]

                    def bias_qkv(oc, t, ps):
                        E("tensor", lambda e, ps=ps, oc=oc: e.matmul(
                            ps[:], ones_f[:], pool[:, 512 * oc:512 * (oc + 1)],
                            start=False, stop=True), inc=sem_t)

                    def post_qkv(oc, t, ps, half=half):
                        tg = 8 * half + t
                        ev = evac0 if t % 2 == 0 else evac1
                        if evmark[t % 2]:
                            p.wait("scalar", sem_gp[t % 2], evmark[t % 2])
                        E("scalar", lambda e, ev=ev, ps=ps: e.copy(ev[:], ps[:]),
                          inc=sem_s)
                        if oc == 0 and t == 0 and half == 0:
                            W("sync", sem_s)
                            p.dma("sync", dbg3[:, 2 * D:2 * D + 512], ev[:], sem_st)
                        if oc < 8:
                            W("vector", sem_s)
                            co = cosr[:, 64 * tg:64 * (tg + 1)].rearrange(
                                "p (j c) -> p j c", j=1)
                            si = sinr[:, 64 * tg:64 * (tg + 1)].rearrange(
                                "p (j c) -> p j c", j=1)
                            co4 = cosr[:, 64 * tg:64 * (tg + 1)].rearrange(
                                "p (c j) -> p c j", j=1)
                            Av = ev[:].rearrange("p (j c) -> p j c", j=4)[:, :, 0:16]
                            Bv = ev[:].rearrange("p (j c) -> p j c", j=4)[:, :, 16:32]
                            cv = cosr[:, 64 * tg:64 * (tg + 1)].rearrange(
                                "p (j c) -> p j c", j=4)
                            sv = sinr[:, 64 * tg:64 * (tg + 1)].rearrange(
                                "p (j c) -> p j c", j=4)
                            T1 = ropes[:, 0:64].rearrange("p (j c) -> p j c", j=4)
                            T2 = ropes[:, 64:128].rearrange("p (j c) -> p j c", j=4)
                            E("vector", lambda e, A=Av, s=sv, T=T1: e.tensor_tensor(T, A, s, ALU.mult))
                            E("vector", lambda e, B=Bv, s=sv, T=T2: e.tensor_tensor(T, B, s, ALU.mult))
                            E("vector", lambda e, A=Av, c=cv: e.tensor_tensor(A, A, c, ALU.mult))
                            E("vector", lambda e, B=Bv, c=cv: e.tensor_tensor(B, B, c, ALU.mult))
                            E("vector", lambda e, A=Av, T=T2: e.tensor_tensor(A, A, T, ALU.subtract))
                            E("vector", lambda e, B=Bv, T=T1: e.tensor_tensor(B, B, T, ALU.add))
                            VF()
                            E("vector", lambda e, ev=ev: e.tensor_tensor(
                                sqs[:], ev[:], ev[:], ALU.mult))
                            VF()
                            E("vector", lambda e: e.tensor_reduce(
                                st16[:, 0:4].rearrange("p (a o) -> p a o", o=1),
                                sqs[:].rearrange("p (a c) -> p a c", c=128),
                                AX.X, ALU.add))
                            VF()
                            hb = 4 * oc
                            E("vector", lambda e, hb=hb: e.tensor_tensor(
                                nrm[:, hb:hb + 4], nrm[:, hb:hb + 4],
                                st16[:, 0:4], ALU.max), inc=sem_v)
                            W("gpsimd", sem_v)
                            dst = (q_sc if oc < 4 else k_sc)
                            cb = 512 * (oc % 4)
                            p.dma("gpsimd", dst[tg, :, cb:cb + 512], ev[:],
                                  sem_gp[t % 2])
                        else:
                            W("gpsimd", sem_s)
                            cb = 512 * (oc - 8)
                            p.dma("gpsimd", v_sc[tg, :, cb:cb + 512], ev[:],
                                  sem_gp[t % 2])
                        evmark[t % 2] = p.val(sem_gp[t % 2])

                    evmark[0] = evmark[1] = 0
                    stream_mm(half, 12, 16, load_qkv, lhsT_xnf, psums, post_qkv,
                              bias_fn=bias_qkv, wstage=wstgs, wbf=wbfs)

                    def load_f0(oc, dc, wf, sem):
                        p.dma("sync", wf[:], w_ffn0[dc, oc], sem)

                    def bias_f0(oc, t, ps):
                        E("tensor", lambda e, ps=ps, oc=oc: e.matmul(
                            ps[:], ones_f[:], pool[:, 3 * D + 512 * oc:3 * D + 512 * (oc + 1)],
                            start=False, stop=True), inc=sem_t)

                    def post_f0(oc, t, ps, half=half):
                        tg = 8 * half + t
                        xs = x_sb[:, D * tg + 512 * oc:D * tg + 512 * (oc + 1)]
                        E("vector", lambda e, xs=xs, ps=ps: e.tensor_tensor(
                            xs, xs, ps[:], ALU.add), inc=sem_v)

                    def scale_f0(oc):
                        return s0bc[:, 512 * oc:512 * (oc + 1)]

                    stream_mm(half, 4, 16, load_f0, lhsT_xnf, psums, post_f0,
                              bias_fn=bias_f0, cast_scale=scale_f0,
                              wstage=wstgs, wbf=wbfs)

            finalize_bound(SQD0)
            p.dma("sync", dbg2[:, 0:D], q_sc[0].rearrange("p c -> p c"), sem_st)
            p.dma("sync", dbg2[:, D:2 * D], k_sc[0], sem_st)
            p.dma("sync", dbg2[:, 2 * D:3 * D], v_sc[0], sem_st)
            p.barrier()

        attention(layer=0)
        dense_proj(w_in=w_dense, b_row=b_dense, n_dc=16, layer=0)

    def finalize_bound(sqd):
        with ExitStack() as ph:
            ntp = ph.enter_context(nc.sbuf_tensor("ntp", [32, 128], F32, allow_name_mangling=True))
            nmx = ph.enter_context(nc.sbuf_tensor("nmx", [32, 1], F32, allow_name_mangling=True))
            pnb = ph.enter_context(nc.psum_tensor("pnb", [128, 512], F32, allow_name_mangling=True))
            p.barrier()
            W("sync", sem_v, p.val(sem_v))
            p.dma("sync", misc_sc[0:1, 0:4096].rearrange("a (p c) -> a p c", p=128),
                  nrm[:, 0:32].rearrange("p c -> p c"), sem_w)
            p.wait("sync", sem_w)
            p.dma("sync", ntp[:],
                  misc_sc[0:1, 0:4096].rearrange("a (p c) -> a c p", p=128)[0],
                  sem_w, slow=True)
            p.wait("vector", sem_w)
            E("vector", lambda e: e.reduce_max(
                nmx[:], ntp[:], AX.X), inc=sem_v)
            VF()
            W("sync", sem_v)
            p.dma("sync", misc_sc[0:1, 0:32].rearrange("a (p c) -> a p c", c=1),
                  nmx[:], sem_w)
            p.wait("sync", sem_w)
            p.dma("sync", r16[:, 0:32], misc_sc[0:1, 0:32], sem_w)
            p.wait("scalar", sem_w)
            E("scalar", lambda e: e.activation(r16[:, 0:32], r16[:, 0:32], AF.Sqrt),
              inc=sem_s)
            W("vector", sem_s)
            E("vector", lambda e, sqd=sqd: e.scalar_tensor_tensor(
                r16[:, 32:48], r16[:, 0:16], -sqd, r16[:, 16:32],
                ALU.mult, ALU.mult), inc=sem_v)
            p.wait("tensor", sem_v)
            E("tensor", lambda e: e.matmul(
                pnb[:, 0:16], ones_f[:], r16[:, 32:48], start=True, stop=True),
                inc=sem_t)
            W("scalar", sem_t)
            E("scalar", lambda e: e.copy(negb[:], pnb[:, 0:16]), inc=sem_s)
            p.barrier()
            p.dma("sync", dbg[:, 0:32], nrm[:, 0:32], sem_st)
            p.dma("sync", dbg[:, 32:48], negb[:, 0:16], sem_st)
            p.emit("sync", lambda e: e.dma_start(out=dbg[0:32, 48:49], in_=ntp[0:32, 0:1],
                   allow_slow_non_contiguous=True), inc=sem_st, inc_by=16)
            p.dma("sync", dbg[0:1, 49:49+32], r16[:, 0:32], sem_st)
            p.barrier()

    def attention(layer):
        if layer == 0:
            nh, dv, sqd = H0, DH0, SQD0
        else:
            nh, dv, sqd = H1, DH1, SQD1
        dva = dv + 1
        ACCW = nh * dva
        with ExitStack() as pha:
            acc = pha.enter_context(nc.sbuf_tensor("acc", [128, 8 * ACCW], F32, allow_name_mangling=True))
            rcp = pha.enter_context(nc.sbuf_tensor("rcp", [128, 16], F32, allow_name_mangling=True))
            evn = pha.enter_context(nc.sbuf_tensor("evn", [128, 128], BF16, allow_name_mangling=True))
            for qh in range(2):
                with ExitStack() as ph:
                    qbuf = ph.enter_context(nc.sbuf_tensor("qbuf", [128, nh * 1024], BF16, allow_name_mangling=True))
                    kslot0 = ph.enter_context(nc.sbuf_tensor("kslot0", [128, nh * 128], BF16, allow_name_mangling=True))
                    kslot1 = ph.enter_context(nc.sbuf_tensor("kslot1", [128, nh * 128], BF16, allow_name_mangling=True))
                    vaug0 = ph.enter_context(nc.sbuf_tensor("vaug0", [128, nh * dva], BF16, allow_name_mangling=True))
                    vaug1 = ph.enter_context(nc.sbuf_tensor("vaug1", [128, nh * dva], BF16, allow_name_mangling=True))
                    pt0 = ph.enter_context(nc.sbuf_tensor("pt0", [128, 1024], BF16, allow_name_mangling=True))
                    pt1 = ph.enter_context(nc.sbuf_tensor("pt1", [128, 1024], BF16, allow_name_mangling=True))
                    ps0 = ph.enter_context(nc.psum_tensor("ps0", [128, 512], F32, allow_name_mangling=True))
                    ps1 = ph.enter_context(nc.psum_tensor("ps1", [128, 512], F32, allow_name_mangling=True))
                    ps2 = ph.enter_context(nc.psum_tensor("ps2", [128, 512], F32, allow_name_mangling=True))
                    ps3 = ph.enter_context(nc.psum_tensor("ps3", [128, 512], F32, allow_name_mangling=True))
                    pv0 = ph.enter_context(nc.psum_tensor("pv0", [128, dva], F32, allow_name_mangling=True))
                    pv1 = ph.enter_context(nc.psum_tensor("pv1", [128, dva], F32, allow_name_mangling=True))
                    pscore = [(ps0, ps1), (ps2, ps3)]
                    pv = [pv0, pv1]
                    ptile = [pt0, pt1]
                    kslot = [kslot0, kslot1]
                    vaug = [vaug0, vaug1]
                    E("vector", lambda e, v=vaug0: e.memset(v[:], 1.0))
                    E("vector", lambda e, v=vaug1: e.memset(v[:], 1.0), inc=sem_v)
                    p.barrier()

                    if layer == 0:
                        for h in range(nh):
                            for qs in range(8):
                                p.dma("sync",
                                      qbuf[:, 1024 * h + 128 * qs:1024 * h + 128 * (qs + 1)],
                                      q_sc[8 * qh + qs, :, 128 * h:128 * (h + 1)],
                                      sem_w, transpose=True)
                    else:
                        for j in range(8):
                            for qs in range(8):
                                p.dma("sync",
                                      qbuf[:, 1024 * j + 128 * qs:1024 * j + 128 * (qs + 1)],
                                      q1t[8 * qh + qs, :, 128 * j:128 * (j + 1)],
                                      sem_w, transpose=True)
                    p.barrier()

                    nslots = 8 if qh == 0 else 16
                    kmark = {0: 0, 1: 0}
                    vmark = {0: 0, 1: 0}
                    for sl in range(nslots):
                        bsel = sl % 2
                        ks, va = kslot[bsel], vaug[bsel]
                        if sl >= 2:
                            p.wait("sync", sem_t, kmark[bsel])
                            p.wait("gpsimd", sem_t, kmark[bsel])
                        if layer == 0:
                            for h in range(nh):
                                p.dma("sync", ks[:, 128 * h:128 * (h + 1)],
                                      k_sc[sl, :, 128 * h:128 * (h + 1)],
                                      sem_kp[bsel], transpose=True)
                            p.dma("gpsimd",
                                  va[:].rearrange("p (h c) -> p h c", h=nh)[:, :, 0:dv],
                                  v_sc[sl].rearrange("p (h c) -> p h c", h=nh),
                                  sem_up[bsel])
                        else:
                            for j in range(8):
                                p.dma("sync", ks[:, 128 * j:128 * (j + 1)],
                                      k1t[sl, :, 128 * j:128 * (j + 1)],
                                      sem_kp[bsel], transpose=True)
                            p.dma("gpsimd",
                                  va[:].rearrange("p (h c) -> p h c", h=nh)[:, :, 0:dv],
                                  v1_sc[sl].rearrange("p (h c) -> p h c", h=nh),
                                  sem_up[bsel])
                        wv = p.val(sem_kp[bsel])
                        gv = p.val(sem_up[bsel])

                        qstart = 128 * sl if qh == 0 else max(0, 128 * sl - 1024)
                        w_ = 1024 - qstart
                        has_diag = (qh == 0) or (sl >= 8)

                        for h in range(nh):
                            psc = pscore[(sl * nh + h) % 2]
                            pte = ptile[(sl * nh + h) % 2]
                            if layer == 0:
                                lk = ks[:, 128 * h:128 * (h + 1)]
                            else:
                                j, b = h // 2, (h % 2) * 64
                                lk = ks[b:b + 64, 128 * j:128 * (j + 1)]
                            if h == 0:
                                p.wait("tensor", sem_kp[bsel], wv)
                            nmm = (w_ + 511) // 512
                            for m in range(nmm):
                                c0 = 512 * m
                                cw = min(512, w_ - c0)
                                if layer == 0:
                                    qs_ap = qbuf[:, 1024 * h + qstart + c0:
                                                 1024 * h + qstart + c0 + cw]
                                else:
                                    b = (h % 2) * 64
                                    qs_ap = qbuf[b:b + 64,
                                                 1024 * (h // 2) + qstart + c0:
                                                 1024 * (h // 2) + qstart + c0 + cw]
                                E("tensor", lambda e, ps=psc[m], lk=lk, q=qs_ap,
                                    cw=cw: e.matmul(ps[:, 0:cw], lk, q,
                                                    start=True, stop=True),
                                  inc=(sem_t if m == nmm - 1 else None))
                            W("scalar", sem_t)
                            for m in range(nmm):
                                c0 = 512 * m
                                cw = min(512, w_ - c0)
                                E("scalar", lambda e, pt=pte, ps=psc[m], c0=c0, cw=cw,
                                    h=h, sqd=sqd: e.activation(
                                        pt[:, c0:c0 + cw], ps[:, 0:cw], AF.Exp,
                                        scale=sqd, bias=negb[:, h:h + 1]),
                                  inc=(sem_s if m == nmm - 1 else None))
                            if has_diag:
                                W("vector", sem_s)
                                E("vector", lambda e, pt=pte: e.tensor_tensor(
                                    pt[:, 0:128], pt[:, 0:128], trim[:], ALU.mult),
                                  inc=sem_v)
                                W("tensor", sem_v)
                            else:
                                W("tensor", sem_s)
                            if h == 0:
                                p.wait("tensor", sem_up[bsel], gv)
                            nqc = w_ // 128
                            for c in range(nqc):
                                pvb = pv[c % 2]
                                lt = (qstart + 128 * c) // 128
                                accsl = acc[:, ACCW * lt + dva * h:
                                            ACCW * lt + dva * (h + 1)]
                                E("tensor", lambda e, pvb=pvb, pte=pte, c=c, va=va,
                                    h=h: e.matmul(pvb[:],
                                                  pte[:, 128 * c:128 * (c + 1)],
                                                  va[:, dva * h:dva * (h + 1)],
                                                  start=True, stop=True), inc=sem_t)
                                W("vector", sem_t)
                                if sl == 0:
                                    E("vector", lambda e, a=accsl, pvb=pvb:
                                        e.tensor_copy(a, pvb[:]), inc=sem_v)
                                else:
                                    E("vector", lambda e, a=accsl, pvb=pvb:
                                        e.tensor_tensor(a, a, pvb[:], ALU.add),
                                        inc=sem_v)
                                p.wait("tensor", sem_v)
                        kmark[bsel] = p.val(sem_t)
                        vmark[bsel] = p.val(sem_v)
                    p.barrier()
                    if layer == 0 and qh == 0:
                        p.dma("sync", dbg4[:, 0:ACCW], acc[:, 0:ACCW], sem_st)
                        p.dma("sync", dbg4[:, ACCW:2 * ACCW],
                              acc[:, 7 * ACCW:8 * ACCW], sem_st)
                        p.barrier()

                with ExitStack() as ph:
                    nfc = (nh * dv // 128)
                    attnf = ph.enter_context(
                        nc.sbuf_tensor("attnf", [128, nfc * 1024], BF16, allow_name_mangling=True))
                    for lt in range(8):
                        lview = acc[:, ACCW * lt:ACCW * (lt + 1)].rearrange(
                            "p (h c) -> p h c", c=dva)[:, :, dv:dv + 1]
                        if lt > 0:
                            W("vector", sem_s)
                        E("vector", lambda e, lv=lview, nh=nh: e.reciprocal(
                            rcp[:, 0:nh].rearrange("p (h c) -> p h c", c=1), lv),
                            inc=sem_v)
                        W("scalar", sem_v)
                        for h in range(nh):
                            if layer == 0:
                                a = acc[:, ACCW * lt + dva * h:ACCW * lt + dva * h + dv]
                                E("scalar", lambda e, a=a, h=h, dv=dv: e.activation(
                                    evn[:, 0:dv], a, AF.Identity, scale=rcp[:, h:h + 1]),
                                    inc=sem_s)
                                W("sync", sem_s)
                                dst = attnf[:, 1024 * h + 128 * lt:
                                            1024 * h + 128 * (lt + 1)]
                                p.dma("sync", dst, evn[:, 0:dv], sem_w, transpose=True)
                                p.wait("scalar", sem_w)
                            else:
                                a = acc[:, ACCW * lt + dva * h:ACCW * lt + dva * h + dv]
                                eo = (h % 2) * dv
                                E("scalar", lambda e, a=a, h=h, dv=dv, eo=eo:
                                    e.activation(evn[:, eo:eo + dv], a, AF.Identity,
                                                 scale=rcp[:, h:h + 1]),
                                    inc=sem_s)
                                if h % 2 == 1:
                                    W("sync", sem_s)
                                    dst = attnf[:, 1024 * (h // 2) + 128 * lt:
                                                1024 * (h // 2) + 128 * (lt + 1)]
                                    p.dma("sync", dst, evn[:, 0:128], sem_w,
                                          transpose=True)
                                    p.wait("scalar", sem_w)
                    p.barrier()
                    dense_half(layer, qh, attnf)
            p.barrier()

    def dense_half(layer, half, attnf):
        w_in = w_dense if layer == 0 else wo
        b_row = b_dense if layer == 0 else bo
        n_dc = 16 if layer == 0 else 8
        with ExitStack() as ph:
            wstgs = [ph.enter_context(nc.sbuf_tensor(f"wstgd{i}", [128, 512], F32, allow_name_mangling=True)) for i in range(4)]
            wbfs = [ph.enter_context(nc.sbuf_tensor(f"wbfd{i}", [128, 512], BF16, allow_name_mangling=True)) for i in range(4)]
            d0 = ph.enter_context(nc.psum_tensor("d0", [128, 512], F32, allow_name_mangling=True))
            d1 = ph.enter_context(nc.psum_tensor("d1", [128, 512], F32, allow_name_mangling=True))
            d2 = ph.enter_context(nc.psum_tensor("d2", [128, 512], F32, allow_name_mangling=True))
            d3 = ph.enter_context(nc.psum_tensor("d3", [128, 512], F32, allow_name_mangling=True))
            d4 = ph.enter_context(nc.psum_tensor("d4", [128, 512], F32, allow_name_mangling=True))
            d5 = ph.enter_context(nc.psum_tensor("d5", [128, 512], F32, allow_name_mangling=True))
            d6 = ph.enter_context(nc.psum_tensor("d6", [128, 512], F32, allow_name_mangling=True))
            d7 = ph.enter_context(nc.psum_tensor("d7", [128, 512], F32, allow_name_mangling=True))
            psums = [d0, d1, d2, d3, d4, d5, d6, d7]
            brow = ph.enter_context(nc.sbuf_tensor("brow", [1, D], F32, allow_name_mangling=True))
            p.dma("sync", brow[:, 0:D], b_row, sem_w)
            p.barrier()

            def load_w(oc, dc, wf, sem):
                p.dma("sync", wf[:], w_in[dc, oc], sem)

            def lhsT(dc, t):
                return attnf[:, 1024 * dc + 128 * t:1024 * dc + 128 * (t + 1)]

            def bias(oc, t, ps):
                E("tensor", lambda e, ps=ps, oc=oc: e.matmul(
                    ps[:], ones_f[:], brow[:, 512 * oc:512 * (oc + 1)],
                    start=False, stop=True), inc=sem_t)

            def post(oc, t, ps, half=half):
                tg = 8 * half + t
                xs = x_sb[:, D * tg + 512 * oc:D * tg + 512 * (oc + 1)]
                E("vector", lambda e, xs=xs, ps=ps: e.tensor_tensor(
                    xs, xs, ps[:], ALU.add), inc=sem_v)

            stream_mm(half, 4, n_dc, load_w, lhsT, psums, post, bias_fn=bias,
                      wstage=wstgs, wbf=wbfs)

    def dense_proj(w_in, b_row, n_dc, layer):
        pass

    def quant_chain(src_bf, scol, tq, qf, dq, qi8, qxt):
        E("scalar", lambda e: e.activation(tq[:], src_bf, AF.Copy, scale=scol),
          inc=sem_s)
        W("vector", sem_s)
        E("vector", lambda e: e.tensor_copy(qi8[:], tq[:]))
        VF()
        E("vector", lambda e: e.tensor_copy(qf[:], qi8[:]))
        VF()
        E("vector", lambda e: e.tensor_tensor(dq[:], tq[:], qf[:], ALU.subtract))
        VF()
        E("vector", lambda e: e.scalar_tensor_tensor(
            tq[:], dq[:], 0.5, qf[:], ALU.is_ge, ALU.add))
        VF()
        E("vector", lambda e: e.scalar_tensor_tensor(
            qf[:], dq[:], -0.5, tq[:], ALU.is_ge, ALU.add))
        VF()
        E("vector", lambda e: e.tensor_scalar(qxt[:], qf[:], -1.0, None, ALU.add),
          inc=sem_v)

    def wstats_tern():
        mats = [(w_gT, wg_t, D, IF), (w_uT, wu_t, D, IF), (w_dT, wd_t, IF, D)]
        with ExitStack() as ph:
            stg = [ph.enter_context(nc.sbuf_tensor(f"tstg{i}", [128, 2048], F32,
                   allow_name_mangling=True)) for i in range(4)]
            tob = [ph.enter_context(nc.sbuf_tensor(f"ttob{i}", [128, 2048], BF16,
                   allow_name_mangling=True)) for i in range(4)]
            tb1 = ph.enter_context(nc.sbuf_tensor("tb1", [128, 2048], F32, allow_name_mangling=True))
            tb2 = ph.enter_context(nc.sbuf_tensor("tb2", [128, 2048], F32, allow_name_mangling=True))
            dumw = ph.enter_context(nc.sbuf_tensor("dumw", [128, 2048], F32, allow_name_mangling=True))
            accw = ph.enter_context(nc.sbuf_tensor("accw", [128, 1], F32, allow_name_mangling=True))
            colw = ph.enter_context(nc.sbuf_tensor("colw", [128, 1], F32, allow_name_mangling=True))
            pws = ph.enter_context(nc.psum_tensor("pws", [1, 8], F32, allow_name_mangling=True))
            pwb = ph.enter_context(nc.psum_tensor("pwb", [128, 8], F32, allow_name_mangling=True))
            for m, (wsrc, wdst, R, C) in enumerate(mats):
                E("vector", lambda e: e.memset(accw[:], 0.0), inc=sem_v)
                W("scalar", sem_v)
                idx = 0
                smark = {i: 0 for i in range(4)}
                for r in range(R // 128):
                    for c in range(C // 2048):
                        b = idx % 4
                        if idx >= 4:
                            p.wait("sync", sem_s, smark[b])
                        p.dma("sync", stg[b][:], wsrc[r, c], sem_wp[idx % 2])
                        p.wait("scalar", sem_wp[idx % 2])
                        E("scalar", lambda e, b=b: e.activation(
                            dumw[:], stg[b][:], AF.Abs, accum_out=colw[:]),
                            inc=sem_s)
                        smark[b] = p.val(sem_s)
                        W("vector", sem_s)
                        E("vector", lambda e: e.tensor_tensor(
                            accw[:], accw[:], colw[:], ALU.add))
                        VF()
                        idx += 1
                E("vector", lambda e: e.memset(vpad[:], 0.0), inc=sem_v)
                p.wait("tensor", sem_v)
                E("tensor", lambda e, m=m: e.matmul(
                    pws[:, m:m + 1], accw[:], onecol[:], start=True, stop=True),
                    inc=sem_t)
                W("scalar", sem_t)
                E("scalar", lambda e, m=m: e.copy(wmr[:, m:m + 1], pws[:, m:m + 1]),
                  inc=sem_s)
                W("vector", sem_s)
                E("vector", lambda e, m=m, N=R * C: e.tensor_scalar(
                    wmr[:, 4 + m:5 + m], wmr[:, m:m + 1], 1.0 / N, EPS_Q,
                    ALU.mult, ALU.max), inc=sem_v)
                p.barrier()
            for m in range(3):
                E("tensor", lambda e, m=m: e.matmul(
                    pwb[:, m:m + 1], ones_f[:], wmr[:, 4 + m:5 + m],
                    start=True, stop=True), inc=sem_t)
            W("scalar", sem_t)
            for m in range(3):
                E("scalar", lambda e, m=m: e.copy(wmc[:, 6 + m:7 + m],
                                                  pwb[:, m:m + 1]), inc=sem_s)
            W("vector", sem_s)
            for m in range(3):
                E("vector", lambda e, m=m: e.tensor_scalar(
                    wmc[:, m:m + 1], wmc[:, 6 + m:7 + m], 0.5, None, ALU.mult))
                E("vector", lambda e, m=m: e.tensor_scalar(
                    wmc[:, 3 + m:4 + m], wmc[:, 6 + m:7 + m], -0.5, None, ALU.mult))
            E("vector", lambda e: e.memset(vpad[:], 0.0), inc=sem_v)
            p.barrier()
            for m, (wsrc, wdst, R, C) in enumerate(mats):
                idx = 0
                vmark = {i: 0 for i in range(4)}
                gmark = {i: 0 for i in range(4)}
                for r in range(R // 128):
                    for c in range(C // 2048):
                        b = idx % 4
                        if idx >= 4:
                            p.wait("sync", sem_v, vmark[b])
                            p.wait("vector", sem_gp[b % 2], gmark[b])
                        p.dma("sync", stg[b][:], wsrc[r, c], sem_wp[idx % 2])
                        p.wait("vector", sem_wp[idx % 2])
                        E("vector", lambda e, b=b, m=m: e.tensor_scalar(
                            tb1[:], stg[b][:], wmc[:, m:m + 1], None, ALU.is_gt))
                        E("vector", lambda e, b=b, m=m: e.tensor_scalar(
                            tb2[:], stg[b][:], wmc[:, 3 + m:4 + m], None, ALU.is_ge),
                            inc=sem_v)
                        vmark[b] = p.val(sem_v)
                        VF()
                        E("vector", lambda e: e.tensor_tensor(
                            tb1[:], tb1[:], tb2[:], ALU.add))
                        VF()
                        E("vector", lambda e, b=b: e.tensor_scalar(
                            tob[b][:], tb1[:], -1.0, None, ALU.add), inc=sem_v)
                        W("gpsimd", sem_v)
                        for j in range(4):
                            p.dma("gpsimd", wdst[r, 4 * c + j],
                                  tob[b][:, 512 * j:512 * (j + 1)], sem_gp[b % 2])
                        gmark[b] = p.val(sem_gp[b % 2])
                        idx += 1
                p.barrier()

    def ffn_gateup(half, qxnf, ph):
        wld = [ph.enter_context(nc.sbuf_tensor(f"guw{i}", [128, 512], BF16,
               allow_name_mangling=True)) for i in range(4)]
        gbuf = ph.enter_context(nc.sbuf_tensor("gbuf", [128, 8 * 512], BF16, allow_name_mangling=True))
        ub = [ph.enter_context(nc.sbuf_tensor(f"ub{i}", [128, 512], BF16,
              allow_name_mangling=True)) for i in range(2)]
        hb = [ph.enter_context(nc.sbuf_tensor(f"hbb{i}", [128, 512], BF16,
              allow_name_mangling=True)) for i in range(2)]
        hwb = ph.enter_context(nc.sbuf_tensor("hwb", [128, 512], BF16, allow_name_mangling=True))
        wlnb = ph.enter_context(nc.sbuf_tensor("wlnb", [128, IF], BF16, allow_name_mangling=True))
        wrow = ph.enter_context(nc.sbuf_tensor("wrow", [1, IF], F32, allow_name_mangling=True))
        colh = [ph.enter_context(nc.sbuf_tensor(f"colh{i}", [128, 1], F32,
                allow_name_mangling=True)) for i in range(2)]
        colm = [ph.enter_context(nc.sbuf_tensor(f"colm{i}", [128, 1], F32,
                allow_name_mangling=True)) for i in range(2)]
        dumq = ph.enter_context(nc.sbuf_tensor("dumq", [128, 512], F32, allow_name_mangling=True))
        with nc.psum_tensor("pwl", [128, 512], F32, allow_name_mangling=True) as pwl:
            p.dma("sync", wrow[:], w_ln, sem_w)
            p.barrier()
            for j in range(IF // 512):
                E("tensor", lambda e, j=j, pwl=pwl: e.matmul(
                    pwl[:], ones_f[:], wrow[:, 512 * j:512 * (j + 1)],
                    start=True, stop=True), inc=sem_t)
                W("scalar", sem_t)
                E("scalar", lambda e, j=j, pwl=pwl: e.copy(
                    wlnb[:, 512 * j:512 * (j + 1)], pwl[:]), inc=sem_s)
                p.barrier()
        with ExitStack() as ph2:
            psums = [ph2.enter_context(nc.psum_tensor(f"gup{t}", [128, 512], F32,
                     allow_name_mangling=True)) for t in range(8)]
            W("scalar", sem_v)
            E("scalar", lambda e, half=half: e.activation(
                gu[:, 0:8], sxi[:, 8 * half:8 * half + 8], AF.Copy,
                scale=wmc[:, 6:7]))
            E("scalar", lambda e, half=half: e.activation(
                gu[:, 8:16], sxi[:, 8 * half:8 * half + 8], AF.Copy,
                scale=wmc[:, 7:8]), inc=sem_s)
            p.barrier()
            idx = 0
            lmark = {i: 0 for i in range(4)}
            hmark = {0: 0, 1: 0}
            ubmark = {0: 0, 1: 0}
            for oc in range(16):
                for phase in range(2):
                    wsrc = wg_t if phase == 0 else wu_t
                    for dc in range(16):
                        b = idx % 4
                        if idx >= 4:
                            p.wait("sync", sem_t, lmark[b])
                        p.dma("sync", wld[b][:], wsrc[dc, oc], sem_wp[idx % 2])
                        p.wait("tensor", sem_wp[idx % 2])
                        for t in range(8):
                            E("tensor", lambda e, ps=psums[t], b=b, dc=dc, t=t:
                                e.matmul(ps[:],
                                         qxnf[:, 1024 * dc + 128 * t:
                                              1024 * dc + 128 * (t + 1)],
                                         wld[b][:], start=(dc == 0),
                                         stop=(dc == 15)),
                              inc=(sem_t if t == 7 else None))
                        lmark[b] = p.val(sem_t)
                        idx += 1
                    W("scalar", sem_t)
                    if phase == 0:
                        for t in range(8):
                            E("scalar", lambda e, t=t, ps=psums[t]: e.activation(
                                gbuf[:, 512 * t:512 * (t + 1)], ps[:], AF.Silu,
                                scale=gu[:, t:t + 1]), inc=sem_s)
                        p.wait("tensor", sem_s)
                    else:
                        W("vector", sem_t)
                        for t in range(8):
                            tg = 8 * half + t
                            bp = t % 2
                            if hmark[bp]:
                                p.wait("vector", sem_gp[bp], hmark[bp])
                            if ubmark[bp]:
                                p.wait("scalar", sem_v, ubmark[bp])
                            E("scalar", lambda e, t=t, ps=psums[t], bp=bp:
                                e.activation(ub[bp][:], ps[:], AF.Identity,
                                             scale=gu[:, 8 + t:9 + t]), inc=sem_s)
                            W("vector", sem_s)
                            E("vector", lambda e, t=t, bp=bp: e.tensor_tensor(
                                hb[bp][:], gbuf[:, 512 * t:512 * (t + 1)],
                                ub[bp][:], ALU.mult), inc=sem_v)
                            ubmark[bp] = p.val(sem_v)
                            VF()
                            W("scalar", sem_v)
                            E("scalar", lambda e, bp=bp: e.activation(
                                dumq[:], hb[bp][:], AF.Square,
                                accum_out=colh[bp][:]), inc=sem_s)
                            E("vector", lambda e, bp=bp, oc=oc: e.tensor_tensor(
                                hwb[:], hb[bp][:],
                                wlnb[:, 512 * oc:512 * (oc + 1)], ALU.mult))
                            VF()
                            E("vector", lambda e, bp=bp: e.reduce_max(
                                colm[bp][:], hwb[:], AX.X,
                                apply_absolute_value=True))
                            VF()
                            VF()
                            VF()
                            E("vector", lambda e, tg=tg, bp=bp: e.tensor_tensor(
                                hmax[:, tg:tg + 1], hmax[:, tg:tg + 1],
                                colm[bp][:], ALU.max))
                            W("vector", sem_s)
                            E("vector", lambda e, tg=tg, bp=bp: e.tensor_tensor(
                                hstat[:, tg:tg + 1], hstat[:, tg:tg + 1],
                                colh[bp][:], ALU.add), inc=sem_v)
                            W("gpsimd", sem_v)
                            p.dma("gpsimd", h_sc[tg, oc], hb[bp][:], sem_gp[bp])
                            hmark[bp] = p.val(sem_gp[bp])
                        if oc < 15:
                            p.wait("tensor", sem_s)
                            p.wait("tensor", sem_v)
                        else:
                            p.barrier()

    def proj_half(half, xnf, ph):
        s1f = ph.enter_context(nc.sbuf_tensor("s1f", [128, 16 * 1024], BF16, allow_name_mangling=True))
        sqs2 = ph.enter_context(nc.sbuf_tensor("sqs2", [128, 512], F32, allow_name_mangling=True))
        st8 = ph.enter_context(nc.sbuf_tensor("st8", [128, 8], F32, allow_name_mangling=True))
        wstgs = [ph.enter_context(nc.sbuf_tensor(f"pw{i}", [128, 512], F32,
                 allow_name_mangling=True)) for i in range(4)]
        wbfs = [ph.enter_context(nc.sbuf_tensor(f"pwb{i}", [128, 512], BF16,
                allow_name_mangling=True)) for i in range(4)]
        evacs = [ph.enter_context(nc.sbuf_tensor(f"pev{i}", [128, 512], BF16,
                 allow_name_mangling=True)) for i in range(2)]
        with ExitStack() as ph2:
            psums = [ph2.enter_context(nc.psum_tensor(f"pj{t}", [128, 512], F32,
                     allow_name_mangling=True)) for t in range(8)]
            for m in range(3):
                p.dma("sync", cvec[0:1, 0:D], b1r[m], sem_w)
                p.dma("sync", cvec[0:1, D:D + DL], bl1[m], sem_w)
                p.barrier()
                s1mark = {0: 0, 1: 0}
                s2mark = {0: 0, 1: 0}
                s2vm = {0: 0, 1: 0}

                def load_s1(oc, dc, wf, sem, m=m):
                    p.dma("sync", wf[:], wk1[m][dc, oc], sem)

                def lhsT_xnf(dc, t):
                    return xnf[:, 1024 * dc + 128 * t:1024 * dc + 128 * (t + 1)]

                def bias_s1(oc, t, ps):
                    E("tensor", lambda e, ps=ps, oc=oc: e.matmul(
                        ps[:], ones_f[:], cvec[0:1, 512 * oc:512 * (oc + 1)],
                        start=False, stop=True), inc=sem_t)

                def post_s1(oc, t, ps, s1mark=s1mark):
                    b = t % 2
                    ev = evacs[b]
                    if s1mark[b]:
                        p.wait("scalar", sem_w, s1mark[b])
                    E("scalar", lambda e, ev=ev, ps=ps: e.copy(ev[:], ps[:]),
                      inc=sem_s)
                    W("sync", sem_s)
                    for j in range(4):
                        mc = 4 * oc + j
                        p.dma("sync",
                              s1f[:, 1024 * mc + 128 * t:1024 * mc + 128 * (t + 1)],
                              ev[:, 128 * j:128 * (j + 1)], sem_w, transpose=True)
                    s1mark[b] = p.val(sem_w)

                stream_mm(half, 4, 16, load_s1, lhsT_xnf, psums, post_s1,
                          bias_fn=bias_s1, wstage=wstgs, wbf=wbfs)

                def load_s2(oc, dc, wf, sem, m=m):
                    p.dma("sync", wf[:], wl1[m][dc, oc], sem)

                def lhsT_s1f(dc, t):
                    return s1f[:, 1024 * dc + 128 * t:1024 * dc + 128 * (t + 1)]

                def bias_s2(oc, t, ps):
                    E("tensor", lambda e, ps=ps, oc=oc: e.matmul(
                        ps[:], ones_f[:], cvec[0:1, D + 512 * oc:D + 512 * (oc + 1)],
                        start=False, stop=True), inc=sem_t)

                def post_s2(oc, t, ps, m=m, half=half, s2mark=s2mark,
                            s2vm=s2vm):
                    tg = 8 * half + t
                    b = t % 2
                    ev = evacs[b]
                    if s2mark[b]:
                        p.wait("scalar", sem_gp[b], s2mark[b])
                    if s2vm[b]:
                        p.wait("scalar", sem_v, s2vm[b])
                    E("scalar", lambda e, ev=ev, ps=ps: e.copy(ev[:], ps[:]),
                      inc=sem_s)
                    if m < 2:
                        W("vector", sem_s)
                        E("vector", lambda e, ev=ev: e.tensor_tensor(
                            sqs2[:], ev[:], ev[:], ALU.mult))
                        VF()
                        E("vector", lambda e: e.tensor_reduce(
                            st8[:].rearrange("p (a o) -> p a o", o=1),
                            sqs2[:].rearrange("p (a c) -> p a c", c=64),
                            AX.X, ALU.add))
                        VF()
                        VF()
                        VF()
                        hb0 = (16 if m == 0 else 0) + 8 * oc
                        E("vector", lambda e, hb0=hb0: e.tensor_tensor(
                            nrm[:, hb0:hb0 + 8], nrm[:, hb0:hb0 + 8], st8[:],
                            ALU.max), inc=sem_v)
                        s2vm[b] = p.val(sem_v)
                        W("gpsimd", sem_v)
                    else:
                        W("gpsimd", sem_s)
                    dst = [k1t, q1t, v1_sc][m]
                    p.dma("gpsimd", dst[tg, :, 512 * oc:512 * (oc + 1)], ev[:],
                          sem_gp[b])
                    s2mark[b] = p.val(sem_gp[b])

                stream_mm(half, 2, 16, load_s2, lhsT_s1f, psums, post_s2,
                          bias_fn=bias_s2, wstage=wstgs, wbf=wbfs)

    def ffn_down():
        with ExitStack() as ph:
            qhf = ph.enter_context(nc.sbuf_tensor("qhf", [128, 64 * 512], BF16, allow_name_mangling=True))
            wlnb = ph.enter_context(nc.sbuf_tensor("wlnb2", [128, IF], BF16, allow_name_mangling=True))
            hld = [ph.enter_context(nc.sbuf_tensor(f"dhl{i}", [128, 512], BF16,
                   allow_name_mangling=True)) for i in range(2)]
            hwq = ph.enter_context(nc.sbuf_tensor("hwq", [128, 512], BF16, allow_name_mangling=True))
            tqd = ph.enter_context(nc.sbuf_tensor("tqd", [128, 512], F32, allow_name_mangling=True))
            qfd = ph.enter_context(nc.sbuf_tensor("qfd", [128, 512], F32, allow_name_mangling=True))
            dqd = ph.enter_context(nc.sbuf_tensor("dqd", [128, 512], F32, allow_name_mangling=True))
            qi8d = ph.enter_context(nc.sbuf_tensor("qi8d", [128, 512], I8, allow_name_mangling=True))
            qbf = [ph.enter_context(nc.sbuf_tensor(f"qbf{i}", [128, 512], BF16,
                   allow_name_mangling=True)) for i in range(2)]
            wld = [ph.enter_context(nc.sbuf_tensor(f"dwl{i}", [128, 512], BF16,
                   allow_name_mangling=True)) for i in range(4)]
            dbf = [ph.enter_context(nc.sbuf_tensor(f"dbf{i}", [128, 512], F32,
                   allow_name_mangling=True)) for i in range(2)]
            with ExitStack() as ph2:
                psums = [ph2.enter_context(nc.psum_tensor(f"pd{t}", [128, 512],
                         F32, allow_name_mangling=True)) for t in range(4)]
                pwl = ph2.enter_context(nc.psum_tensor("pwl2", [128, 512], F32, allow_name_mangling=True))
                for hh in range(2):
                    p.dma("sync", cvec[0:1, 0:4096], w_ln[0:1, 4096 * hh:4096 * (hh + 1)], sem_w)
                    p.barrier()
                    for j in range(8):
                        E("tensor", lambda e, j=j: e.matmul(
                            pwl[:], ones_f[:], cvec[0:1, 512 * j:512 * (j + 1)],
                            start=True, stop=True), inc=sem_t)
                        W("scalar", sem_t)
                        E("scalar", lambda e, j=j, hh=hh: e.copy(
                            wlnb[:, 4096 * hh + 512 * j:4096 * hh + 512 * (j + 1)],
                            pwl[:]), inc=sem_s)
                        p.barrier()
                for grp in range(4):
                    qmark = {0: 0, 1: 0}
                    idx = 0
                    for tl in range(4):
                        tg = 4 * grp + tl
                        for oc in range(16):
                            b = idx % 2
                            if idx >= 2:
                                p.wait("scalar", sem_v, qmark[b])
                            p.dma("scalar", hld[b][:], h_sc[tg, oc], sem_cp[b])
                            p.wait("vector", sem_cp[b])
                            W("vector", sem_s)
                            E("vector", lambda e, b=b, oc=oc: e.tensor_tensor(
                                hwq[:], hld[b][:],
                                wlnb[:, 512 * oc:512 * (oc + 1)], ALU.mult),
                                inc=sem_v)
                            qmark[b] = p.val(sem_v)
                            W("scalar", sem_v)
                            qb = qbf[idx % 2]
                            if idx >= 2:
                                p.wait("vector", sem_w, tmark2[idx % 2])
                            quant_chain(hwq[:], q2s[:, tg:tg + 1], tqd, qfd,
                                        dqd, qi8d, qb)
                            W("sync", sem_v)
                            for j in range(4):
                                c = 4 * oc + j
                                p.dma("sync",
                                      qhf[:, 512 * c + 128 * tl:
                                          512 * c + 128 * (tl + 1)],
                                      qb[:, 128 * j:128 * (j + 1)], sem_w,
                                      transpose=True)
                            tmark2[idx % 2] = p.val(sem_w)
                            idx += 1
                    p.barrier()
                    idx = 0
                    lmark = {i: 0 for i in range(4)}
                    dmark = {0: 0, 1: 0}
                    for oc in range(4):
                        for dc in range(64):
                            b = idx % 4
                            if idx >= 4:
                                p.wait("sync", sem_t, lmark[b])
                            p.dma("sync", wld[b][:], wd_t[dc, oc], sem_wp[idx % 2])
                            p.wait("tensor", sem_wp[idx % 2])
                            for tl in range(4):
                                E("tensor", lambda e, ps=psums[tl], b=b, dc=dc,
                                    tl=tl: e.matmul(
                                        ps[:],
                                        qhf[:, 512 * dc + 128 * tl:
                                            512 * dc + 128 * (tl + 1)],
                                        wld[b][:], start=(dc == 0),
                                        stop=(dc == 63)),
                                  inc=(sem_t if tl == 3 else None))
                            lmark[b] = p.val(sem_t)
                            idx += 1
                        W("scalar", sem_t)
                        for tl in range(4):
                            tg = 4 * grp + tl
                            bp = tl % 2
                            if dmark[bp]:
                                p.wait("scalar", sem_v, dmark[bp])
                            E("scalar", lambda e, tl=tl, tg=tg, bp=bp,
                                ps=psums[tl]: e.activation(
                                    dbf[bp][:], ps[:], AF.Identity,
                                    scale=dss[:, tg:tg + 1]), inc=sem_s)
                            W("vector", sem_s)
                            E("vector", lambda e, tg=tg, oc=oc, bp=bp:
                                e.tensor_tensor(
                                    x_sb[:, D * tg + 512 * oc:
                                         D * tg + 512 * (oc + 1)],
                                    x_sb[:, D * tg + 512 * oc:
                                         D * tg + 512 * (oc + 1)],
                                    dbf[bp][:], ALU.add), inc=sem_v)
                            dmark[bp] = p.val(sem_v)
                        if oc < 3:
                            p.wait("tensor", sem_s)
                            p.wait("tensor", sem_v)
                        else:
                            p.barrier()

    def rms_finalize():
        E("vector", lambda e: e.tensor_scalar(
            rcq[:], hstat[:], 1.0 / IF, EPS_RMS, ALU.mult, ALU.add))
        VF()
        VF()
        VF()
        E("vector", lambda e: e.reciprocal(rr2[:], rcq[:]), inc=sem_v)
        W("scalar", sem_v)
        E("scalar", lambda e: e.activation(rr2[:], rr2[:], AF.Sqrt), inc=sem_s)
        W("vector", sem_s)
        E("vector", lambda e: e.tensor_tensor(hmax[:], hmax[:], rr2[:], ALU.mult))
        VF()
        VF()
        VF()
        E("vector", lambda e: e.tensor_scalar(
            hmax[:], hmax[:], EPS_Q, None, ALU.max))
        VF()
        VF()
        VF()
        E("vector", lambda e: e.reciprocal(rcq[:], hmax[:]))
        VF()
        VF()
        VF()
        E("vector", lambda e: e.tensor_tensor(q2s[:], rcq[:], rr2[:], ALU.mult))
        VF()
        VF()
        VF()
        E("vector", lambda e: e.tensor_scalar(
            q2s[:], q2s[:], 127.0, None, ALU.mult))
        E("vector", lambda e: e.tensor_scalar(
            dss[:], hmax[:], 1.0 / 127.0, None, ALU.mult), inc=sem_v)
        W("scalar", sem_v)
        E("scalar", lambda e: e.activation(
            dss[:], dss[:], AF.Copy, scale=wmc[:, 8:9]), inc=sem_s)
        p.barrier()
        p.dma("sync", dbg4[:, 0:16], sxi[:], sem_st)
        p.dma("sync", dbg4[:, 16:32], hstat[:], sem_st)
        p.dma("sync", dbg4[:, 32:48], hmax[:], sem_st)
        p.dma("sync", dbg4[:, 48:64], q2s[:], sem_st)
        p.dma("sync", dbg4[:, 64:80], dss[:], sem_st)
        p.dma("sync", dbg4[:, 80:96], rr2[:], sem_st)
        p.dma("sync", dbg4[:, 96:108], wmc[:], sem_st)
        p.dma("sync", dbg4[:, 108:124], gu[:], sem_st)
        p.barrier()

    def layer1():
        E("vector", lambda e: e.memset(nrm[:], 0.0))
        E("vector", lambda e: e.memset(hstat[:], 0.0))
        E("vector", lambda e: e.memset(hmax[:], 0.0), inc=sem_v)
        p.barrier()
        wstats_tern()
        for half in range(2):
            with ExitStack() as phq:
                qxnf = phq.enter_context(nc.sbuf_tensor("qxnf", [128, 16 * 1024], BF16, allow_name_mangling=True))
                with ExitStack() as ph:
                    xnt = ph.enter_context(nc.sbuf_tensor("xnt1", [128, D], BF16, allow_name_mangling=True))
                    dumA = ph.enter_context(nc.sbuf_tensor("dumA1", [128, D], F32, allow_name_mangling=True))
                    qfb = ph.enter_context(nc.sbuf_tensor("qfb", [128, D], F32, allow_name_mangling=True))
                    dqb = ph.enter_context(nc.sbuf_tensor("dqb", [128, D], F32, allow_name_mangling=True))
                    qi8 = ph.enter_context(nc.sbuf_tensor("qi8", [128, D], I8, allow_name_mangling=True))
                    qxt = ph.enter_context(nc.sbuf_tensor("qxt", [128, D], BF16, allow_name_mangling=True))
                    mxc = ph.enter_context(nc.sbuf_tensor("mxc", [128, 1], F32, allow_name_mangling=True))
                    stk = ph.enter_context(nc.sbuf_tensor("stk", [128, 1], F32, allow_name_mangling=True))
                    for tl in range(8):
                        t = 8 * half + tl
                        ln_stats_apply(t, xnt[:], dumA[:], dumA[:])
                        W("vector", sem_s)
                        E("vector", lambda e: e.reduce_max(
                            mxc[:], xnt[:], AX.X, apply_absolute_value=True))
                        VF()
                        VF()
                        VF()
                        E("vector", lambda e, t=t: e.tensor_scalar(
                            sxi[:, t:t + 1], mxc[:], EPS_Q, 1.0 / 127.0,
                            ALU.max, ALU.mult))
                        VF()
                        VF()
                        VF()
                        E("vector", lambda e, t=t: e.reciprocal(
                            stk[:], sxi[:, t:t + 1]), inc=sem_v)
                        W("scalar", sem_v)
                        quant_chain(xnt[:], stk[:], dumA, qfb, dqb, qi8, qxt)
                        W("sync", sem_v)
                        for dc in range(16):
                            p.dma("sync",
                                  qxnf[:, 1024 * dc + 128 * tl:
                                       1024 * dc + 128 * (tl + 1)],
                                  qxt[:, 128 * dc:128 * (dc + 1)], sem_w,
                                  transpose=True)
                        p.barrier()
                with ExitStack() as ph:
                    ffn_gateup(half, qxnf, ph)
            with ExitStack() as ph:
                xnf = ph.enter_context(nc.sbuf_tensor("xnf1", [128, 16 * 1024], BF16, allow_name_mangling=True))
                xnt = ph.enter_context(nc.sbuf_tensor("xnt2", [128, D], BF16, allow_name_mangling=True))
                for tl in range(8):
                    t = 8 * half + tl
                    E("scalar", lambda e, t=t: e.activation(
                        xnt[:], x_sb[:, D * t:D * (t + 1)], AF.Identity,
                        scale=ln_rs[:, t:t + 1], bias=ln_mu[:, t:t + 1]),
                        inc=sem_s)
                    W("sync", sem_s)
                    xn_to_xnf(tl, xnt, xnf)
                    p.barrier()
                proj_half(half, xnf, ph)
        finalize_bound(SQD1)
        attention(layer=1)
        rms_finalize()
        ffn_down()

    tmark = {0: 0, 1: 0}
    tmark2 = {0: 0, 1: 0}
    evmark = {0: 0, 1: 0}

    layer0()
    layer1()

    with nc.sbuf_tensor("oout", [128, 2 * D], F32, allow_name_mangling=True) as oout:
        p.barrier()
        for t in range(NT):
            buf = oout[:, (t % 2) * D:(t % 2) * D + D]
            if t >= 2:
                p.wait("scalar", sem_st)
            E("scalar", lambda e, b=buf, t=t: e.copy(
                b, x_sb[:, D * t:D * (t + 1)]), inc=sem_s)
            W("sync", sem_s)
            p.dma("sync", out[128 * t:128 * (t + 1), :], buf, sem_st)
        p.wait("sync", sem_st)

    with nc.Block() as block:
        block.tensor(lambda e: [f(e) for f in p.ops["tensor"]])
        block.vector(lambda e: [f(e) for f in p.ops["vector"]])
        block.scalar(lambda e: [f(e) for f in p.ops["scalar"]])
        block.sync(lambda e: [f(e) for f in p.ops["sync"]])
        block.gpsimd(lambda e: [f(e) for f in p.ops["gpsimd"]])

    es.close()
    return nc


def _prep_inputs(inputs):
    f32 = np.float32
    g = lambda k: np.ascontiguousarray(np.asarray(inputs[k], dtype=f32))

    def tile(w, cc=512):
        R, C = w.shape
        return np.ascontiguousarray(
            w.reshape(R // 128, 128, C // cc, cc).transpose(0, 2, 1, 3))

    wq = g("L0_attn_W_qkv")
    perm = np.concatenate([
        np.concatenate([np.arange(384 * h, 384 * h + 128) for h in range(16)]),
        np.concatenate([np.arange(384 * h + 128, 384 * h + 256) for h in range(16)]),
        np.concatenate([np.arange(384 * h + 256, 384 * h + 384) for h in range(16)]),
    ])
    w_qkv = np.ascontiguousarray(wq[:, perm])
    b_qkv = np.ascontiguousarray(g("L0_attn_b_qkv")[perm])[None, :]

    s = 2048
    inv_freq = (1.0 / (10000.0 ** (np.arange(0, 32, 2, dtype=f32) / 32.0))).astype(f32)
    t = np.arange(s, dtype=f32)
    freqs = np.outer(t, inv_freq).astype(f32)
    cos16 = np.cos(freqs).astype(f32)
    sin16 = np.sin(freqs).astype(f32)
    cos_r = np.ascontiguousarray(np.tile(cos16, (1, 4)))
    sin_r = np.ascontiguousarray(np.tile(sin16, (1, 4)))

    tri = (np.arange(128)[None, :] >= np.arange(128)[:, None]).astype(BF)

    base = {
        "w_qkv": tile(w_qkv), "b_qkv": b_qkv,
        "w_dense": tile(g("L0_attn_W_dense")),
        "b_dense": g("L0_attn_b_dense")[None, :],
        "w_ffn0": tile(np.ascontiguousarray(g("L0_ffn_weight").T)),
        "s_ffn0": g("L0_ffn_scale_weight")[None, :],
        "b_ffn0": g("L0_ffn_bias")[None, :],
        "wk1": tile(g("L1_attn_wk")), "wq1": tile(g("L1_attn_wq")),
        "wv1": tile(g("L1_attn_wv")),
        "wkl": tile(g("L1_attn_wkl")), "wql": tile(g("L1_attn_wql")),
        "wvl": tile(g("L1_attn_wvl")),
        "bk1": g("L1_attn_bk")[None, :], "bq1": g("L1_attn_bq")[None, :],
        "bv1": g("L1_attn_bv")[None, :],
        "bkl": g("L1_attn_bkl")[None, :], "bql": g("L1_attn_bql")[None, :],
        "bvl": g("L1_attn_bvl")[None, :],
        "wo": tile(g("L1_attn_wo")), "bo": g("L1_attn_bo")[None, :],
        "w_gT": tile(np.ascontiguousarray(g("L1_ffn_w_gate").T), 2048),
        "w_uT": tile(np.ascontiguousarray(g("L1_ffn_w_up").T), 2048),
        "w_dT": tile(np.ascontiguousarray(g("L1_ffn_w_down").T), 2048),
        "w_ln": g("L1_ffn_ln_weight")[None, :],
        "identity": np.eye(128, dtype=BF),
        "trimask": tri,
        "cos_r": cos_r, "sin_r": sin_r,
    }
    x = np.asarray(inputs["L0_attn_hidden_states"], dtype=f32)
    maps = []
    for c in range(8):
        m = dict(base)
        m["x"] = np.ascontiguousarray(x[c % 4].reshape(TOK, D))
        maps.append(m)
    return maps


_NC_CACHE = {}
_LAST_DBG = None


def kernel(**inputs):
    maps = _prep_inputs(inputs)
    if "nc" not in _NC_CACHE:
        _NC_CACHE["nc"] = build_program()
    nc = _NC_CACHE["nc"]
    res = run_bass_kernel_spmd(nc, maps, core_ids=list(range(8)))
    kernel._LAST_RES = res
    global _LAST_DBG
    try:
        _LAST_DBG = np.stack([np.asarray(res.results[b]["dbg"]) for b in range(8)])
        np.save("/tmp/dev_dbg2.npy", np.asarray(res.results[0]["dbg2"], dtype=np.float32))
        np.save("/tmp/dev_dbg3.npy", np.asarray(res.results[0]["dbg3"], dtype=np.float32))
        np.save("/tmp/dev_dbg4.npy", np.asarray(res.results[0]["dbg4"], dtype=np.float32))
    except Exception:
        _LAST_DBG = None
    outs = [np.asarray(res.results[b]["out"], dtype=np.float32).reshape(1, TOK // D * 0 + 2048, D)
            for b in range(4)]
    return np.concatenate(outs, axis=0)


if __name__ == "__main__":
    nc = build_program()
    print("built ok")
